# version 16
# baseline (speedup 1.0000x reference)
import numpy as np
import ml_dtypes
from contextlib import ExitStack

import concourse.bass as bass
import concourse.mybir as mybir
from concourse.bass_utils import run_bass_kernel_spmd

F32 = mybir.dt.float32
BF16 = mybir.dt.bfloat16
FP8 = mybir.dt.float8e4
AF = mybir.ActivationFunctionType
ALU = mybir.AluOpType

D = 1024
T = 4096
NH = 16
DH = 64
C = 128
SC = 256
NSC = T // SC
HF = 4096
SLICE = 512
NSLICE = HF // SLICE
EPS = 1e-5
DBG = {}


class Prog:

    ENGINES = ("sync", "gpsimd", "scalar", "vector", "tensor")

    def __init__(self, nc):
        self.nc = nc
        self.ops = {e: [] for e in self.ENGINES}
        self.cnt = {}
        self.handles = {}

    def sem(self, es, name):
        self.handles[name] = es.enter_context(self.nc.semaphore(name))
        self.cnt[name] = 0

    def stage(self, engine, fn, inc=None, amt=1):
        if inc is not None:
            self.cnt[inc] += amt
            h = self.handles[inc]

            def wrapped(eng, fn=fn, h=h, amt=amt):
                fn(eng).then_inc(h, amt)
            self.ops[engine].append(wrapped)
        else:
            self.ops[engine].append(fn)

    COMP_SEM = {"vector": "s_dve", "scalar": "s_se", "gpsimd": "s_gp"}

    def comp(self, engine, fn, inc=None, amt=1):
        sem = self.COMP_SEM[engine]
        self.wait(engine, sem)
        self.stage(engine, fn, inc=sem)

    def dma(self, engine, out, in_, inc, **kw):
        self.stage(engine, lambda e, o=out, i=in_, kw=kw: e.dma_start(out=o, in_=i, **kw),
                   inc=inc, amt=16)

    def wait(self, engine, semname, value=None):
        v = self.cnt[semname] if value is None else value
        h = self.handles[semname]
        if v <= 0:
            return
        self.ops[engine].append(lambda eng, h=h, v=v: eng.wait_ge(h, v))

    def now(self, semname):
        return self.cnt[semname]

    def replay(self, block):
        p = self

        @block.sync
        def _(eng):
            for fn in p.ops["sync"]:
                fn(eng)

        @block.gpsimd
        def _(eng):
            for fn in p.ops["gpsimd"]:
                fn(eng)

        @block.scalar
        def _(eng):
            for fn in p.ops["scalar"]:
                fn(eng)

        @block.vector
        def _(eng):
            for fn in p.ops["vector"]:
                fn(eng)

        @block.tensor
        def _(eng):
            for fn in p.ops["tensor"]:
                fn(eng)


def build_nc():
    nc = bass.Bass("TRN2", target_bir_lowering=False)

    xT = nc.dram_tensor("xT", [D, T], F32, kind="ExternalInput").ap()
    wqT = nc.dram_tensor("wqT", [D, D], F32, kind="ExternalInput").ap()
    wkT = nc.dram_tensor("wkT", [D, D], F32, kind="ExternalInput").ap()
    wvT = nc.dram_tensor("wvT", [D, D], F32, kind="ExternalInput").ap()
    wbT = nc.dram_tensor("wbT", [D, D], F32, kind="ExternalInput").ap()
    waT = nc.dram_tensor("waT", [D, D], F32, kind="ExternalInput").ap()
    w1T = nc.dram_tensor("w1T", [D, HF], F32, kind="ExternalInput").ap()
    w2T = nc.dram_tensor("w2T", [HF, D], F32, kind="ExternalInput").ap()
    bq_d = nc.dram_tensor("bq", [1, D], F32, kind="ExternalInput").ap()
    bk_d = nc.dram_tensor("bk", [1, D], F32, kind="ExternalInput").ap()
    bv_d = nc.dram_tensor("bv", [1, D], F32, kind="ExternalInput").ap()
    bb_d = nc.dram_tensor("bb", [1, D], F32, kind="ExternalInput").ap()
    ba_d = nc.dram_tensor("ba", [1, D], F32, kind="ExternalInput").ap()
    b1_d = nc.dram_tensor("b1", [1, HF], F32, kind="ExternalInput").ap()
    b2_d = nc.dram_tensor("b2", [1, D], F32, kind="ExternalInput").ap()
    outT = nc.dram_tensor("outT", [D, T], F32, kind="ExternalOutput").ap()

    s1_scr = nc.dram_tensor("s1_scr", [D, HF], FP8, kind="Internal").ap()
    a1_scr = nc.dram_tensor("a1_scr", [1, HF], F32, kind="Internal").ap()
    s2_scr = nc.dram_tensor("s2_scr", [HF, D], FP8, kind="Internal").ap()
    b1_scr = nc.dram_tensor("b1_scr", [1, HF], BF16, kind="Internal").ap()
    pcur_scr = nc.dram_tensor("pcur_scr", [64, 128], F32, kind="Internal").ap()

    ident_f_np = np.eye(128, dtype=np.float32)
    triu_np = (np.triu(np.ones((128, 128), np.float32)) / 64.0).astype(np.float32)
    maskneg_np = np.where(np.arange(128)[:, None] > np.arange(128)[None, :],
                          np.float32(-1e9), np.float32(0.0))
    apat_np = np.zeros((16, 128), np.float32)
    for h in range(16):
        apat_np[h, (h % 2) * 64:(h % 2) * 64 + 64] = 1.0
    bpat_np = np.zeros((16, 512), np.float32)
    for h in range(16):
        bpat_np[h, (h // 2) * 64:(h // 2) * 64 + 64] = 1.0

    c_ident_f = nc.inline_tensor(ident_f_np, "c_ident_f").ap()
    c_triu = nc.inline_tensor(triu_np, "c_triu").ap()
    c_mask = nc.inline_tensor(maskneg_np, "c_mask").ap()
    c_apat = nc.inline_tensor(apat_np, "c_apat").ap()
    c_bpat = nc.inline_tensor(bpat_np, "c_bpat").ap()
    c_ones_bf = nc.inline_tensor(np.ones((1, 256), ml_dtypes.bfloat16), "c_ones_bf").ap()
    c_ones_f = nc.inline_tensor(np.ones((1, 128), np.float32), "c_ones_f").ap()
    c_ones_col = nc.inline_tensor(np.ones((128, 1), ml_dtypes.bfloat16), "c_ones_col").ap()

    es = ExitStack()
    sbuf = lambda name, shape, dt: es.enter_context(nc.sbuf_tensor(name, shape, dt))
    psum = lambda name, shape, dt: es.enter_context(nc.psum_tensor(name, shape, dt))

    xreg = [sbuf("xreg0", [128, 8 * SC], F32), sbuf("xreg1", [128, 8 * SC], F32)]
    xln = sbuf("xln", [128, 8 * SC], BF16)

    wq_s = sbuf("wq_s", [128, 8 * D], BF16)
    wk_s = sbuf("wk_s", [128, 8 * D], BF16)
    wv_s = sbuf("wv_s", [128, 8 * D], BF16)
    wb_s = sbuf("wb_s", [128, 8 * D], BF16)
    wa_s = sbuf("wa_s", [128, 8 * D], BF16)

    w1s = [sbuf("w1s%d" % i, [128, 8 * 256], FP8) for i in range(4)]
    w2s = [sbuf("w2s%d" % i, [128, 2 * D], FP8) for i in range(4)]
    acol = sbuf("acol", [128, 32], F32)
    b1col = sbuf("b1col", [128, 32], F32)
    hbf = [sbuf("hbf0", [128, 512], BF16), sbuf("hbf1", [128, 512], BF16)]

    qf = sbuf("qf", [128, 8 * SC], BF16)
    kf = sbuf("kf", [128, 8 * SC], BF16)
    qt = [sbuf("qt0", [128, 8 * C], BF16), sbuf("qt1", [128, 8 * C], BF16)]
    ktm = [sbuf("ktm0", [128, D], BF16), sbuf("ktm1", [128, D], BF16)]
    vbf = sbuf("vbf", [128, D], BF16)
    vbb = sbuf("vbb", [128, D], BF16)
    kbb = sbuf("kbb", [128, D], BF16)
    scrA = sbuf("scrA", [128, D], F32)
    wbc = [sbuf("wbc0", [128, 2048], F32), sbuf("wbc1", [128, 2048], F32)]
    cbc = [sbuf("cbc0", [128, 1024], BF16), sbuf("cbc1", [128, 1024], BF16)]
    attn_d = sbuf("attn_d", [128, 2048], BF16)
    ysb = sbuf("ysb", [128, 1024], F32)
    state = sbuf("state", [128, 512], F32)
    state_bf = sbuf("state_bf", [128, 512], BF16)
    stg = [sbuf("stg0", [128, 32], F32), sbuf("stg1", [128, 32], F32)]
    pfm = [sbuf("pfm0", [32, 128], F32), sbuf("pfm1", [32, 128], F32)]
    ptm = sbuf("ptm", [128, 32], F32)
    lam = sbuf("lam", [128, 32], F32)
    bmr = sbuf("bmr", [128, 16], F32)
    asc = sbuf("asc", [16, 2], F32)
    asrep = sbuf("asrep", [16, 512], F32)
    lnrow = sbuf("lnrow", [1, 1280], F32)
    s2st = [sbuf("s2st0", [128, 1024], FP8), sbuf("s2st1", [128, 1024], FP8)]

    bq_r = sbuf("bq_r", [1, D], BF16)
    bk_r = sbuf("bk_r", [1, D], BF16)
    bv_r = sbuf("bv_r", [1, D], BF16)
    bb_r = sbuf("bb_r", [1, D], BF16)
    ba_r = sbuf("ba_r", [1, D], F32)
    b2a_r = sbuf("b2a_r", [1, D], BF16)
    ha2 = sbuf("ha2", [128, 8], F32)

    ident_f = sbuf("ident_f", [128, 128], F32)
    triu = sbuf("triu", [128, 128], F32)
    maskc = sbuf("maskc", [128, 128], F32)
    apat = sbuf("apat", [16, 128], F32)
    bpat = sbuf("bpat", [16, 512], F32)
    ones_bf = sbuf("ones_bf", [1, 256], BF16)
    ones_f = sbuf("ones_f", [1, 128], F32)
    ones_col = sbuf("ones_col", [128, 1], BF16)

    pA = psum("pA", [128, 1024], F32)
    pB = psum("pB", [128, 1024], F32)
    pC = psum("pC", [128, 1024], F32)
    pD = psum("pD", [128, 512], F32)
    pE = psum("pE", [128, 512], F32)

    p = Prog(nc)
    for s in ("s_pro", "s_prog", "s_bias", "s_x0", "s_x1", "s_w0", "s_w1", "s_w2", "s_w3",
              "s_pg0", "s_pg1", "s_pg2", "s_pg3", "s_po0", "s_po1", "s_scr", "s_scr1", "s_wb0", "s_wb1",
              "s_cbc", "s_cbc1", "s_kt", "s_kt1", "s_out", "s_b2",
              "s_a1",
              "s_pe", "s_dve", "s_se", "s_gp"):
        p.sem(es, s)

    MM = lambda e, **kw: e.matmul(skip_group_check=True, **kw)

    for dst, src in [(ident_f, c_ident_f), (triu, c_triu), (maskc, c_mask),
                     (apat, c_apat), (bpat, c_bpat), (ones_bf, c_ones_bf),
                     (ones_f, c_ones_f), (ones_col, c_ones_col)]:
        p.dma("sync", dst[:], src[:, :], "s_pro")
    p.dma("gpsimd", bq_r[:], bq_d[:, :], "s_bias")
    p.dma("gpsimd", bk_r[:], bk_d[:, :], "s_bias")
    p.dma("gpsimd", bv_r[:], bv_d[:, :], "s_bias")
    p.dma("gpsimd", bb_r[:], bb_d[:, :], "s_bias")
    p.dma("sync", ba_r[:], ba_d[:, :], "s_pro")
    p.wait("tensor", "s_pro")
    p.wait("scalar", "s_pro")
    p.wait("gpsimd", "s_pro")
    p.wait("vector", "s_pro")
    p.wait("vector", "s_bias")
    p.comp("vector", lambda e: e.tensor_scalar_mul(out=bq_r[:], in0=bq_r[:], scalar1=0.125))
    p.comp("vector", lambda e: e.tensor_scalar_mul(out=bk_r[:], in0=bk_r[:], scalar1=0.125),
            inc="s_dve")
    free_mark = {i: ("s_dve", 0) for i in range(4)}
    stage_ap = lambda i: xreg[i // 2][:, (i % 2) * 1024:(i % 2) * 1024 + 1024]
    t = 0
    for wsb, wdr, scl in [(wq_s, wqT, 0.125), (wk_s, wkT, 0.125), (wv_s, wvT, None),
                          (wb_s, wbT, None), (wa_s, waT, None)]:
        for r in range(8):
            sb = t % 4
            eng = "sync" if t % 2 == 0 else "scalar"
            p.wait(eng, *free_mark[sb])
            p.dma(eng, stage_ap(sb), wdr[r * 128:(r + 1) * 128, :],
                  "s_pg%d" % sb)
            p.wait("vector", "s_pg%d" % sb)
            if scl is None:
                p.comp("vector", lambda e, wsb=wsb, r=r, sb=sb: e.tensor_copy(
                    out=wsb[:, r * D:(r + 1) * D], in_=stage_ap(sb)))
            else:
                p.comp("vector", lambda e, wsb=wsb, r=r, sb=sb, scl=scl: e.tensor_scalar_mul(
                    out=wsb[:, r * D:(r + 1) * D], in0=stage_ap(sb), scalar1=scl))
            free_mark[sb] = ("s_dve", p.now("s_dve"))
            t += 1

    for hg in range(4):
        hg_sl = slice(hg * 1024, (hg + 1) * 1024)
        mm_mark = {}
        se_mark = {}
        p.wait("scalar", "s_dve")
        for r in range(8):
            par = r % 2
            sb = (hg * 8 + r) % 4
            eng = "sync" if par == 0 else "scalar"
            p.wait(eng, *free_mark[sb])
            p.dma(eng, stage_ap(sb),
                  w1T[r * 128:(r + 1) * 128, hg_sl], "s_pg%d" % sb)
            p.wait("scalar", "s_pg%d" % sb)
            if r >= 2:
                p.wait("scalar", "s_pe", mm_mark[r - 2])
            p.comp("scalar", lambda e, par=par, sb=sb: e.activation(
                out=xln[:, par * 1024:(par + 1) * 1024], in_=stage_ap(sb), func=AF.Abs),
                inc="s_se")
            se_mark[r] = p.now("s_se")
            free_mark[sb] = ("s_se", se_mark[r])
            p.wait("tensor", "s_se")
            for half in range(2):
                p.stage("tensor", lambda e, par=par, half=half, r=r: MM(
                    e, out=pB[0:1, half * 512:(half + 1) * 512],
                    lhsT=ones_col[:], rhs=xln[:, par * 1024 + half * 512: par * 1024 + (half + 1) * 512],
                    start=(r == 0), stop=(r == 7 and half == 1)),
                    inc="s_pe" if half == 1 else None)
            mm_mark[r] = p.now("s_pe")
        p.wait("vector", "s_pe")
        p.wait("vector", "s_a1")
        p.comp("vector", lambda e: e.tensor_scalar_mul(
            out=lnrow[0:1, 0:1024], in0=pB[0:1, 0:1024], scalar1=1.0 / D), inc="s_dve")
        p.wait("sync", "s_dve")
        p.dma("sync", a1_scr[0:1, hg_sl], lnrow[0:1, 0:1024], "s_a1")
        for r in range(8):
            par = r % 2
            sb = (hg * 8 + r) % 4
            eng = "sync" if par == 0 else "scalar"
            p.wait(eng, *free_mark[sb])
            p.dma(eng, stage_ap(sb),
                  w1T[r * 128:(r + 1) * 128, hg_sl], "s_pg%d" % sb)
            p.wait("scalar", "s_pg%d" % sb)
            p.wait("scalar", "s_po%d" % par)
            p.comp("scalar", lambda e, par=par, sb=sb: e.activation(
                out=s2st[par][:], in_=stage_ap(sb), func=AF.Sign),
                inc="s_se")
            free_mark[sb] = ("s_se", p.now("s_se"))
            p.wait("gpsimd", "s_se")
            p.dma("gpsimd", s1_scr[r * 128:(r + 1) * 128, hg_sl],
                  s2st[par][:], "s_po%d" % par)

    p.wait("sync", "s_a1")
    p.dma("sync", acol[:, :], a1_scr[0:1, :].rearrange("q (c p) -> (q p) c", p=128),
          "s_a1", allow_slow_non_contiguous=True)
    p.dma("sync", b1col[:, :], b1_d[0:1, :].rearrange("q (c p) -> (q p) c", p=128),
          "s_a1", allow_slow_non_contiguous=True)

    se_mark = {}
    mm_mark = {}
    p.wait("scalar", "s_dve")
    for kc in range(32):
        par = kc % 2
        sb = kc % 4
        eng = "sync" if par == 0 else "scalar"
        p.wait(eng, *free_mark[sb])
        p.dma(eng, stage_ap(sb),
              w2T[kc * 128:(kc + 1) * 128, :], "s_pg%d" % sb)
        p.wait("scalar", "s_pg%d" % sb)
        p.wait("scalar", "s_po%d" % par)
        if kc >= 2:
            p.wait("scalar", "s_pe", mm_mark[kc - 2])
        p.comp("scalar", lambda e, par=par, sb=sb: e.activation(
            out=xln[:, par * 1024:(par + 1) * 1024], in_=stage_ap(sb), func=AF.Abs))
        p.comp("scalar", lambda e, par=par, sb=sb: e.activation(
            out=s2st[par][:], in_=stage_ap(sb), func=AF.Sign),
            inc="s_se")
        free_mark[sb] = ("s_se", p.now("s_se"))
        se_mark[kc] = p.now("s_se")
        p.wait("tensor", "s_se")
        for half in range(2):
            p.stage("tensor", lambda e, par=par, half=half, kc=kc: MM(
                e, out=pB[0:1, half * 512:(half + 1) * 512],
                lhsT=ones_col[:], rhs=xln[:, par * 1024 + half * 512: par * 1024 + (half + 1) * 512],
                start=(kc == 0), stop=(kc == 31 and half == 1)),
                inc="s_pe" if half == 1 else None)
        mm_mark[kc] = p.now("s_pe")
        p.wait("gpsimd", "s_se")
        p.dma("gpsimd", s2_scr[kc * 128:(kc + 1) * 128, :], s2st[par][:], "s_po%d" % par)
    p.wait("vector", "s_pe")
    p.wait("vector", "s_a1")
    p.comp("vector", lambda e: e.tensor_scalar_mul(
        out=lnrow[0:1, 0:1024], in0=pB[0:1, 0:1024], scalar1=1.0 / HF), inc="s_dve")
    p.wait("tensor", "s_dve")
    for dc in range(8):
        p.stage("tensor", lambda e, dc=dc: MM(
            e, out=pA[:, dc:dc + 1], lhsT=lnrow[0:1, dc * 128:(dc + 1) * 128],
            rhs=ones_f[0:1, 0:1], start=(dc == 0), stop=(dc == 7)),
            inc="s_pe" if dc == 7 else None)
    p.wait("vector", "s_pe")
    p.comp("vector", lambda e: e.tensor_scalar_mul(out=ha2[:], in0=pA[:, 0:8], scalar1=0.5))
    p.dma("sync", scrA[0:1, 0:1024], b2_d[:, :], "s_b2")
    p.wait("vector", "s_b2")
    p.comp("vector", lambda e: e.reciprocal(out=ysb[0:1, 0:1024], in_=lnrow[0:1, 0:1024]),
            inc="s_dve")
    p.wait("vector", "s_dve")
    p.comp("vector", lambda e: e.tensor_tensor(
        out=b2a_r[:], in0=scrA[0:1, 0:1024], in1=ysb[0:1, 0:1024], op=ALU.mult))
    p.comp("vector", lambda e: e.memset(state[:], 0.0))
    p.comp("vector", lambda e: e.memset(state_bf[:], 0.0), inc="s_dve")

    p.wait("sync", "s_se")
    p.wait("sync", "s_dve")
    p.dma("sync", xreg[0][:].rearrange("p (r t) -> p r t", r=8),
          xT[:, 0:SC].rearrange("(r p) t -> p r t", p=128), "s_x0")
    p.wait("gpsimd", "s_po0")
    p.wait("gpsimd", "s_po1")

    def emit_ln(par):
        xr = xreg[par]
        xbf, xsq = qf, attn_d
        p.wait("vector", "s_pe")
        p.wait("scalar", "s_pe")
        p.wait("scalar", "s_dve")
        p.comp("vector", lambda e: e.tensor_copy(out=xbf[:], in_=xr[:]), inc="s_dve")
        p.comp("scalar", lambda e: e.activation(out=xsq[:], in_=xr[:], func=AF.Square),
                inc="s_se")
        p.wait("tensor", "s_dve")
        p.wait("tensor", "s_se")
        for r in range(8):
            p.stage("tensor", lambda e, r=r: MM(
                e, out=pD[0:1, 0:SC], lhsT=ones_col[:], rhs=xbf[:, r * SC:(r + 1) * SC],
                start=(r == 0), stop=(r == 7)))
        for r in range(8):
            p.stage("tensor", lambda e, r=r: MM(
                e, out=pD[0:1, SC:2 * SC], lhsT=ones_col[:], rhs=xsq[:, r * SC:(r + 1) * SC],
                start=False, stop=(r == 7)), inc="s_pe" if r == 7 else None)
        p.wait("vector", "s_pe")
        p.comp("vector", lambda e: e.tensor_scalar_mul(
            out=lnrow[0:1, 0:256], in0=pD[0:1, 0:SC], scalar1=1.0 / D))
        p.comp("vector", lambda e: e.tensor_tensor(
            out=lnrow[0:1, 256:512], in0=lnrow[0:1, 0:256], in1=lnrow[0:1, 0:256],
            op=ALU.mult))
        p.comp("vector", lambda e: e.scalar_tensor_tensor(
            out=lnrow[0:1, 512:768], in0=pD[0:1, SC:2 * SC], scalar=1.0 / D,
            in1=lnrow[0:1, 256:512], op0=ALU.mult, op1=ALU.subtract))
        p.comp("vector", lambda e: e.tensor_scalar_add(
            out=lnrow[0:1, 512:768], in0=lnrow[0:1, 512:768], scalar1=EPS), inc="s_dve")
        p.wait("scalar", "s_dve")
        p.comp("scalar", lambda e: e.activation(
            out=lnrow[0:1, 768:1024], in_=lnrow[0:1, 512:768], func=AF.Ln))
        p.comp("scalar", lambda e: e.activation(
            out=lnrow[0:1, 768:1024], in_=lnrow[0:1, 768:1024], func=AF.Exp, scale=-0.5),
            inc="s_se")
        p.wait("vector", "s_se")
        p.comp("vector", lambda e: e.scalar_tensor_tensor(
            out=lnrow[0:1, 1024:1280], in0=lnrow[0:1, 0:256], scalar=-1.0,
            in1=lnrow[0:1, 768:1024], op0=ALU.mult, op1=ALU.mult), inc="s_dve")
        p.wait("tensor", "s_dve")
        p.stage("tensor", lambda e: MM(
            e, out=pE[:, 0:SC], lhsT=ones_f[0:1, 0:128], rhs=lnrow[0:1, 768:1024],
            start=True, stop=True))
        p.stage("tensor", lambda e: MM(
            e, out=pE[:, SC:2 * SC], lhsT=ones_f[0:1, 0:128], rhs=lnrow[0:1, 1024:1280],
            start=False, stop=True), inc="s_pe")
        p.wait("vector", "s_pe")
        for r in range(8):
            p.comp("vector", lambda e, r=r: e.tensor_tensor(
                out=scrA[:, 0:SC], in0=xr[:, r * SC:(r + 1) * SC], in1=pE[:, 0:SC],
                op=ALU.mult))
            p.comp("vector", lambda e, r=r: e.tensor_tensor(
                out=xln[:, r * SC:(r + 1) * SC], in0=scrA[:, 0:SC], in1=pE[:, SC:2 * SC],
                op=ALU.add), inc="s_dve" if r == 7 else None)

    ffn_last_use = [0, 0, 0, 0]
    l2_done_for_h = {}
    NS2 = 16

    def emit_ffn(par):
        p.wait("tensor", "s_dve")
        gelu_done = {}
        l1_done = {}
        wmark = {}

        def load_slice(s):
            sp = s % 4
            p.wait("gpsimd", "s_pe", ffn_last_use[sp])
            ws = "s_w%d" % sp
            p.dma("gpsimd", w1s[sp][:].rearrange("p (r h) -> p r h", r=8),
                  s1_scr[:, s * 256:(s + 1) * 256].rearrange("(r p) h -> p r h", p=128),
                  ws)
            p.dma("gpsimd", w2s[sp][:].rearrange("p (j d) -> p j d", j=2),
                  s2_scr[s * 256:(s + 1) * 256, :].rearrange("(j p) d -> p j d", p=128),
                  ws)
            wmark[s] = p.now(ws)

        for s in range(3):
            load_slice(s)
        for s in range(NS2 + 1):
            if s < NS2:
                sp = s % 4
                l1t, bank = ((pA, 0), (pA, 512), (pD, 0), (pE, 0))[s % 4]
                p.wait("tensor", "s_w%d" % sp, wmark[s])
                if s >= 4:
                    p.wait("tensor", "s_se", gelu_done[s - 4])
                for hc in range(2):
                    for r in range(8):
                        p.stage("tensor", lambda e, sp=sp, hc=hc, r=r, bank=bank, l1t=l1t: MM(
                            e, out=l1t[:, bank + hc * 256: bank + (hc + 1) * 256],
                            lhsT=w1s[sp][:, r * 256 + hc * 128: r * 256 + (hc + 1) * 128],
                            rhs=xln[:, r * SC:(r + 1) * SC],
                            start=(hc == 0 and r == 0), stop=(r == 7)),
                            inc="s_pe" if (hc == 1 and r == 7) else None)
                l1_done[s] = p.now("s_pe")
                p.wait("scalar", "s_pe", l1_done[s])
                p.wait("scalar", "s_a1")
                if s >= 2:
                    p.wait("scalar", "s_pe", l2_done_for_h[s - 2])
                for hc in range(2):
                    p.comp("scalar", lambda e, s=s, hc=hc, bank=bank, l1t=l1t: e.activation(
                        out=hbf[s % 2][:, hc * 256:(hc + 1) * 256],
                        in_=l1t[:, bank + hc * 256: bank + (hc + 1) * 256],
                        func=AF.Gelu, scale=acol[:, s * 2 + hc: s * 2 + hc + 1],
                        bias=b1col[:, s * 2 + hc: s * 2 + hc + 1]))
                gelu_done[s] = p.now("s_se")
            if s >= 1:
                spp = (s - 1) % 4
                p.wait("tensor", "s_se", gelu_done[s - 1])
                for j in range(2):
                    for dc in range(8):
                        tgt = pB if dc < 4 else pC
                        p.stage("tensor", lambda e, spp=spp, j=j, dc=dc, s=s, tgt=tgt: MM(
                            e, out=tgt[:, (dc % 4) * 256:(dc % 4) * 256 + 256],
                            lhsT=w2s[spp][:, j * D + dc * 128: j * D + (dc + 1) * 128],
                            rhs=hbf[(s - 1) % 2][:, j * 256:(j + 1) * 256],
                            start=(s == 1 and j == 0 and dc % 2 == 0), stop=False),
                            inc="s_pe" if (j == 1 and dc == 7) else None)
                ffn_last_use[spp] = p.now("s_pe")
                l2_done_for_h[s - 1] = p.now("s_pe")
                if s + 2 < NS2:
                    load_slice(s + 2)
        for dc in range(8):
            tgt = pB if dc < 4 else pC
            p.stage("tensor", lambda e, dc=dc, tgt=tgt: MM(
                e, out=tgt[:, (dc % 4) * 256:(dc % 4) * 256 + 256],
                lhsT=b2a_r[0:1, dc * 128:(dc + 1) * 128],
                rhs=ones_bf[0:1, 0:SC], start=False, stop=True),
                inc="s_pe" if dc == 7 else None)
        p.wait("vector", "s_pe")
        for dc in range(8):
            tgt = pB if dc < 4 else pC
            p.comp("vector", lambda e, dc=dc, tgt=tgt, par=par: e.scalar_tensor_tensor(
                out=xreg[par][:, dc * SC:(dc + 1) * SC],
                in0=tgt[:, (dc % 4) * 256:(dc % 4) * 256 + 256],
                scalar=ha2[:, dc:dc + 1],
                in1=xreg[par][:, dc * SC:(dc + 1) * SC],
                op0=ALU.mult, op1=ALU.add), inc="s_dve" if dc == 7 else None)

    dv_attnd = {0: 0, 1: 0}
    dv_vbb = [0]
    dv_xadd = [0]
    for sc in range(NSC):
        par = sc % 2
        if sc + 1 < NSC:
            p.wait("sync", "s_out", 16 * sc)
            p.dma("sync", xreg[(sc + 1) % 2][:].rearrange("p (r t) -> p r t", r=8),
                  xT[:, (sc + 1) * SC:(sc + 2) * SC].rearrange("(r p) t -> p r t", p=128),
                  "s_x%d" % ((sc + 1) % 2))
        xs = "s_x%d" % (sc % 2)
        xv = 16 * (sc // 2 + 1)
        p.wait("vector", xs, xv)
        p.wait("scalar", xs, xv)
        p.wait("tensor", xs, xv)

        PH = DBG.get("phase", 99)
        if PH >= 1:
            emit_ln(par)
        if PH >= 2:
            emit_ffn(par)
        if PH < 3:
            continue

        emit_ln(par)
        p.wait("tensor", "s_dve")
        for proj in range(2):
            wsb = wq_s if proj == 0 else wk_s
            brow = bq_r if proj == 0 else bk_r
            for oc in range(8):
                if proj == 0:
                    tgt, off = (pA, oc * 256) if oc < 4 else (pB, (oc - 4) * 256)
                else:
                    if oc < 4:
                        tgt, off = pC, oc * 256
                    elif oc < 6:
                        tgt, off = pD, (oc - 4) * 256
                    else:
                        tgt, off = pE, (oc - 6) * 256
                for r in range(8):
                    p.stage("tensor", lambda e, wsb=wsb, oc=oc, r=r, tgt=tgt, off=off: MM(
                        e, out=tgt[:, off:off + 256],
                        lhsT=wsb[:, r * D + oc * 128: r * D + (oc + 1) * 128],
                        rhs=xln[:, r * SC:(r + 1) * SC],
                        start=(r == 0 and off % 512 == 0), stop=False))
                p.stage("tensor", lambda e, brow=brow, oc=oc, tgt=tgt, off=off: MM(
                    e, out=tgt[:, off:off + 256],
                    lhsT=brow[0:1, oc * 128:(oc + 1) * 128],
                    rhs=ones_bf[0:1, 0:SC], start=False, stop=True),
                    inc="s_pe" if oc == 7 else None)
        p.wait("scalar", "s_pe")
        p.wait("vector", "s_pe")
        p.comp("scalar", lambda e: e.activation(out=qf[:, 0:1024], in_=pA[:],
                                                 func=AF.Identity))
        p.comp("scalar", lambda e: e.activation(out=qf[:, 1024:2048], in_=pB[:],
                                                 func=AF.Identity), inc="s_se")
        se_qk = p.now("s_se")
        p.comp("vector", lambda e: e.tensor_copy(out=kf[:, 0:1024], in_=pC[:]))
        p.comp("vector", lambda e: e.tensor_copy(out=kf[:, 1024:1536], in_=pD[:]))
        p.comp("vector", lambda e: e.tensor_copy(out=kf[:, 1536:2048], in_=pE[:]),
                inc="s_dve")

        if PH < 4:
            continue
        dv_acopy = {}
        dv_lam = {}
        pe_tri = {}
        dv_pfm = {}
        pe_al = {}
        p.wait("tensor", "s_dve")
        for c in range(2):
            if c == 1:
                p.wait("tensor", "s_dve", dv_acopy[0])
            for half in range(2):
                for r in range(8):
                    p.stage("tensor", lambda e, half=half, r=r, c=c: MM(
                        e, out=pC[:, half * 512:(half + 1) * 512],
                        lhsT=xln[:, r * SC + c * 128: r * SC + c * 128 + 128],
                        rhs=wa_s[:, r * D + half * 512: r * D + (half + 1) * 512],
                        start=(r == 0), stop=False))
                p.stage("tensor", lambda e, half=half: MM(
                    e, out=pC[:, half * 512:(half + 1) * 512],
                    lhsT=ones_f[0:1, 0:128],
                    rhs=ba_r[0:1, half * 512:(half + 1) * 512],
                    start=False, stop=True),
                    inc="s_pe" if half == 1 else None)
            pe_al[c] = p.now("s_pe")
            p.wait("vector", "s_pe", pe_al[c])
            p.comp("vector", lambda e: e.tensor_copy(out=ysb[:], in_=pC[:]),
                    inc="s_dve")
            dv_acopy[c] = p.now("s_dve")
            p.comp("vector", lambda e: e.tensor_scalar(
                out=ysb[:], in0=ysb[:], scalar1=10.0, scalar2=-10.0,
                op0=ALU.min, op1=ALU.max), inc="s_dve")
            p.wait("scalar", "s_dve")
            p.comp("scalar", lambda e: e.activation(out=ysb[:], in_=ysb[:],
                                                     func=AF.Exp, scale=-1.0))
            p.comp("scalar", lambda e: e.activation(out=ysb[:], in_=ysb[:],
                                                     func=AF.Ln, bias=1.0), inc="s_se")
            p.wait("vector", "s_se")
            p.comp("vector", lambda e, c=c: e.tensor_reduce(
                out=lam[:, c * 16:(c + 1) * 16, None],
                in_=ysb[:].rearrange("p (h d) -> p h d", h=16),
                axis=mybir.AxisListType.X, op=ALU.add), inc="s_dve")
            dv_lam[c] = p.now("s_dve")
        for c in range(2):
            p.wait("tensor", "s_dve", dv_lam[c])
            p.stage("tensor", lambda e, c=c: MM(
                e, out=pD[:, c * 16:(c + 1) * 16], lhsT=triu[:],
                rhs=lam[:, c * 16:(c + 1) * 16], start=True, stop=True),
                inc="s_pe")
            pe_tri[c] = p.now("s_pe")
            p.wait("vector", "s_pe", pe_tri[c])
            if c == 0:
                p.wait("vector", "s_gp")
            p.comp("vector", lambda e, c=c: e.tensor_copy(
                out=ptm[:, c * 16:(c + 1) * 16], in_=pD[:, c * 16:(c + 1) * 16]))
            p.comp("vector", lambda e, c=c: e.tensor_copy(
                out=stg[c][:, 0:16], in_=pD[:, c * 16:(c + 1) * 16]), inc="s_dve")
            p.wait("scalar", "s_pe", pe_tri[c])
            p.comp("scalar", lambda e, c=c: e.activation(
                out=stg[c][:, 16:32], in_=pD[:, c * 16:(c + 1) * 16],
                func=AF.Exp, scale=-1.0), inc="s_se")
        for c in range(2):
            p.wait("tensor", "s_se")
            p.wait("tensor", "s_dve")
            tcorner = pB if c == 0 else pA
            p.stage("tensor", lambda e, c=c, tcorner=tcorner: e.transpose(
                out=tcorner[0:32, 896:1024], in_=stg[c][:], identity=ident_f[:]),
                inc="s_pe")
            p.wait("vector", "s_pe")
            p.comp("vector", lambda e, c=c, tcorner=tcorner: e.tensor_copy(
                out=pfm[c][:], in_=tcorner[0:32, 896:1024]), inc="s_dve")
            dv_pfm[c] = p.now("s_dve")
            p.wait("scalar", "s_dve", dv_pfm[c])
            p.comp("scalar", lambda e, c=c: e.activation(
                out=asc[:, c:c + 1], in_=pfm[c][0:16, 127:128],
                func=AF.Exp, scale=-1.0), inc="s_se")
        scrm = {}
        cbcm = {}
        ktmm = {}
        for c in range(2):
            scr_s = "s_scr" if c == 0 else "s_scr1"
            cbc_s = "s_cbc" if c == 0 else "s_cbc1"
            p.wait("sync", "s_dve", dv_pfm[c])
            if c == 0:
                p.wait("sync", "s_cbc")
                p.wait("sync", "s_cbc1")
            p.dma("sync", pcur_scr[c * 32:(c + 1) * 32, :], pfm[c][:], scr_s)
            scrm[c] = p.now(scr_s)
            p.wait("gpsimd", scr_s, scrm[c])
            p.wait("gpsimd", "s_dve")
            for hh in range(2):
                p.dma("gpsimd",
                      cbc[c][64 * hh:64 * hh + 64, :].rearrange("q (b i) -> q b i", b=8),
                      pcur_scr[c * 32 + 16 + hh:c * 32 + 32:2, :]
                      .unsqueeze(0).broadcast_to((64, 8, 128)),
                      cbc_s)
            cbcm[c] = p.now(cbc_s)
        p.wait("scalar", "s_dve")
        for c in range(2):
            kt_s = "s_kt" if c == 0 else "s_kt1"
            for r in range(8):
                p.stage("scalar", lambda e, r=r, c=c: e.dma_start_transpose(
                    out=ktm[c][:, r * 128:(r + 1) * 128],
                    in_=kf[:, r * SC + c * 128: r * SC + c * 128 + 128]),
                    inc=kt_s, amt=16)
            ktmm[c] = p.now(kt_s)

        explm = {}
        for c in range(2):
            p.wait("sync", "s_scr" if c == 0 else "s_scr1", scrm[c])
            p.wait("sync", "s_dve", dv_attnd[c])
            wbsem = "s_wb0" if c == 0 else "s_wb1"
            p.dma("sync", wbc[c][:].rearrange("p (h i) -> p h i", h=16),
                  pcur_scr[c * 32:c * 32 + 16, :]
                  .unsqueeze(0).broadcast_to((128, 16, 128)), wbsem)
            wm = p.now(wbsem)
            p.wait("gpsimd", wbsem, wm)
            p.comp("gpsimd", lambda e, c=c: e.tensor_tensor(
                out=wbc[c][:].rearrange("p (h i) -> p h i", h=16),
                in0=ptm[:, c * 16:(c + 1) * 16, None].broadcast_to((128, 16, 128)),
                in1=wbc[c][:].rearrange("p (h i) -> p h i", h=16), op=ALU.subtract))
            p.comp("gpsimd", lambda e, c=c: e.tensor_tensor(
                out=wbc[c][:].rearrange("p (h i) -> p h i", h=16),
                in0=wbc[c][:].rearrange("p (h i) -> p h i", h=16),
                in1=maskc[:, None, :].broadcast_to((128, 16, 128)), op=ALU.add),
                inc="s_gp")
            p.wait("scalar", "s_gp")
            p.comp("scalar", lambda e, c=c: e.activation(
                out=wbc[c][:], in_=wbc[c][:], func=AF.Exp), inc="s_se")
            explm[c] = p.now("s_se")

        for c in range(2):
            n = sc * 2 + c
            p.wait("tensor", "s_se", se_qk)
            p.wait("tensor", "s_dve", dv_pfm[1])
            p.wait("tensor", "s_dve", dv_xadd[0])
            for pi in range(2):
                wsb, brow, tgt = ((wv_s, bv_r, pA), (wb_s, bb_r, pB))[pi]
                for half in range(2):
                    for r in range(8):
                        p.stage("tensor",
                                lambda e, wsb=wsb, half=half, r=r, c=c, tgt=tgt: MM(
                                    e, out=tgt[:, half * 512:(half + 1) * 512],
                                    lhsT=xln[:, r * SC + c * 128: r * SC + c * 128 + 128],
                                    rhs=wsb[:, r * D + half * 512: r * D + (half + 1) * 512],
                                    start=(r == 0), stop=False))
                    p.stage("tensor", lambda e, brow=brow, half=half, tgt=tgt: MM(
                        e, out=tgt[:, half * 512:(half + 1) * 512],
                        lhsT=ones_bf[0:1, 0:128],
                        rhs=brow[0:1, half * 512:(half + 1) * 512],
                        start=False, stop=True),
                        inc="s_pe" if half == 1 else None)
            pe_vba = p.now("s_pe")
            p.wait("vector", "s_pe", pe_vba - 1)
            p.comp("vector", lambda e: e.tensor_copy(out=vbf[:], in_=pA[:]), inc="s_dve")
            dv_vbf = p.now("s_dve")
            p.wait("scalar", "s_pe", pe_vba)
            p.wait("scalar", "s_dve", dv_vbb[0])
            p.comp("scalar", lambda e: e.activation(out=scrA[:], in_=pB[:], func=AF.Exp))
            p.comp("scalar", lambda e: e.activation(out=scrA[:], in_=scrA[:],
                                                     func=AF.Ln, bias=1.0), inc="s_se")
            se_beta = p.now("s_se")
            p.wait("tensor", "s_dve", dv_vbf)
            p.wait("tensor", "s_se", se_beta)
            seen_banks = set()
            order = list(range(0, 16, 2)) + [-1] + list(range(1, 16, 2))
            for h in order:
                if h < 0:
                    p.stage("tensor", lambda e: MM(
                        e, out=pE[0:1, 511:512], lhsT=ones_col[:],
                        rhs=ones_col[:], start=True, stop=True))
                    continue
                g = h // 4
                tgt = pA if g < 2 else pB
                off = (g % 2) * 512
                j = h % 4
                pb = 64 * (h % 2)
                bank = (id(tgt), off // 512)
                st_flag = bank not in seen_banks
                seen_banks.add(bank)
                p.stage("tensor", lambda e, h=h, j=j, c=c, tgt=tgt, off=off, pb=pb,
                        st_flag=st_flag: MM(
                        e, out=tgt[:, off + j * 128: off + (j + 1) * 128],
                        lhsT=kf[pb:pb + 64,
                                (h // 2) * SC + c * 128: (h // 2) * SC + c * 128 + 128],
                        rhs=qf[pb:pb + 64,
                               (h // 2) * SC + c * 128: (h // 2) * SC + c * 128 + 128],
                        start=st_flag, stop=True),
                        inc="s_pe" if h == 15 else None)
            pe_attn = p.now("s_pe")
            if DBG.get("sub", 99) < 6:
                continue
            p.wait("vector", "s_se", se_beta)
            p.comp("vector", lambda e: e.tensor_tensor(
                out=vbb[:], in0=vbf[:], in1=scrA[:], op=ALU.mult))
            dv_vbb[0] = p.now("s_dve")
            p.comp("vector", lambda e: e.tensor_reduce(
                out=bmr[:, :, None], in_=scrA[:].rearrange("p (h d) -> p h d", h=16),
                axis=mybir.AxisListType.X, op=ALU.add))
            p.comp("vector", lambda e: e.tensor_scalar_mul(
                out=bmr[:], in0=bmr[:], scalar1=1.0 / 64.0), inc="s_dve")
            p.wait("vector", "s_kt" if c == 0 else "s_kt1", ktmm[c])
            p.comp("vector", lambda e, c=c: e.tensor_tensor(
                out=kbb[:].rearrange("p (h d) -> p h d", h=16),
                in0=ktm[c][:].rearrange("p (h d) -> p h d", h=16),
                in1=bmr[:, :, None].broadcast_to((128, 16, 64)), op=ALU.mult))
            p.comp("vector", lambda e, c=c: e.tensor_scalar(
                out=asrep[:], in0=bpat[:], scalar1=asc[:, c:c + 1], scalar2=None,
                op0=ALU.mult), inc="s_dve")
            p.wait("vector", "s_se", explm[c])
            for g in range(4):
                tgt = pA if g < 2 else pB
                off = (g % 2) * 512
                p.wait("vector", "s_pe", pe_attn)
                p.comp("vector", lambda e, g=g, tgt=tgt, off=off, c=c: e.tensor_tensor(
                    out=attn_d[:, g * 512:(g + 1) * 512], in0=tgt[:, off:off + 512],
                    in1=wbc[c][:, g * 512:(g + 1) * 512], op=ALU.mult),
                    inc="s_dve" if g == 3 else None)
            dv_attnd[c] = p.now("s_dve")
            p.wait("vector", "s_cbc" if c == 0 else "s_cbc1", cbcm[c])
            p.comp("vector", lambda e, c=c: e.tensor_tensor(
                out=qt[c][:].rearrange("p (r t) -> p r t", r=8),
                in0=qf[:].rearrange("p (r t) -> p r t", r=8)[:, :, c * 128:(c + 1) * 128],
                in1=cbc[c][:].rearrange("p (r t) -> p r t", r=8), op=ALU.mult),
                inc="s_dve")
            if DBG.get("sub", 99) < 8:
                continue
            p.wait("tensor", "s_dve")
            for h in range(16):
                p.stage("tensor", lambda e, h=h: MM(
                    e, out=pC[:, h * 64:(h + 1) * 64],
                    lhsT=attn_d[:, h * 128:(h + 1) * 128],
                    rhs=vbb[:, h * 64:(h + 1) * 64],
                    start=(h % 8 == 0), stop=False))
            if n > 0:
                for h in list(range(0, 16, 2)) + [-1] + list(range(1, 16, 2)):
                    if h < 0:
                        p.stage("tensor", lambda e: MM(
                            e, out=pE[0:1, 510:511], lhsT=ones_col[:],
                            rhs=ones_col[:], start=True, stop=True))
                        continue
                    pb = 64 * (h % 2)
                    p.stage("tensor", lambda e, h=h, pb=pb, c=c: MM(
                        e, out=pC[:, h * 64:(h + 1) * 64],
                        lhsT=qt[c][pb:pb + 64, (h // 2) * 128:(h // 2) * 128 + 128],
                        rhs=state_bf[pb:pb + 64, (h // 2) * 64:(h // 2) * 64 + 64],
                        start=False, stop=(h == 15)))
            if DBG.get("sub", 99) < 9:
                continue
            for h in range(16):
                tp = {} if h % 2 == 0 else {"tile_position": (0, 64)}
                pb = 64 * (h % 2)
                p.stage("tensor", lambda e, h=h, tp=tp, pb=pb: MM(
                    e, out=pD[pb:pb + 64, (h // 2) * 64:(h // 2) * 64 + 64],
                    lhsT=kbb[:, h * 64:(h + 1) * 64],
                    rhs=vbf[:, h * 64:(h + 1) * 64],
                    start=(h <= 1), stop=(h == 15), **tp),
                    inc="s_pe" if h == 15 else None)
            if n > 0:
                p.stage("tensor", lambda e: MM(
                    e, out=pE[:, 0:512], lhsT=apat[:], rhs=asrep[:],
                    start=True, stop=True), inc="s_pe")
            pe_scan = p.now("s_pe")
            if DBG.get("sub", 99) < 10:
                continue
            p.wait("vector", "s_pe", pe_scan)
            if n > 0:
                p.comp("vector", lambda e: e.tensor_tensor(
                    out=state[:], in0=state[:], in1=pE[:, 0:512], op=ALU.mult))
                p.comp("vector", lambda e: e.tensor_tensor(
                    out=state[:], in0=state[:], in1=pD[:], op=ALU.add))
            else:
                p.comp("vector", lambda e: e.tensor_copy(out=state[:], in_=pD[:]))
            p.comp("vector", lambda e: e.tensor_copy(out=state_bf[:], in_=state[:]),
                    inc="s_dve")
            p.comp("vector", lambda e: e.tensor_copy(out=ysb[:], in_=pC[:]), inc="s_dve")
            if DBG.get("sub", 99) < 11:
                continue
            p.wait("tensor", "s_dve")
            for r in range(8):
                tb = pB if r < 4 else pA
                toff = (r % 4) * 128
                p.stage("tensor", lambda e, r=r, tb=tb, toff=toff: e.transpose(
                    out=tb[:, toff:toff + 128], in_=ysb[:, r * 128:(r + 1) * 128],
                    identity=ident_f[:]), inc="s_pe" if r == 7 else None)
            p.wait("vector", "s_pe")
            for r in range(8):
                tb = pB if r < 4 else pA
                toff = (r % 4) * 128
                p.comp("vector", lambda e, r=r, c=c, tb=tb, toff=toff, par=par:
                        e.tensor_tensor(
                            out=xreg[par][:, r * SC + c * 128: r * SC + c * 128 + 128],
                            in0=xreg[par][:, r * SC + c * 128: r * SC + c * 128 + 128],
                            in1=tb[:, toff:toff + 128], op=ALU.add),
                        inc="s_dve" if r == 7 else None)
            dv_xadd[0] = p.now("s_dve")

        if PH < 5:
            continue
        emit_ln(par)
        emit_ffn(par)
        p.wait("sync", "s_dve")
        p.dma("sync", outT[:, sc * SC:(sc + 1) * SC].rearrange("(r p) t -> p r t", p=128),
              xreg[par][:].rearrange("p (r t) -> p r t", r=8), "s_out")

    if p.now("s_out"):
        p.wait("sync", "s_out")

    with es:
        block = es.enter_context(nc.Block())
        p.replay(block)
    return nc


_NC_CACHE = {}


def kernel(**inputs):
    x = np.asarray(inputs["L0_attn_x"])
    Wq = np.asarray(inputs["L0_attn_Wq"]); bq = np.asarray(inputs["L0_attn_bq"])
    Wk = np.asarray(inputs["L0_attn_Wk"]); bk = np.asarray(inputs["L0_attn_bk"])
    Wv = np.asarray(inputs["L0_attn_Wv"]); bv = np.asarray(inputs["L0_attn_bv"])
    Wb = np.asarray(inputs["L0_attn_Wbeta"]); bb = np.asarray(inputs["L0_attn_bbeta"])
    Wa = np.asarray(inputs["L0_attn_Walpha"]); ba = np.asarray(inputs["L0_attn_balpha"])
    w1 = np.asarray(inputs["L0_ffn_w1"]); b1 = np.asarray(inputs["L0_ffn_b1"])
    w2 = np.asarray(inputs["L0_ffn_w2"]); b2 = np.asarray(inputs["L0_ffn_b2"])

    ct = lambda a: np.ascontiguousarray(a, dtype=np.float32)
    shared = {
        "wqT": ct(Wq.T), "wkT": ct(Wk.T), "wvT": ct(Wv.T),
        "wbT": ct(Wb.T), "waT": ct(Wa.T),
        "w1T": ct(w1.T), "w2T": ct(w2.T),
        "bq": ct(bq.reshape(1, D)), "bk": ct(bk.reshape(1, D)),
        "bv": ct(bv.reshape(1, D)), "bb": ct(bb.reshape(1, D)),
        "ba": ct(ba.reshape(1, D)), "b1": ct(b1.reshape(1, HF)),
        "b2": ct(b2.reshape(1, D)),
    }
    in_maps = []
    for b in range(8):
        m = dict(shared)
        m["xT"] = ct(x[b].T)
        in_maps.append(m)

    if "nc" not in _NC_CACHE:
        _NC_CACHE["nc"] = build_nc()
    nc = _NC_CACHE["nc"]
    res = run_bass_kernel_spmd(nc, in_maps, core_ids=list(range(8)))
    out = np.stack([np.asarray(res.results[i]["outT"]).T for i in range(8)])
    return np.ascontiguousarray(out, dtype=np.float32)


# version 18
# speedup vs baseline: 1.0138x; 1.0138x over previous
import numpy as np
import ml_dtypes
from contextlib import ExitStack

import concourse.bass as bass
import concourse.mybir as mybir
from concourse.bass_utils import run_bass_kernel_spmd

F32 = mybir.dt.float32
BF16 = mybir.dt.bfloat16
FP8 = mybir.dt.float8e4
AF = mybir.ActivationFunctionType
ALU = mybir.AluOpType

D = 1024
T = 4096
NH = 16
DH = 64
C = 128
SC = 256
NSC = T // SC
HF = 4096
SLICE = 512
NSLICE = HF // SLICE
EPS = 1e-5
DBG = {}


class Prog:

    ENGINES = ("sync", "gpsimd", "scalar", "vector", "tensor")

    def __init__(self, nc):
        self.nc = nc
        self.ops = {e: [] for e in self.ENGINES}
        self.cnt = {}
        self.handles = {}

    def sem(self, es, name):
        self.handles[name] = es.enter_context(self.nc.semaphore(name))
        self.cnt[name] = 0

    def stage(self, engine, fn, inc=None, amt=1):
        if inc is not None:
            self.cnt[inc] += amt
            h = self.handles[inc]

            def wrapped(eng, fn=fn, h=h, amt=amt):
                fn(eng).then_inc(h, amt)
            self.ops[engine].append(wrapped)
        else:
            self.ops[engine].append(fn)

    COMP_SEM = {"vector": "s_dve", "scalar": "s_se", "gpsimd": "s_gp"}

    def comp(self, engine, fn, inc=None, amt=1):
        sem = self.COMP_SEM[engine]
        self.wait(engine, sem)
        self.stage(engine, fn, inc=sem)

    def dma(self, engine, out, in_, inc, **kw):
        self.stage(engine, lambda e, o=out, i=in_, kw=kw: e.dma_start(out=o, in_=i, **kw),
                   inc=inc, amt=16)

    def wait(self, engine, semname, value=None):
        v = self.cnt[semname] if value is None else value
        h = self.handles[semname]
        if v <= 0:
            return
        self.ops[engine].append(lambda eng, h=h, v=v: eng.wait_ge(h, v))

    def now(self, semname):
        return self.cnt[semname]

    def replay(self, block):
        p = self

        @block.sync
        def _(eng):
            for fn in p.ops["sync"]:
                fn(eng)

        @block.gpsimd
        def _(eng):
            for fn in p.ops["gpsimd"]:
                fn(eng)

        @block.scalar
        def _(eng):
            for fn in p.ops["scalar"]:
                fn(eng)

        @block.vector
        def _(eng):
            for fn in p.ops["vector"]:
                fn(eng)

        @block.tensor
        def _(eng):
            for fn in p.ops["tensor"]:
                fn(eng)


def build_nc():
    nc = bass.Bass("TRN2", target_bir_lowering=False)

    xT = nc.dram_tensor("xT", [D, T], F32, kind="ExternalInput").ap()
    wqT = nc.dram_tensor("wqT", [D, D], F32, kind="ExternalInput").ap()
    wkT = nc.dram_tensor("wkT", [D, D], F32, kind="ExternalInput").ap()
    wvT = nc.dram_tensor("wvT", [D, D], F32, kind="ExternalInput").ap()
    wbT = nc.dram_tensor("wbT", [D, D], F32, kind="ExternalInput").ap()
    waT = nc.dram_tensor("waT", [D, D], F32, kind="ExternalInput").ap()
    w1T = nc.dram_tensor("w1T", [D, HF], F32, kind="ExternalInput").ap()
    w2T = nc.dram_tensor("w2T", [HF, D], F32, kind="ExternalInput").ap()
    bq_d = nc.dram_tensor("bq", [1, D], F32, kind="ExternalInput").ap()
    bk_d = nc.dram_tensor("bk", [1, D], F32, kind="ExternalInput").ap()
    bv_d = nc.dram_tensor("bv", [1, D], F32, kind="ExternalInput").ap()
    bb_d = nc.dram_tensor("bb", [1, D], F32, kind="ExternalInput").ap()
    ba_d = nc.dram_tensor("ba", [1, D], F32, kind="ExternalInput").ap()
    b1_d = nc.dram_tensor("b1", [1, HF], F32, kind="ExternalInput").ap()
    b2_d = nc.dram_tensor("b2", [1, D], F32, kind="ExternalInput").ap()
    outT = nc.dram_tensor("outT", [D, T], F32, kind="ExternalOutput").ap()

    s1_scr = nc.dram_tensor("s1_scr", [D, HF], FP8, kind="Internal").ap()
    a1_scr = nc.dram_tensor("a1_scr", [1, HF], F32, kind="Internal").ap()
    s2_scr = nc.dram_tensor("s2_scr", [HF, D], FP8, kind="Internal").ap()
    b1_scr = nc.dram_tensor("b1_scr", [1, HF], BF16, kind="Internal").ap()
    pcur_scr = nc.dram_tensor("pcur_scr", [64, 128], F32, kind="Internal").ap()

    ident_f_np = np.eye(128, dtype=np.float32)
    triu_np = (np.triu(np.ones((128, 128), np.float32)) / 64.0).astype(np.float32)
    maskneg_np = np.where(np.arange(128)[:, None] > np.arange(128)[None, :],
                          np.float32(-1e9), np.float32(0.0))
    apat_np = np.zeros((16, 128), np.float32)
    for h in range(16):
        apat_np[h, (h % 2) * 64:(h % 2) * 64 + 64] = 1.0
    bpat_np = np.zeros((16, 512), np.float32)
    for h in range(16):
        bpat_np[h, (h // 2) * 64:(h // 2) * 64 + 64] = 1.0

    c_ident_f = nc.inline_tensor(ident_f_np, "c_ident_f").ap()
    c_triu = nc.inline_tensor(triu_np, "c_triu").ap()
    c_mask = nc.inline_tensor(maskneg_np, "c_mask").ap()
    c_apat = nc.inline_tensor(apat_np, "c_apat").ap()
    c_bpat = nc.inline_tensor(bpat_np, "c_bpat").ap()
    c_ones_bf = nc.inline_tensor(np.ones((1, 256), ml_dtypes.bfloat16), "c_ones_bf").ap()
    c_ones_f = nc.inline_tensor(np.ones((1, 128), np.float32), "c_ones_f").ap()
    c_ones_col = nc.inline_tensor(np.ones((128, 1), ml_dtypes.bfloat16), "c_ones_col").ap()

    es = ExitStack()
    sbuf = lambda name, shape, dt: es.enter_context(nc.sbuf_tensor(name, shape, dt))
    psum = lambda name, shape, dt: es.enter_context(nc.psum_tensor(name, shape, dt))

    xreg = [sbuf("xreg0", [128, 8 * SC], F32), sbuf("xreg1", [128, 8 * SC], F32)]
    xln = sbuf("xln", [128, 8 * SC], BF16)

    wq_s = sbuf("wq_s", [128, 8 * D], BF16)
    wk_s = sbuf("wk_s", [128, 8 * D], BF16)
    wv_s = sbuf("wv_s", [128, 8 * D], BF16)
    wb_s = sbuf("wb_s", [128, 8 * D], BF16)
    wa_s = sbuf("wa_s", [128, 8 * D], BF16)

    w1s = [sbuf("w1s%d" % i, [128, 8 * 256], FP8) for i in range(4)]
    w2s = [sbuf("w2s%d" % i, [128, 2 * D], FP8) for i in range(4)]
    acol = sbuf("acol", [128, 32], F32)
    b1col = sbuf("b1col", [128, 32], F32)
    hbf = [sbuf("hbf0", [128, 512], BF16), sbuf("hbf1", [128, 512], BF16)]

    qf = sbuf("qf", [128, 8 * SC], BF16)
    kf = sbuf("kf", [128, 8 * SC], BF16)
    qt = [sbuf("qt0", [128, 8 * C], BF16), sbuf("qt1", [128, 8 * C], BF16)]
    ktm = [sbuf("ktm0", [128, D], BF16), sbuf("ktm1", [128, D], BF16)]
    vbf = sbuf("vbf", [128, D], BF16)
    vbb = sbuf("vbb", [128, D], BF16)
    kbb = sbuf("kbb", [128, D], BF16)
    scrA = sbuf("scrA", [128, D], F32)
    wbc = [sbuf("wbc0", [128, 2048], F32), sbuf("wbc1", [128, 2048], F32)]
    cbc = [sbuf("cbc0", [128, 1024], BF16), sbuf("cbc1", [128, 1024], BF16)]
    attn_d = sbuf("attn_d", [128, 2048], BF16)
    ysb = sbuf("ysb", [128, 1024], F32)
    state = sbuf("state", [128, 512], F32)
    state_bf = sbuf("state_bf", [128, 512], BF16)
    stg = [sbuf("stg0", [128, 32], F32), sbuf("stg1", [128, 32], F32)]
    pfm = [sbuf("pfm0", [32, 128], F32), sbuf("pfm1", [32, 128], F32)]
    ptm = sbuf("ptm", [128, 32], F32)
    lam = sbuf("lam", [128, 32], F32)
    bmr = sbuf("bmr", [128, 16], F32)
    asc = sbuf("asc", [16, 2], F32)
    asrep = sbuf("asrep", [16, 512], F32)
    lnrow = sbuf("lnrow", [1, 1280], F32)
    s2st = [sbuf("s2st0", [128, 1024], FP8), sbuf("s2st1", [128, 1024], FP8)]

    bq_r = sbuf("bq_r", [1, D], BF16)
    bk_r = sbuf("bk_r", [1, D], BF16)
    bv_r = sbuf("bv_r", [1, D], BF16)
    bb_r = sbuf("bb_r", [1, D], BF16)
    ba_r = sbuf("ba_r", [1, D], F32)
    b2a_r = sbuf("b2a_r", [1, D], BF16)
    ha2 = sbuf("ha2", [128, 8], F32)

    ident_f = sbuf("ident_f", [128, 128], F32)
    triu = sbuf("triu", [128, 128], F32)
    maskc = sbuf("maskc", [128, 128], F32)
    apat = sbuf("apat", [16, 128], F32)
    bpat = sbuf("bpat", [16, 512], F32)
    ones_bf = sbuf("ones_bf", [1, 256], BF16)
    ones_f = sbuf("ones_f", [1, 128], F32)
    ones_col = sbuf("ones_col", [128, 1], BF16)

    pA = psum("pA", [128, 1024], F32)
    pB = psum("pB", [128, 1024], F32)
    pC = psum("pC", [128, 1024], F32)
    pD = psum("pD", [128, 512], F32)
    pE = psum("pE", [128, 512], F32)

    p = Prog(nc)
    for s in ("s_pro", "s_prog", "s_bias", "s_x0", "s_x1", "s_w0", "s_w1", "s_w2", "s_w3",
              "s_pg0", "s_pg1", "s_po0", "s_po1", "s_scr", "s_scr1", "s_wb0", "s_wb1",
              "s_cbc", "s_cbc1", "s_kt", "s_kt1", "s_out", "s_b2",
              "s_a1",
              "s_pe", "s_dve", "s_se", "s_gp"):
        p.sem(es, s)

    MM = lambda e, **kw: e.matmul(skip_group_check=True, **kw)

    for dst, src in [(ident_f, c_ident_f), (triu, c_triu), (maskc, c_mask),
                     (apat, c_apat), (bpat, c_bpat), (ones_bf, c_ones_bf),
                     (ones_f, c_ones_f), (ones_col, c_ones_col)]:
        p.dma("sync", dst[:], src[:, :], "s_pro")
    p.dma("gpsimd", bq_r[:], bq_d[:, :], "s_bias")
    p.dma("gpsimd", bk_r[:], bk_d[:, :], "s_bias")
    p.dma("gpsimd", bv_r[:], bv_d[:, :], "s_bias")
    p.dma("gpsimd", bb_r[:], bb_d[:, :], "s_bias")
    p.dma("sync", ba_r[:], ba_d[:, :], "s_pro")
    p.wait("tensor", "s_pro")
    p.wait("scalar", "s_pro")
    p.wait("gpsimd", "s_pro")
    p.wait("vector", "s_pro")
    p.wait("vector", "s_bias")
    p.comp("vector", lambda e: e.tensor_scalar_mul(out=bq_r[:], in0=bq_r[:], scalar1=0.125))
    p.comp("vector", lambda e: e.tensor_scalar_mul(out=bk_r[:], in0=bk_r[:], scalar1=0.125),
            inc="s_dve")
    free_mark = {0: ("s_dve", 0), 1: ("s_dve", 0)}
    t = 0
    for wsb, wdr, scl in [(wq_s, wqT, 0.125), (wk_s, wkT, 0.125), (wv_s, wvT, None),
                          (wb_s, wbT, None), (wa_s, waT, None)]:
        for r in range(8):
            par = t % 2
            eng = "sync" if par == 0 else "scalar"
            p.wait(eng, *free_mark[par])
            p.dma(eng, xreg[par][:, 0:1024], wdr[r * 128:(r + 1) * 128, :],
                  "s_pg%d" % par)
            p.wait("vector", "s_pg%d" % par)
            if scl is None:
                p.comp("vector", lambda e, wsb=wsb, r=r, par=par: e.tensor_copy(
                    out=wsb[:, r * D:(r + 1) * D], in_=xreg[par][:, 0:1024]))
            else:
                p.comp("vector", lambda e, wsb=wsb, r=r, par=par, scl=scl: e.tensor_scalar_mul(
                    out=wsb[:, r * D:(r + 1) * D], in0=xreg[par][:, 0:1024], scalar1=scl))
            free_mark[par] = ("s_dve", p.now("s_dve"))
            t += 1

    for hg in range(4):
        hg_sl = slice(hg * 1024, (hg + 1) * 1024)
        mm_mark = {}
        se_mark = {}
        p.wait("scalar", "s_dve")
        for r in range(8):
            par = r % 2
            eng = "sync" if par == 0 else "scalar"
            p.wait(eng, *free_mark[par])
            p.dma(eng, xreg[par][:, 0:1024],
                  w1T[r * 128:(r + 1) * 128, hg_sl], "s_pg%d" % par)
            p.wait("scalar", "s_pg%d" % par)
            if r >= 2:
                p.wait("scalar", "s_pe", mm_mark[r - 2])
            p.comp("scalar", lambda e, par=par: e.activation(
                out=xln[:, par * 1024:(par + 1) * 1024], in_=xreg[par][:, 0:1024], func=AF.Abs),
                inc="s_se")
            se_mark[r] = p.now("s_se")
            free_mark[par] = ("s_se", se_mark[r])
            p.wait("tensor", "s_se")
            for half in range(2):
                p.stage("tensor", lambda e, par=par, half=half, r=r: MM(
                    e, out=pB[0:1, half * 512:(half + 1) * 512],
                    lhsT=ones_col[:], rhs=xln[:, par * 1024 + half * 512: par * 1024 + (half + 1) * 512],
                    start=(r == 0), stop=(r == 7 and half == 1)),
                    inc="s_pe" if half == 1 else None)
            mm_mark[r] = p.now("s_pe")
        p.wait("vector", "s_pe")
        p.wait("vector", "s_a1")
        p.comp("vector", lambda e: e.tensor_scalar_mul(
            out=lnrow[0:1, 0:1024], in0=pB[0:1, 0:1024], scalar1=1.0 / D), inc="s_dve")
        p.wait("sync", "s_dve")
        p.dma("sync", a1_scr[0:1, hg_sl], lnrow[0:1, 0:1024], "s_a1")
        for r in range(8):
            par = r % 2
            eng = "sync" if par == 0 else "scalar"
            p.wait(eng, *free_mark[par])
            p.dma(eng, xreg[par][:, 0:1024],
                  w1T[r * 128:(r + 1) * 128, hg_sl], "s_pg%d" % par)
            p.wait("scalar", "s_pg%d" % par)
            p.wait("scalar", "s_po%d" % par)
            p.comp("scalar", lambda e, par=par: e.activation(
                out=s2st[par][:], in_=xreg[par][:, 0:1024], func=AF.Sign),
                inc="s_se")
            free_mark[par] = ("s_se", p.now("s_se"))
            p.wait("gpsimd", "s_se")
            p.dma("gpsimd", s1_scr[r * 128:(r + 1) * 128, hg_sl],
                  s2st[par][:], "s_po%d" % par)

    p.wait("sync", "s_a1")
    p.dma("sync", acol[:, :], a1_scr[0:1, :].rearrange("q (c p) -> (q p) c", p=128),
          "s_a1", allow_slow_non_contiguous=True)
    p.dma("sync", b1col[:, :], b1_d[0:1, :].rearrange("q (c p) -> (q p) c", p=128),
          "s_a1", allow_slow_non_contiguous=True)

    se_mark = {}
    mm_mark = {}
    p.wait("scalar", "s_dve")
    for kc in range(32):
        par = kc % 2
        eng = "sync" if par == 0 else "scalar"
        p.wait(eng, *free_mark[par])
        p.dma(eng, xreg[par][:, 0:1024],
              w2T[kc * 128:(kc + 1) * 128, :], "s_pg%d" % par)
        p.wait("scalar", "s_pg%d" % par)
        p.wait("scalar", "s_po%d" % par)
        if kc >= 2:
            p.wait("scalar", "s_pe", mm_mark[kc - 2])
        p.comp("scalar", lambda e, par=par: e.activation(
            out=xln[:, par * 1024:(par + 1) * 1024], in_=xreg[par][:, 0:1024], func=AF.Abs))
        p.comp("scalar", lambda e, par=par: e.activation(
            out=s2st[par][:], in_=xreg[par][:, 0:1024], func=AF.Sign),
            inc="s_se")
        free_mark[par] = ("s_se", p.now("s_se"))
        se_mark[kc] = p.now("s_se")
        p.wait("tensor", "s_se")
        for half in range(2):
            p.stage("tensor", lambda e, par=par, half=half, kc=kc: MM(
                e, out=pB[0:1, half * 512:(half + 1) * 512],
                lhsT=ones_col[:], rhs=xln[:, par * 1024 + half * 512: par * 1024 + (half + 1) * 512],
                start=(kc == 0), stop=(kc == 31 and half == 1)),
                inc="s_pe" if half == 1 else None)
        mm_mark[kc] = p.now("s_pe")
        p.wait("gpsimd", "s_se")
        p.dma("gpsimd", s2_scr[kc * 128:(kc + 1) * 128, :], s2st[par][:], "s_po%d" % par)
    p.wait("vector", "s_pe")
    p.wait("vector", "s_a1")
    p.comp("vector", lambda e: e.tensor_scalar_mul(
        out=lnrow[0:1, 0:1024], in0=pB[0:1, 0:1024], scalar1=1.0 / HF), inc="s_dve")
    p.wait("tensor", "s_dve")
    for dc in range(8):
        p.stage("tensor", lambda e, dc=dc: MM(
            e, out=pA[:, dc:dc + 1], lhsT=lnrow[0:1, dc * 128:(dc + 1) * 128],
            rhs=ones_f[0:1, 0:1], start=(dc == 0), stop=(dc == 7)),
            inc="s_pe" if dc == 7 else None)
    p.wait("vector", "s_pe")
    p.comp("vector", lambda e: e.tensor_scalar_mul(out=ha2[:], in0=pA[:, 0:8], scalar1=0.5))
    p.dma("sync", scrA[0:1, 0:1024], b2_d[:, :], "s_b2")
    p.wait("vector", "s_b2")
    p.comp("vector", lambda e: e.reciprocal(out=ysb[0:1, 0:1024], in_=lnrow[0:1, 0:1024]),
            inc="s_dve")
    p.wait("vector", "s_dve")
    p.comp("vector", lambda e: e.tensor_tensor(
        out=b2a_r[:], in0=scrA[0:1, 0:1024], in1=ysb[0:1, 0:1024], op=ALU.mult))
    p.comp("vector", lambda e: e.memset(state[:], 0.0))
    p.comp("vector", lambda e: e.memset(state_bf[:], 0.0), inc="s_dve")

    p.wait("sync", "s_se")
    p.wait("sync", "s_dve")
    p.dma("sync", xreg[0][:].rearrange("p (r t) -> p r t", r=8),
          xT[:, 0:SC].rearrange("(r p) t -> p r t", p=128), "s_x0")
    p.wait("gpsimd", "s_po0")
    p.wait("gpsimd", "s_po1")

    def emit_ln(par):
        xr = xreg[par]
        xbf, xsq = qf, attn_d
        p.wait("vector", "s_pe")
        p.wait("scalar", "s_pe")
        p.wait("scalar", "s_dve")
        p.comp("vector", lambda e: e.tensor_copy(out=xbf[:], in_=xr[:]), inc="s_dve")
        p.comp("scalar", lambda e: e.activation(out=xsq[:], in_=xr[:], func=AF.Square),
                inc="s_se")
        p.wait("tensor", "s_dve")
        p.wait("tensor", "s_se")
        for r in range(8):
            p.stage("tensor", lambda e, r=r: MM(
                e, out=pD[0:1, 0:SC], lhsT=ones_col[:], rhs=xbf[:, r * SC:(r + 1) * SC],
                start=(r == 0), stop=(r == 7)))
        for r in range(8):
            p.stage("tensor", lambda e, r=r: MM(
                e, out=pD[0:1, SC:2 * SC], lhsT=ones_col[:], rhs=xsq[:, r * SC:(r + 1) * SC],
                start=False, stop=(r == 7)), inc="s_pe" if r == 7 else None)
        p.wait("vector", "s_pe")
        p.comp("vector", lambda e: e.tensor_scalar_mul(
            out=lnrow[0:1, 0:256], in0=pD[0:1, 0:SC], scalar1=1.0 / D))
        p.comp("vector", lambda e: e.tensor_tensor(
            out=lnrow[0:1, 256:512], in0=lnrow[0:1, 0:256], in1=lnrow[0:1, 0:256],
            op=ALU.mult))
        p.comp("vector", lambda e: e.scalar_tensor_tensor(
            out=lnrow[0:1, 512:768], in0=pD[0:1, SC:2 * SC], scalar=1.0 / D,
            in1=lnrow[0:1, 256:512], op0=ALU.mult, op1=ALU.subtract))
        p.comp("vector", lambda e: e.tensor_scalar_add(
            out=lnrow[0:1, 512:768], in0=lnrow[0:1, 512:768], scalar1=EPS), inc="s_dve")
        p.wait("scalar", "s_dve")
        p.comp("scalar", lambda e: e.activation(
            out=lnrow[0:1, 768:1024], in_=lnrow[0:1, 512:768], func=AF.Ln))
        p.comp("scalar", lambda e: e.activation(
            out=lnrow[0:1, 768:1024], in_=lnrow[0:1, 768:1024], func=AF.Exp, scale=-0.5),
            inc="s_se")
        p.wait("vector", "s_se")
        p.comp("vector", lambda e: e.scalar_tensor_tensor(
            out=lnrow[0:1, 1024:1280], in0=lnrow[0:1, 0:256], scalar=-1.0,
            in1=lnrow[0:1, 768:1024], op0=ALU.mult, op1=ALU.mult), inc="s_dve")
        p.wait("tensor", "s_dve")
        p.stage("tensor", lambda e: MM(
            e, out=pE[:, 0:SC], lhsT=ones_f[0:1, 0:128], rhs=lnrow[0:1, 768:1024],
            start=True, stop=True))
        p.stage("tensor", lambda e: MM(
            e, out=pE[:, SC:2 * SC], lhsT=ones_f[0:1, 0:128], rhs=lnrow[0:1, 1024:1280],
            start=False, stop=True), inc="s_pe")
        p.wait("vector", "s_pe")
        for r in range(8):
            p.comp("vector", lambda e, r=r: e.tensor_tensor(
                out=scrA[:, 0:SC], in0=xr[:, r * SC:(r + 1) * SC], in1=pE[:, 0:SC],
                op=ALU.mult))
            p.comp("vector", lambda e, r=r: e.tensor_tensor(
                out=xln[:, r * SC:(r + 1) * SC], in0=scrA[:, 0:SC], in1=pE[:, SC:2 * SC],
                op=ALU.add), inc="s_dve" if r == 7 else None)

    ffn_last_use = [0, 0, 0, 0]
    l2_done_for_h = {}
    NS2 = 16

    def emit_ffn(par):
        p.wait("tensor", "s_dve")
        gelu_done = {}
        l1_done = {}
        wmark = {}

        def load_slice(s):
            sp = s % 4
            p.wait("gpsimd", "s_pe", ffn_last_use[sp])
            ws = "s_w%d" % sp
            p.dma("gpsimd", w1s[sp][:].rearrange("p (r h) -> p r h", r=8),
                  s1_scr[:, s * 256:(s + 1) * 256].rearrange("(r p) h -> p r h", p=128),
                  ws)
            p.dma("gpsimd", w2s[sp][:].rearrange("p (j d) -> p j d", j=2),
                  s2_scr[s * 256:(s + 1) * 256, :].rearrange("(j p) d -> p j d", p=128),
                  ws)
            wmark[s] = p.now(ws)

        for s in range(3):
            load_slice(s)
        for s in range(NS2 + 1):
            if s < NS2:
                sp = s % 4
                l1t, bank = ((pA, 0), (pA, 512), (pD, 0), (pE, 0))[s % 4]
                p.wait("tensor", "s_w%d" % sp, wmark[s])
                if s >= 4:
                    p.wait("tensor", "s_se", gelu_done[s - 4])
                for hc in range(2):
                    for r in range(8):
                        p.stage("tensor", lambda e, sp=sp, hc=hc, r=r, bank=bank, l1t=l1t: MM(
                            e, out=l1t[:, bank + hc * 256: bank + (hc + 1) * 256],
                            lhsT=w1s[sp][:, r * 256 + hc * 128: r * 256 + (hc + 1) * 128],
                            rhs=xln[:, r * SC:(r + 1) * SC],
                            start=(hc == 0 and r == 0), stop=(r == 7)),
                            inc="s_pe" if (hc == 1 and r == 7) else None)
                l1_done[s] = p.now("s_pe")
                p.wait("scalar", "s_pe", l1_done[s])
                p.wait("scalar", "s_a1")
                if s >= 2:
                    p.wait("scalar", "s_pe", l2_done_for_h[s - 2])
                for hc in range(2):
                    p.comp("scalar", lambda e, s=s, hc=hc, bank=bank, l1t=l1t: e.activation(
                        out=hbf[s % 2][:, hc * 256:(hc + 1) * 256],
                        in_=l1t[:, bank + hc * 256: bank + (hc + 1) * 256],
                        func=AF.Gelu, scale=acol[:, s * 2 + hc: s * 2 + hc + 1],
                        bias=b1col[:, s * 2 + hc: s * 2 + hc + 1]))
                gelu_done[s] = p.now("s_se")
            if s >= 1:
                spp = (s - 1) % 4
                p.wait("tensor", "s_se", gelu_done[s - 1])
                for j in range(2):
                    for dc in range(8):
                        tgt = pB if dc < 4 else pC
                        p.stage("tensor", lambda e, spp=spp, j=j, dc=dc, s=s, tgt=tgt: MM(
                            e, out=tgt[:, (dc % 4) * 256:(dc % 4) * 256 + 256],
                            lhsT=w2s[spp][:, j * D + dc * 128: j * D + (dc + 1) * 128],
                            rhs=hbf[(s - 1) % 2][:, j * 256:(j + 1) * 256],
                            start=(s == 1 and j == 0 and dc % 2 == 0), stop=False),
                            inc="s_pe" if (j == 1 and dc == 7) else None)
                ffn_last_use[spp] = p.now("s_pe")
                l2_done_for_h[s - 1] = p.now("s_pe")
                if s + 2 < NS2:
                    load_slice(s + 2)
        for dc in range(8):
            tgt = pB if dc < 4 else pC
            p.stage("tensor", lambda e, dc=dc, tgt=tgt: MM(
                e, out=tgt[:, (dc % 4) * 256:(dc % 4) * 256 + 256],
                lhsT=b2a_r[0:1, dc * 128:(dc + 1) * 128],
                rhs=ones_bf[0:1, 0:SC], start=False, stop=True),
                inc="s_pe" if dc == 7 else None)
        p.wait("vector", "s_pe")
        for dc in range(8):
            tgt = pB if dc < 4 else pC
            p.comp("vector", lambda e, dc=dc, tgt=tgt, par=par: e.scalar_tensor_tensor(
                out=xreg[par][:, dc * SC:(dc + 1) * SC],
                in0=tgt[:, (dc % 4) * 256:(dc % 4) * 256 + 256],
                scalar=ha2[:, dc:dc + 1],
                in1=xreg[par][:, dc * SC:(dc + 1) * SC],
                op0=ALU.mult, op1=ALU.add), inc="s_dve" if dc == 7 else None)

    dv_attnd = {0: 0, 1: 0}
    dv_vbb = [0]
    dv_xadd = [0]
    for sc in range(NSC):
        par = sc % 2
        if sc + 1 < NSC:
            p.wait("sync", "s_out", 16 * sc)
            p.dma("sync", xreg[(sc + 1) % 2][:].rearrange("p (r t) -> p r t", r=8),
                  xT[:, (sc + 1) * SC:(sc + 2) * SC].rearrange("(r p) t -> p r t", p=128),
                  "s_x%d" % ((sc + 1) % 2))
        xs = "s_x%d" % (sc % 2)
        xv = 16 * (sc // 2 + 1)
        p.wait("vector", xs, xv)
        p.wait("scalar", xs, xv)
        p.wait("tensor", xs, xv)

        PH = DBG.get("phase", 99)
        if PH >= 1:
            emit_ln(par)
        if PH >= 2:
            emit_ffn(par)
        if PH < 3:
            continue

        emit_ln(par)
        p.wait("tensor", "s_dve")
        for proj in range(2):
            wsb = wq_s if proj == 0 else wk_s
            brow = bq_r if proj == 0 else bk_r
            for oc in range(8):
                if proj == 0:
                    tgt, off = (pA, oc * 256) if oc < 4 else (pB, (oc - 4) * 256)
                else:
                    if oc < 4:
                        tgt, off = pC, oc * 256
                    elif oc < 6:
                        tgt, off = pD, (oc - 4) * 256
                    else:
                        tgt, off = pE, (oc - 6) * 256
                for r in range(8):
                    p.stage("tensor", lambda e, wsb=wsb, oc=oc, r=r, tgt=tgt, off=off: MM(
                        e, out=tgt[:, off:off + 256],
                        lhsT=wsb[:, r * D + oc * 128: r * D + (oc + 1) * 128],
                        rhs=xln[:, r * SC:(r + 1) * SC],
                        start=(r == 0 and off % 512 == 0), stop=False))
                p.stage("tensor", lambda e, brow=brow, oc=oc, tgt=tgt, off=off: MM(
                    e, out=tgt[:, off:off + 256],
                    lhsT=brow[0:1, oc * 128:(oc + 1) * 128],
                    rhs=ones_bf[0:1, 0:SC], start=False, stop=True),
                    inc="s_pe" if oc == 7 else None)
        p.wait("scalar", "s_pe")
        p.wait("vector", "s_pe")
        p.comp("scalar", lambda e: e.activation(out=qf[:, 0:1024], in_=pA[:],
                                                 func=AF.Identity))
        p.comp("scalar", lambda e: e.activation(out=qf[:, 1024:2048], in_=pB[:],
                                                 func=AF.Identity), inc="s_se")
        se_qk = p.now("s_se")
        p.comp("vector", lambda e: e.tensor_copy(out=kf[:, 0:1024], in_=pC[:]))
        p.comp("vector", lambda e: e.tensor_copy(out=kf[:, 1024:1536], in_=pD[:]))
        p.comp("vector", lambda e: e.tensor_copy(out=kf[:, 1536:2048], in_=pE[:]),
                inc="s_dve")

        if PH < 4:
            continue
        dv_acopy = {}
        dv_lam = {}
        pe_tri = {}
        dv_pfm = {}
        pe_al = {}
        p.wait("tensor", "s_dve")
        for c in range(2):
            if c == 1:
                p.wait("tensor", "s_dve", dv_acopy[0])
            for half in range(2):
                for r in range(8):
                    p.stage("tensor", lambda e, half=half, r=r, c=c: MM(
                        e, out=pC[:, half * 512:(half + 1) * 512],
                        lhsT=xln[:, r * SC + c * 128: r * SC + c * 128 + 128],
                        rhs=wa_s[:, r * D + half * 512: r * D + (half + 1) * 512],
                        start=(r == 0), stop=False))
                p.stage("tensor", lambda e, half=half: MM(
                    e, out=pC[:, half * 512:(half + 1) * 512],
                    lhsT=ones_f[0:1, 0:128],
                    rhs=ba_r[0:1, half * 512:(half + 1) * 512],
                    start=False, stop=True),
                    inc="s_pe" if half == 1 else None)
            pe_al[c] = p.now("s_pe")
            p.wait("vector", "s_pe", pe_al[c])
            p.comp("vector", lambda e: e.tensor_copy(out=ysb[:], in_=pC[:]),
                    inc="s_dve")
            dv_acopy[c] = p.now("s_dve")
            p.comp("vector", lambda e: e.tensor_scalar(
                out=ysb[:], in0=ysb[:], scalar1=10.0, scalar2=-10.0,
                op0=ALU.min, op1=ALU.max), inc="s_dve")
            p.wait("scalar", "s_dve")
            p.comp("scalar", lambda e: e.activation(out=ysb[:], in_=ysb[:],
                                                     func=AF.Exp, scale=-1.0))
            p.comp("scalar", lambda e: e.activation(out=ysb[:], in_=ysb[:],
                                                     func=AF.Ln, bias=1.0), inc="s_se")
            p.wait("vector", "s_se")
            p.comp("vector", lambda e, c=c: e.tensor_reduce(
                out=lam[:, c * 16:(c + 1) * 16, None],
                in_=ysb[:].rearrange("p (h d) -> p h d", h=16),
                axis=mybir.AxisListType.X, op=ALU.add), inc="s_dve")
            dv_lam[c] = p.now("s_dve")
        for c in range(2):
            p.wait("tensor", "s_dve", dv_lam[c])
            p.stage("tensor", lambda e, c=c: MM(
                e, out=pD[:, c * 16:(c + 1) * 16], lhsT=triu[:],
                rhs=lam[:, c * 16:(c + 1) * 16], start=True, stop=True),
                inc="s_pe")
            pe_tri[c] = p.now("s_pe")
            p.wait("vector", "s_pe", pe_tri[c])
            if c == 0:
                p.wait("vector", "s_gp")
            p.comp("vector", lambda e, c=c: e.tensor_copy(
                out=ptm[:, c * 16:(c + 1) * 16], in_=pD[:, c * 16:(c + 1) * 16]))
            p.comp("vector", lambda e, c=c: e.tensor_copy(
                out=stg[c][:, 0:16], in_=pD[:, c * 16:(c + 1) * 16]), inc="s_dve")
            p.wait("scalar", "s_pe", pe_tri[c])
            p.comp("scalar", lambda e, c=c: e.activation(
                out=stg[c][:, 16:32], in_=pD[:, c * 16:(c + 1) * 16],
                func=AF.Exp, scale=-1.0), inc="s_se")
        for c in range(2):
            p.wait("tensor", "s_se")
            p.wait("tensor", "s_dve")
            tcorner = pB if c == 0 else pC
            p.stage("tensor", lambda e, c=c, tcorner=tcorner: e.transpose(
                out=tcorner[0:32, 896:1024], in_=stg[c][:], identity=ident_f[:]),
                inc="s_pe")
            p.wait("vector", "s_pe")
            p.comp("vector", lambda e, c=c, tcorner=tcorner: e.tensor_copy(
                out=pfm[c][:], in_=tcorner[0:32, 896:1024]), inc="s_dve")
            dv_pfm[c] = p.now("s_dve")
            p.wait("scalar", "s_dve", dv_pfm[c])
            p.comp("scalar", lambda e, c=c: e.activation(
                out=asc[:, c:c + 1], in_=pfm[c][0:16, 127:128],
                func=AF.Exp, scale=-1.0), inc="s_se")
        scrm = {}
        cbcm = {}
        ktmm = {}
        for c in range(2):
            scr_s = "s_scr" if c == 0 else "s_scr1"
            cbc_s = "s_cbc" if c == 0 else "s_cbc1"
            p.wait("sync", "s_dve", dv_pfm[c])
            if c == 0:
                p.wait("sync", "s_cbc")
                p.wait("sync", "s_cbc1")
            p.dma("sync", pcur_scr[c * 32:(c + 1) * 32, :], pfm[c][:], scr_s)
            scrm[c] = p.now(scr_s)
            p.wait("gpsimd", scr_s, scrm[c])
            p.wait("gpsimd", "s_dve")
            for hh in range(2):
                p.dma("gpsimd",
                      cbc[c][64 * hh:64 * hh + 64, :].rearrange("q (b i) -> q b i", b=8),
                      pcur_scr[c * 32 + 16 + hh:c * 32 + 32:2, :]
                      .unsqueeze(0).broadcast_to((64, 8, 128)),
                      cbc_s)
            cbcm[c] = p.now(cbc_s)
        p.wait("scalar", "s_dve")
        for c in range(2):
            kt_s = "s_kt" if c == 0 else "s_kt1"
            for r in range(8):
                p.stage("scalar", lambda e, r=r, c=c: e.dma_start_transpose(
                    out=ktm[c][:, r * 128:(r + 1) * 128],
                    in_=kf[:, r * SC + c * 128: r * SC + c * 128 + 128]),
                    inc=kt_s, amt=16)
            ktmm[c] = p.now(kt_s)

        explm = {}
        gpms = {}
        for c in range(2):
            p.wait("sync", "s_scr" if c == 0 else "s_scr1", scrm[c])
            p.wait("sync", "s_dve", dv_attnd[c])
            wbsem = "s_wb0" if c == 0 else "s_wb1"
            p.dma("sync", wbc[c][:].rearrange("p (h i) -> p h i", h=16),
                  pcur_scr[c * 32:c * 32 + 16, :]
                  .unsqueeze(0).broadcast_to((128, 16, 128)), wbsem)
            wm = p.now(wbsem)
            p.wait("gpsimd", wbsem, wm)
            p.comp("gpsimd", lambda e, c=c: e.tensor_tensor(
                out=wbc[c][:].rearrange("p (h i) -> p h i", h=16),
                in0=ptm[:, c * 16:(c + 1) * 16, None].broadcast_to((128, 16, 128)),
                in1=wbc[c][:].rearrange("p (h i) -> p h i", h=16), op=ALU.subtract))
            p.comp("gpsimd", lambda e, c=c: e.tensor_tensor(
                out=wbc[c][:].rearrange("p (h i) -> p h i", h=16),
                in0=wbc[c][:].rearrange("p (h i) -> p h i", h=16),
                in1=maskc[:, None, :].broadcast_to((128, 16, 128)), op=ALU.add),
                inc="s_gp")
            gpms[c] = p.now("s_gp")

        for c in range(2):
            n = sc * 2 + c
            p.wait("tensor", "s_se", se_qk)
            p.wait("tensor", "s_dve", dv_pfm[0])
            p.wait("tensor", "s_dve", dv_xadd[0])
            for pi in range(2):
                wsb, brow, tgt = ((wv_s, bv_r, pA), (wb_s, bb_r, pB))[pi]
                for half in range(2):
                    for r in range(8):
                        p.stage("tensor",
                                lambda e, wsb=wsb, half=half, r=r, c=c, tgt=tgt: MM(
                                    e, out=tgt[:, half * 512:(half + 1) * 512],
                                    lhsT=xln[:, r * SC + c * 128: r * SC + c * 128 + 128],
                                    rhs=wsb[:, r * D + half * 512: r * D + (half + 1) * 512],
                                    start=(r == 0), stop=False))
                    p.stage("tensor", lambda e, brow=brow, half=half, tgt=tgt: MM(
                        e, out=tgt[:, half * 512:(half + 1) * 512],
                        lhsT=ones_bf[0:1, 0:128],
                        rhs=brow[0:1, half * 512:(half + 1) * 512],
                        start=False, stop=True),
                        inc="s_pe" if half == 1 else None)
            pe_vba = p.now("s_pe")
            p.wait("vector", "s_pe", pe_vba - 1)
            p.comp("vector", lambda e: e.tensor_copy(out=vbf[:], in_=pA[:]), inc="s_dve")
            dv_vbf = p.now("s_dve")
            p.wait("scalar", "s_pe", pe_vba)
            p.wait("scalar", "s_dve", dv_vbb[0])
            p.comp("scalar", lambda e: e.activation(out=scrA[:], in_=pB[:], func=AF.Exp))
            p.comp("scalar", lambda e: e.activation(out=scrA[:], in_=scrA[:],
                                                     func=AF.Ln, bias=1.0), inc="s_se")
            se_beta = p.now("s_se")
            if c == 0:
                p.wait("scalar", "s_gp", gpms[0])
                p.comp("scalar", lambda e: e.activation(
                    out=wbc[0][:], in_=wbc[0][:], func=AF.Exp), inc="s_se")
                explm[0] = p.now("s_se")
            p.wait("tensor", "s_dve", dv_vbf)
            p.wait("tensor", "s_se", se_beta)
            seen_banks = set()
            order = list(range(0, 16, 2)) + [-1] + list(range(1, 16, 2))
            for h in order:
                if h < 0:
                    p.stage("tensor", lambda e: MM(
                        e, out=pE[0:1, 511:512], lhsT=ones_col[:],
                        rhs=ones_col[:], start=True, stop=True))
                    continue
                g = h // 4
                tgt = pA if g < 2 else pB
                off = (g % 2) * 512
                j = h % 4
                pb = 64 * (h % 2)
                bank = (id(tgt), off // 512)
                st_flag = bank not in seen_banks
                seen_banks.add(bank)
                p.stage("tensor", lambda e, h=h, j=j, c=c, tgt=tgt, off=off, pb=pb,
                        st_flag=st_flag: MM(
                        e, out=tgt[:, off + j * 128: off + (j + 1) * 128],
                        lhsT=kf[pb:pb + 64,
                                (h // 2) * SC + c * 128: (h // 2) * SC + c * 128 + 128],
                        rhs=qf[pb:pb + 64,
                               (h // 2) * SC + c * 128: (h // 2) * SC + c * 128 + 128],
                        start=st_flag, stop=True),
                        inc="s_pe" if h == 15 else None)
            pe_attn = p.now("s_pe")
            if DBG.get("sub", 99) < 6:
                continue
            p.wait("vector", "s_se", se_beta)
            p.comp("vector", lambda e: e.tensor_tensor(
                out=vbb[:], in0=vbf[:], in1=scrA[:], op=ALU.mult))
            dv_vbb[0] = p.now("s_dve")
            p.comp("vector", lambda e: e.tensor_reduce(
                out=bmr[:, :, None], in_=scrA[:].rearrange("p (h d) -> p h d", h=16),
                axis=mybir.AxisListType.X, op=ALU.add))
            p.comp("vector", lambda e: e.tensor_scalar_mul(
                out=bmr[:], in0=bmr[:], scalar1=1.0 / 64.0), inc="s_dve")
            p.wait("vector", "s_kt" if c == 0 else "s_kt1", ktmm[c])
            p.comp("vector", lambda e, c=c: e.tensor_tensor(
                out=kbb[:].rearrange("p (h d) -> p h d", h=16),
                in0=ktm[c][:].rearrange("p (h d) -> p h d", h=16),
                in1=bmr[:, :, None].broadcast_to((128, 16, 64)), op=ALU.mult))
            p.comp("vector", lambda e, c=c: e.tensor_scalar(
                out=asrep[:], in0=bpat[:], scalar1=asc[:, c:c + 1], scalar2=None,
                op0=ALU.mult), inc="s_dve")
            p.wait("vector", "s_se", explm[c])
            for g in range(4):
                tgt = pA if g < 2 else pB
                off = (g % 2) * 512
                p.wait("vector", "s_pe", pe_attn)
                p.comp("vector", lambda e, g=g, tgt=tgt, off=off, c=c: e.tensor_tensor(
                    out=attn_d[:, g * 512:(g + 1) * 512], in0=tgt[:, off:off + 512],
                    in1=wbc[c][:, g * 512:(g + 1) * 512], op=ALU.mult),
                    inc="s_dve" if g == 3 else None)
            dv_attnd[c] = p.now("s_dve")
            if c == 0:
                p.wait("scalar", "s_gp", gpms[1])
                p.comp("scalar", lambda e: e.activation(
                    out=wbc[1][:], in_=wbc[1][:], func=AF.Exp), inc="s_se")
                explm[1] = p.now("s_se")
            p.wait("vector", "s_cbc" if c == 0 else "s_cbc1", cbcm[c])
            p.comp("vector", lambda e, c=c: e.tensor_tensor(
                out=qt[c][:].rearrange("p (r t) -> p r t", r=8),
                in0=qf[:].rearrange("p (r t) -> p r t", r=8)[:, :, c * 128:(c + 1) * 128],
                in1=cbc[c][:].rearrange("p (r t) -> p r t", r=8), op=ALU.mult),
                inc="s_dve")
            if DBG.get("sub", 99) < 8:
                continue
            p.wait("tensor", "s_dve")
            for h in range(16):
                p.stage("tensor", lambda e, h=h: MM(
                    e, out=pC[:, h * 64:(h + 1) * 64],
                    lhsT=attn_d[:, h * 128:(h + 1) * 128],
                    rhs=vbb[:, h * 64:(h + 1) * 64],
                    start=(h % 8 == 0), stop=False))
            if n > 0:
                for h in list(range(0, 16, 2)) + [-1] + list(range(1, 16, 2)):
                    if h < 0:
                        p.stage("tensor", lambda e: MM(
                            e, out=pE[0:1, 510:511], lhsT=ones_col[:],
                            rhs=ones_col[:], start=True, stop=True))
                        continue
                    pb = 64 * (h % 2)
                    p.stage("tensor", lambda e, h=h, pb=pb, c=c: MM(
                        e, out=pC[:, h * 64:(h + 1) * 64],
                        lhsT=qt[c][pb:pb + 64, (h // 2) * 128:(h // 2) * 128 + 128],
                        rhs=state_bf[pb:pb + 64, (h // 2) * 64:(h // 2) * 64 + 64],
                        start=False, stop=(h == 15)))
            if DBG.get("sub", 99) < 9:
                continue
            for h in range(16):
                tp = {} if h % 2 == 0 else {"tile_position": (0, 64)}
                pb = 64 * (h % 2)
                p.stage("tensor", lambda e, h=h, tp=tp, pb=pb: MM(
                    e, out=pD[pb:pb + 64, (h // 2) * 64:(h // 2) * 64 + 64],
                    lhsT=kbb[:, h * 64:(h + 1) * 64],
                    rhs=vbf[:, h * 64:(h + 1) * 64],
                    start=(h <= 1), stop=(h == 15), **tp),
                    inc="s_pe" if h == 15 else None)
            if n > 0:
                p.stage("tensor", lambda e: MM(
                    e, out=pE[:, 0:512], lhsT=apat[:], rhs=asrep[:],
                    start=True, stop=True), inc="s_pe")
            pe_scan = p.now("s_pe")
            if DBG.get("sub", 99) < 10:
                continue
            p.wait("vector", "s_pe", pe_scan)
            if n > 0:
                p.comp("vector", lambda e: e.tensor_tensor(
                    out=state[:], in0=state[:], in1=pE[:, 0:512], op=ALU.mult))
                p.comp("vector", lambda e: e.tensor_tensor(
                    out=state[:], in0=state[:], in1=pD[:], op=ALU.add))
            else:
                p.comp("vector", lambda e: e.tensor_copy(out=state[:], in_=pD[:]))
            p.comp("vector", lambda e: e.tensor_copy(out=state_bf[:], in_=state[:]),
                    inc="s_dve")
            p.comp("vector", lambda e: e.tensor_copy(out=ysb[:], in_=pC[:]), inc="s_dve")
            if DBG.get("sub", 99) < 11:
                continue
            p.wait("tensor", "s_dve")
            for r in range(8):
                tb = pB if r < 4 else pA
                toff = (r % 4) * 128
                p.stage("tensor", lambda e, r=r, tb=tb, toff=toff: e.transpose(
                    out=tb[:, toff:toff + 128], in_=ysb[:, r * 128:(r + 1) * 128],
                    identity=ident_f[:]), inc="s_pe" if r == 7 else None)
            p.wait("vector", "s_pe")
            for r in range(8):
                tb = pB if r < 4 else pA
                toff = (r % 4) * 128
                p.comp("vector", lambda e, r=r, c=c, tb=tb, toff=toff, par=par:
                        e.tensor_tensor(
                            out=xreg[par][:, r * SC + c * 128: r * SC + c * 128 + 128],
                            in0=xreg[par][:, r * SC + c * 128: r * SC + c * 128 + 128],
                            in1=tb[:, toff:toff + 128], op=ALU.add),
                        inc="s_dve" if r == 7 else None)
            dv_xadd[0] = p.now("s_dve")

        if PH < 5:
            continue
        emit_ln(par)
        emit_ffn(par)
        p.wait("sync", "s_dve")
        p.dma("sync", outT[:, sc * SC:(sc + 1) * SC].rearrange("(r p) t -> p r t", p=128),
              xreg[par][:].rearrange("p (r t) -> p r t", r=8), "s_out")

    if p.now("s_out"):
        p.wait("sync", "s_out")

    with es:
        block = es.enter_context(nc.Block())
        p.replay(block)
    return nc


_NC_CACHE = {}


def kernel(**inputs):
    x = np.asarray(inputs["L0_attn_x"])
    Wq = np.asarray(inputs["L0_attn_Wq"]); bq = np.asarray(inputs["L0_attn_bq"])
    Wk = np.asarray(inputs["L0_attn_Wk"]); bk = np.asarray(inputs["L0_attn_bk"])
    Wv = np.asarray(inputs["L0_attn_Wv"]); bv = np.asarray(inputs["L0_attn_bv"])
    Wb = np.asarray(inputs["L0_attn_Wbeta"]); bb = np.asarray(inputs["L0_attn_bbeta"])
    Wa = np.asarray(inputs["L0_attn_Walpha"]); ba = np.asarray(inputs["L0_attn_balpha"])
    w1 = np.asarray(inputs["L0_ffn_w1"]); b1 = np.asarray(inputs["L0_ffn_b1"])
    w2 = np.asarray(inputs["L0_ffn_w2"]); b2 = np.asarray(inputs["L0_ffn_b2"])

    ct = lambda a: np.ascontiguousarray(a, dtype=np.float32)
    shared = {
        "wqT": ct(Wq.T), "wkT": ct(Wk.T), "wvT": ct(Wv.T),
        "wbT": ct(Wb.T), "waT": ct(Wa.T),
        "w1T": ct(w1.T), "w2T": ct(w2.T),
        "bq": ct(bq.reshape(1, D)), "bk": ct(bk.reshape(1, D)),
        "bv": ct(bv.reshape(1, D)), "bb": ct(bb.reshape(1, D)),
        "ba": ct(ba.reshape(1, D)), "b1": ct(b1.reshape(1, HF)),
        "b2": ct(b2.reshape(1, D)),
    }
    in_maps = []
    for b in range(8):
        m = dict(shared)
        m["xT"] = ct(x[b].T)
        in_maps.append(m)

    if "nc" not in _NC_CACHE:
        _NC_CACHE["nc"] = build_nc()
    nc = _NC_CACHE["nc"]
    res = run_bass_kernel_spmd(nc, in_maps, core_ids=list(range(8)))
    out = np.stack([np.asarray(res.results[i]["outT"]).T for i in range(8)])
    return np.ascontiguousarray(out, dtype=np.float32)


# version 19
# speedup vs baseline: 1.0149x; 1.0011x over previous
import numpy as np
import ml_dtypes
from contextlib import ExitStack

import concourse.bass as bass
import concourse.mybir as mybir
from concourse.bass_utils import run_bass_kernel_spmd

F32 = mybir.dt.float32
BF16 = mybir.dt.bfloat16
FP8 = mybir.dt.float8e4
AF = mybir.ActivationFunctionType
ALU = mybir.AluOpType

D = 1024
T = 4096
NH = 16
DH = 64
C = 128
SC = 256
NSC = T // SC
HF = 4096
SLICE = 512
NSLICE = HF // SLICE
EPS = 1e-5
DBG = {}


class Prog:

    ENGINES = ("sync", "gpsimd", "scalar", "vector", "tensor")

    def __init__(self, nc):
        self.nc = nc
        self.ops = {e: [] for e in self.ENGINES}
        self.cnt = {}
        self.handles = {}

    def sem(self, es, name):
        self.handles[name] = es.enter_context(self.nc.semaphore(name))
        self.cnt[name] = 0

    def stage(self, engine, fn, inc=None, amt=1):
        if inc is not None:
            self.cnt[inc] += amt
            h = self.handles[inc]

            def wrapped(eng, fn=fn, h=h, amt=amt):
                fn(eng).then_inc(h, amt)
            self.ops[engine].append(wrapped)
        else:
            self.ops[engine].append(fn)

    COMP_SEM = {"vector": "s_dve", "scalar": "s_se", "gpsimd": "s_gp"}

    def comp(self, engine, fn, inc=None, amt=1):
        sem = self.COMP_SEM[engine]
        self.wait(engine, sem)
        self.stage(engine, fn, inc=sem)

    def dma(self, engine, out, in_, inc, **kw):
        self.stage(engine, lambda e, o=out, i=in_, kw=kw: e.dma_start(out=o, in_=i, **kw),
                   inc=inc, amt=16)

    def wait(self, engine, semname, value=None):
        v = self.cnt[semname] if value is None else value
        h = self.handles[semname]
        if v <= 0:
            return
        self.ops[engine].append(lambda eng, h=h, v=v: eng.wait_ge(h, v))

    def now(self, semname):
        return self.cnt[semname]

    def replay(self, block):
        p = self

        @block.sync
        def _(eng):
            for fn in p.ops["sync"]:
                fn(eng)

        @block.gpsimd
        def _(eng):
            for fn in p.ops["gpsimd"]:
                fn(eng)

        @block.scalar
        def _(eng):
            for fn in p.ops["scalar"]:
                fn(eng)

        @block.vector
        def _(eng):
            for fn in p.ops["vector"]:
                fn(eng)

        @block.tensor
        def _(eng):
            for fn in p.ops["tensor"]:
                fn(eng)


def build_nc():
    nc = bass.Bass("TRN2", target_bir_lowering=False)

    xT = nc.dram_tensor("xT", [D, T], F32, kind="ExternalInput").ap()
    wqT = nc.dram_tensor("wqT", [D, D], F32, kind="ExternalInput").ap()
    wkT = nc.dram_tensor("wkT", [D, D], F32, kind="ExternalInput").ap()
    wvT = nc.dram_tensor("wvT", [D, D], F32, kind="ExternalInput").ap()
    wbT = nc.dram_tensor("wbT", [D, D], F32, kind="ExternalInput").ap()
    waT = nc.dram_tensor("waT", [D, D], F32, kind="ExternalInput").ap()
    w1T = nc.dram_tensor("w1T", [D, HF], F32, kind="ExternalInput").ap()
    w2T = nc.dram_tensor("w2T", [HF, D], F32, kind="ExternalInput").ap()
    bq_d = nc.dram_tensor("bq", [1, D], F32, kind="ExternalInput").ap()
    bk_d = nc.dram_tensor("bk", [1, D], F32, kind="ExternalInput").ap()
    bv_d = nc.dram_tensor("bv", [1, D], F32, kind="ExternalInput").ap()
    bb_d = nc.dram_tensor("bb", [1, D], F32, kind="ExternalInput").ap()
    ba_d = nc.dram_tensor("ba", [1, D], F32, kind="ExternalInput").ap()
    b1_d = nc.dram_tensor("b1", [1, HF], F32, kind="ExternalInput").ap()
    b2_d = nc.dram_tensor("b2", [1, D], F32, kind="ExternalInput").ap()
    outT = nc.dram_tensor("outT", [D, T], F32, kind="ExternalOutput").ap()

    s1_scr = nc.dram_tensor("s1_scr", [D, HF], FP8, kind="Internal").ap()
    a1_scr = nc.dram_tensor("a1_scr", [1, HF], F32, kind="Internal").ap()
    s2_scr = nc.dram_tensor("s2_scr", [HF, D], FP8, kind="Internal").ap()
    b1_scr = nc.dram_tensor("b1_scr", [1, HF], BF16, kind="Internal").ap()
    pcur_scr = nc.dram_tensor("pcur_scr", [64, 128], F32, kind="Internal").ap()

    ident_f_np = np.eye(128, dtype=np.float32)
    triu_np = (np.triu(np.ones((128, 128), np.float32)) / 64.0).astype(np.float32)
    maskneg_np = np.where(np.arange(128)[:, None] > np.arange(128)[None, :],
                          np.float32(-1e9), np.float32(0.0))
    apat_np = np.zeros((16, 128), np.float32)
    for h in range(16):
        apat_np[h, (h % 2) * 64:(h % 2) * 64 + 64] = 1.0
    bpat_np = np.zeros((16, 512), np.float32)
    for h in range(16):
        bpat_np[h, (h // 2) * 64:(h // 2) * 64 + 64] = 1.0

    c_ident_f = nc.inline_tensor(ident_f_np, "c_ident_f").ap()
    c_triu = nc.inline_tensor(triu_np, "c_triu").ap()
    c_mask = nc.inline_tensor(maskneg_np, "c_mask").ap()
    c_apat = nc.inline_tensor(apat_np, "c_apat").ap()
    c_bpat = nc.inline_tensor(bpat_np, "c_bpat").ap()
    c_ones_bf = nc.inline_tensor(np.ones((1, 256), ml_dtypes.bfloat16), "c_ones_bf").ap()
    c_ones_f = nc.inline_tensor(np.ones((1, 128), np.float32), "c_ones_f").ap()
    c_ones_col = nc.inline_tensor(np.ones((128, 1), ml_dtypes.bfloat16), "c_ones_col").ap()

    es = ExitStack()
    sbuf = lambda name, shape, dt: es.enter_context(nc.sbuf_tensor(name, shape, dt))
    psum = lambda name, shape, dt: es.enter_context(nc.psum_tensor(name, shape, dt))

    xreg = [sbuf("xreg0", [128, 8 * SC], F32), sbuf("xreg1", [128, 8 * SC], F32)]
    xln = sbuf("xln", [128, 8 * SC], BF16)

    wq_s = sbuf("wq_s", [128, 8 * D], BF16)
    wk_s = sbuf("wk_s", [128, 8 * D], BF16)
    wv_s = sbuf("wv_s", [128, 8 * D], BF16)
    wb_s = sbuf("wb_s", [128, 8 * D], BF16)
    wa_s = sbuf("wa_s", [128, 8 * D], BF16)

    w1s = [sbuf("w1s%d" % i, [128, 8 * 256], FP8) for i in range(4)]
    w2s = [sbuf("w2s%d" % i, [128, 2 * D], FP8) for i in range(4)]
    acol = sbuf("acol", [128, 32], F32)
    b1col = sbuf("b1col", [128, 32], F32)
    hbf = [sbuf("hbf0", [128, 512], BF16), sbuf("hbf1", [128, 512], BF16)]

    qf = sbuf("qf", [128, 8 * SC], BF16)
    kf = sbuf("kf", [128, 8 * SC], BF16)
    qt = [sbuf("qt0", [128, 8 * C], BF16), sbuf("qt1", [128, 8 * C], BF16)]
    ktm = [sbuf("ktm0", [128, D], BF16), sbuf("ktm1", [128, D], BF16)]
    vbf = sbuf("vbf", [128, D], BF16)
    vbb = sbuf("vbb", [128, D], BF16)
    kbb = sbuf("kbb", [128, D], BF16)
    scrA = sbuf("scrA", [128, D], F32)
    wbc = [sbuf("wbc0", [128, 2048], F32), sbuf("wbc1", [128, 2048], F32)]
    cbc = [sbuf("cbc0", [128, 1024], BF16), sbuf("cbc1", [128, 1024], BF16)]
    attn_d = sbuf("attn_d", [128, 2048], BF16)
    ysb = sbuf("ysb", [128, 1024], F32)
    state = sbuf("state", [128, 512], F32)
    state_bf = sbuf("state_bf", [128, 512], BF16)
    stg = [sbuf("stg0", [128, 32], F32), sbuf("stg1", [128, 32], F32)]
    pfm = [sbuf("pfm0", [32, 128], F32), sbuf("pfm1", [32, 128], F32)]
    ptm = sbuf("ptm", [128, 32], F32)
    lam = sbuf("lam", [128, 32], F32)
    bmr = sbuf("bmr", [128, 16], F32)
    asc = sbuf("asc", [16, 2], F32)
    asrep = sbuf("asrep", [16, 512], F32)
    lnrow = sbuf("lnrow", [1, 1280], F32)
    s2st = [sbuf("s2st0", [128, 1024], FP8), sbuf("s2st1", [128, 1024], FP8)]

    bq_r = sbuf("bq_r", [1, D], BF16)
    bk_r = sbuf("bk_r", [1, D], BF16)
    bv_r = sbuf("bv_r", [1, D], BF16)
    bb_r = sbuf("bb_r", [1, D], BF16)
    ba_r = sbuf("ba_r", [1, D], F32)
    b2a_r = sbuf("b2a_r", [1, D], BF16)
    ha2 = sbuf("ha2", [128, 8], F32)

    ident_f = sbuf("ident_f", [128, 128], F32)
    triu = sbuf("triu", [128, 128], F32)
    maskc = sbuf("maskc", [128, 128], F32)
    apat = sbuf("apat", [16, 128], F32)
    bpat = sbuf("bpat", [16, 512], F32)
    ones_bf = sbuf("ones_bf", [1, 256], BF16)
    ones_f = sbuf("ones_f", [1, 128], F32)
    ones_col = sbuf("ones_col", [128, 1], BF16)

    pA = psum("pA", [128, 1024], F32)
    pB = psum("pB", [128, 1024], F32)
    pC = psum("pC", [128, 1024], F32)
    pD = psum("pD", [128, 512], F32)
    pE = psum("pE", [128, 512], F32)

    p = Prog(nc)
    for s in ("s_pro", "s_prog", "s_bias", "s_x0", "s_x1", "s_w0", "s_w1", "s_w2", "s_w3",
              "s_pg0", "s_pg1", "s_po0", "s_po1", "s_scr", "s_scr1", "s_wb0", "s_wb1",
              "s_cbc", "s_cbc1", "s_kt", "s_kt1", "s_out", "s_b2",
              "s_a1",
              "s_pe", "s_dve", "s_se", "s_gp"):
        p.sem(es, s)

    MM = lambda e, **kw: e.matmul(skip_group_check=True, **kw)

    for dst, src in [(ident_f, c_ident_f), (triu, c_triu), (maskc, c_mask),
                     (apat, c_apat), (bpat, c_bpat), (ones_bf, c_ones_bf),
                     (ones_f, c_ones_f), (ones_col, c_ones_col)]:
        p.dma("sync", dst[:], src[:, :], "s_pro")
    p.dma("gpsimd", bq_r[:], bq_d[:, :], "s_bias")
    p.dma("gpsimd", bk_r[:], bk_d[:, :], "s_bias")
    p.dma("gpsimd", bv_r[:], bv_d[:, :], "s_bias")
    p.dma("gpsimd", bb_r[:], bb_d[:, :], "s_bias")
    p.dma("sync", ba_r[:], ba_d[:, :], "s_pro")
    p.wait("tensor", "s_pro")
    p.wait("scalar", "s_pro")
    p.wait("gpsimd", "s_pro")
    p.wait("vector", "s_pro")
    p.wait("vector", "s_bias")
    p.comp("vector", lambda e: e.tensor_scalar_mul(out=bq_r[:], in0=bq_r[:], scalar1=0.125))
    p.comp("vector", lambda e: e.tensor_scalar_mul(out=bk_r[:], in0=bk_r[:], scalar1=0.125),
            inc="s_dve")
    free_mark = {0: ("s_dve", 0), 1: ("s_dve", 0)}
    t = 0
    for wsb, wdr, scl in [(wq_s, wqT, 0.125), (wk_s, wkT, 0.125), (wv_s, wvT, None),
                          (wb_s, wbT, None), (wa_s, waT, None)]:
        for r in range(8):
            par = t % 2
            eng = "sync" if par == 0 else "scalar"
            p.wait(eng, *free_mark[par])
            p.dma(eng, xreg[par][:, 0:1024], wdr[r * 128:(r + 1) * 128, :],
                  "s_pg%d" % par)
            p.wait("vector", "s_pg%d" % par)
            if scl is None:
                p.comp("vector", lambda e, wsb=wsb, r=r, par=par: e.tensor_copy(
                    out=wsb[:, r * D:(r + 1) * D], in_=xreg[par][:, 0:1024]))
            else:
                p.comp("vector", lambda e, wsb=wsb, r=r, par=par, scl=scl: e.tensor_scalar_mul(
                    out=wsb[:, r * D:(r + 1) * D], in0=xreg[par][:, 0:1024], scalar1=scl))
            free_mark[par] = ("s_dve", p.now("s_dve"))
            t += 1

    for hg in range(4):
        hg_sl = slice(hg * 1024, (hg + 1) * 1024)
        mm_mark = {}
        se_mark = {}
        p.wait("scalar", "s_dve")
        for r in range(8):
            par = r % 2
            eng = "sync" if par == 0 else "scalar"
            p.wait(eng, *free_mark[par])
            p.dma(eng, xreg[par][:, 0:1024],
                  w1T[r * 128:(r + 1) * 128, hg_sl], "s_pg%d" % par)
            p.wait("scalar", "s_pg%d" % par)
            if r >= 2:
                p.wait("scalar", "s_pe", mm_mark[r - 2])
            p.comp("scalar", lambda e, par=par: e.activation(
                out=xln[:, par * 1024:(par + 1) * 1024], in_=xreg[par][:, 0:1024], func=AF.Abs),
                inc="s_se")
            se_mark[r] = p.now("s_se")
            free_mark[par] = ("s_se", se_mark[r])
            p.wait("tensor", "s_se")
            for half in range(2):
                p.stage("tensor", lambda e, par=par, half=half, r=r: MM(
                    e, out=pB[0:1, half * 512:(half + 1) * 512],
                    lhsT=ones_col[:], rhs=xln[:, par * 1024 + half * 512: par * 1024 + (half + 1) * 512],
                    start=(r == 0), stop=(r == 7 and half == 1)),
                    inc="s_pe" if half == 1 else None)
            mm_mark[r] = p.now("s_pe")
        p.wait("vector", "s_pe")
        p.wait("vector", "s_a1")
        p.comp("vector", lambda e: e.tensor_scalar_mul(
            out=lnrow[0:1, 0:1024], in0=pB[0:1, 0:1024], scalar1=1.0 / D), inc="s_dve")
        p.wait("sync", "s_dve")
        p.dma("sync", a1_scr[0:1, hg_sl], lnrow[0:1, 0:1024], "s_a1")
        for r in range(8):
            par = r % 2
            eng = "sync" if par == 0 else "scalar"
            p.wait(eng, *free_mark[par])
            p.dma(eng, xreg[par][:, 0:1024],
                  w1T[r * 128:(r + 1) * 128, hg_sl], "s_pg%d" % par)
            p.wait("scalar", "s_pg%d" % par)
            p.wait("scalar", "s_po%d" % par)
            p.comp("scalar", lambda e, par=par: e.activation(
                out=s2st[par][:], in_=xreg[par][:, 0:1024], func=AF.Sign),
                inc="s_se")
            free_mark[par] = ("s_se", p.now("s_se"))
            p.wait("gpsimd", "s_se")
            p.dma("gpsimd", s1_scr[r * 128:(r + 1) * 128, hg_sl],
                  s2st[par][:], "s_po%d" % par)

    p.wait("sync", "s_a1")
    p.dma("sync", acol[:, :], a1_scr[0:1, :].rearrange("q (c p) -> (q p) c", p=128),
          "s_a1", allow_slow_non_contiguous=True)
    p.dma("sync", b1col[:, :], b1_d[0:1, :].rearrange("q (c p) -> (q p) c", p=128),
          "s_a1", allow_slow_non_contiguous=True)

    se_mark = {}
    mm_mark = {}
    p.wait("scalar", "s_dve")
    for kc in range(32):
        par = kc % 2
        eng = "sync" if par == 0 else "scalar"
        p.wait(eng, *free_mark[par])
        p.dma(eng, xreg[par][:, 0:1024],
              w2T[kc * 128:(kc + 1) * 128, :], "s_pg%d" % par)
        p.wait("scalar", "s_pg%d" % par)
        p.wait("scalar", "s_po%d" % par)
        if kc >= 2:
            p.wait("scalar", "s_pe", mm_mark[kc - 2])
        p.comp("scalar", lambda e, par=par: e.activation(
            out=xln[:, par * 1024:(par + 1) * 1024], in_=xreg[par][:, 0:1024], func=AF.Abs))
        p.comp("scalar", lambda e, par=par: e.activation(
            out=s2st[par][:], in_=xreg[par][:, 0:1024], func=AF.Sign),
            inc="s_se")
        free_mark[par] = ("s_se", p.now("s_se"))
        se_mark[kc] = p.now("s_se")
        p.wait("tensor", "s_se")
        for half in range(2):
            p.stage("tensor", lambda e, par=par, half=half, kc=kc: MM(
                e, out=pB[0:1, half * 512:(half + 1) * 512],
                lhsT=ones_col[:], rhs=xln[:, par * 1024 + half * 512: par * 1024 + (half + 1) * 512],
                start=(kc == 0), stop=(kc == 31 and half == 1)),
                inc="s_pe" if half == 1 else None)
        mm_mark[kc] = p.now("s_pe")
        p.wait("gpsimd", "s_se")
        p.dma("gpsimd", s2_scr[kc * 128:(kc + 1) * 128, :], s2st[par][:], "s_po%d" % par)
    p.wait("vector", "s_pe")
    p.wait("vector", "s_a1")
    p.comp("vector", lambda e: e.tensor_scalar_mul(
        out=lnrow[0:1, 0:1024], in0=pB[0:1, 0:1024], scalar1=1.0 / HF), inc="s_dve")
    p.wait("tensor", "s_dve")
    for dc in range(8):
        p.stage("tensor", lambda e, dc=dc: MM(
            e, out=pA[:, dc:dc + 1], lhsT=lnrow[0:1, dc * 128:(dc + 1) * 128],
            rhs=ones_f[0:1, 0:1], start=(dc == 0), stop=(dc == 7)),
            inc="s_pe" if dc == 7 else None)
    p.wait("vector", "s_pe")
    p.comp("vector", lambda e: e.tensor_scalar_mul(out=ha2[:], in0=pA[:, 0:8], scalar1=0.5))
    p.dma("sync", scrA[0:1, 0:1024], b2_d[:, :], "s_b2")
    p.wait("vector", "s_b2")
    p.comp("vector", lambda e: e.reciprocal(out=ysb[0:1, 0:1024], in_=lnrow[0:1, 0:1024]),
            inc="s_dve")
    p.wait("vector", "s_dve")
    p.comp("vector", lambda e: e.tensor_tensor(
        out=b2a_r[:], in0=scrA[0:1, 0:1024], in1=ysb[0:1, 0:1024], op=ALU.mult))
    p.comp("vector", lambda e: e.memset(state[:], 0.0))
    p.comp("vector", lambda e: e.memset(state_bf[:], 0.0), inc="s_dve")

    p.wait("sync", "s_se")
    p.wait("sync", "s_dve")
    p.dma("sync", xreg[0][:].rearrange("p (r t) -> p r t", r=8),
          xT[:, 0:SC].rearrange("(r p) t -> p r t", p=128), "s_x0")
    p.wait("gpsimd", "s_po0")
    p.wait("gpsimd", "s_po1")

    def emit_ln(par, wmarks=None):
        xr = xreg[par]
        xbf, xsq = qf, attn_d
        p.wait("vector", "s_pe")
        p.wait("scalar", "s_pe")
        xbm = {}
        xsm = {}
        for r in range(8):
            if wmarks is not None:
                p.wait("vector", "s_dve", wmarks[r])
                p.wait("scalar", "s_dve", wmarks[r])
            elif r == 0:
                p.wait("vector", "s_dve")
                p.wait("scalar", "s_dve")
            p.comp("vector", lambda e, r=r: e.tensor_copy(
                out=xbf[:, r * SC:(r + 1) * SC], in_=xr[:, r * SC:(r + 1) * SC]),
                inc="s_dve")
            xbm[r] = p.now("s_dve")
            p.comp("scalar", lambda e, r=r: e.activation(
                out=xsq[:, r * SC:(r + 1) * SC], in_=xr[:, r * SC:(r + 1) * SC],
                func=AF.Square), inc="s_se")
            xsm[r] = p.now("s_se")
        for r in range(8):
            p.wait("tensor", "s_dve", xbm[r])
            p.wait("tensor", "s_se", xsm[r])
            p.stage("tensor", lambda e, r=r: MM(
                e, out=pD[0:1, 0:SC], lhsT=ones_col[:], rhs=xbf[:, r * SC:(r + 1) * SC],
                start=(r == 0), stop=(r == 7)))
            p.stage("tensor", lambda e, r=r: MM(
                e, out=pD[0:1, SC:2 * SC], lhsT=ones_col[:], rhs=xsq[:, r * SC:(r + 1) * SC],
                start=False, stop=(r == 7)), inc="s_pe" if r == 7 else None)
        p.wait("vector", "s_pe")
        p.comp("vector", lambda e: e.tensor_scalar_mul(
            out=lnrow[0:1, 0:256], in0=pD[0:1, 0:SC], scalar1=1.0 / D))
        p.comp("vector", lambda e: e.tensor_tensor(
            out=lnrow[0:1, 256:512], in0=lnrow[0:1, 0:256], in1=lnrow[0:1, 0:256],
            op=ALU.mult))
        p.comp("vector", lambda e: e.scalar_tensor_tensor(
            out=lnrow[0:1, 512:768], in0=pD[0:1, SC:2 * SC], scalar=1.0 / D,
            in1=lnrow[0:1, 256:512], op0=ALU.mult, op1=ALU.subtract))
        p.comp("vector", lambda e: e.tensor_scalar_add(
            out=lnrow[0:1, 512:768], in0=lnrow[0:1, 512:768], scalar1=EPS), inc="s_dve")
        p.wait("scalar", "s_dve")
        p.comp("scalar", lambda e: e.activation(
            out=lnrow[0:1, 768:1024], in_=lnrow[0:1, 512:768], func=AF.Ln))
        p.comp("scalar", lambda e: e.activation(
            out=lnrow[0:1, 768:1024], in_=lnrow[0:1, 768:1024], func=AF.Exp, scale=-0.5),
            inc="s_se")
        p.wait("vector", "s_se")
        p.comp("vector", lambda e: e.scalar_tensor_tensor(
            out=lnrow[0:1, 1024:1280], in0=lnrow[0:1, 0:256], scalar=-1.0,
            in1=lnrow[0:1, 768:1024], op0=ALU.mult, op1=ALU.mult), inc="s_dve")
        p.wait("tensor", "s_dve")
        p.stage("tensor", lambda e: MM(
            e, out=pE[:, 0:SC], lhsT=ones_f[0:1, 0:128], rhs=lnrow[0:1, 768:1024],
            start=True, stop=True))
        p.stage("tensor", lambda e: MM(
            e, out=pE[:, SC:2 * SC], lhsT=ones_f[0:1, 0:128], rhs=lnrow[0:1, 1024:1280],
            start=False, stop=True), inc="s_pe")
        p.wait("vector", "s_pe")
        for r in range(8):
            p.comp("vector", lambda e, r=r: e.tensor_tensor(
                out=scrA[:, 0:SC], in0=xr[:, r * SC:(r + 1) * SC], in1=pE[:, 0:SC],
                op=ALU.mult))
            p.comp("vector", lambda e, r=r: e.tensor_tensor(
                out=xln[:, r * SC:(r + 1) * SC], in0=scrA[:, 0:SC], in1=pE[:, SC:2 * SC],
                op=ALU.add), inc="s_dve" if r == 7 else None)

    ffn_last_use = [0, 0, 0, 0]
    l2_done_for_h = {}
    NS2 = 16

    def emit_ffn(par):
        p.wait("tensor", "s_dve")
        gelu_done = {}
        l1_done = {}
        wmark = {}

        def load_slice(s):
            sp = s % 4
            p.wait("gpsimd", "s_pe", ffn_last_use[sp])
            ws = "s_w%d" % sp
            p.dma("gpsimd", w1s[sp][:].rearrange("p (r h) -> p r h", r=8),
                  s1_scr[:, s * 256:(s + 1) * 256].rearrange("(r p) h -> p r h", p=128),
                  ws)
            p.dma("gpsimd", w2s[sp][:].rearrange("p (j d) -> p j d", j=2),
                  s2_scr[s * 256:(s + 1) * 256, :].rearrange("(j p) d -> p j d", p=128),
                  ws)
            wmark[s] = p.now(ws)

        for s in range(3):
            load_slice(s)
        for s in range(NS2 + 1):
            if s < NS2:
                sp = s % 4
                l1t, bank = ((pA, 0), (pA, 512), (pD, 0), (pE, 0))[s % 4]
                p.wait("tensor", "s_w%d" % sp, wmark[s])
                if s >= 4:
                    p.wait("tensor", "s_se", gelu_done[s - 4])
                for hc in range(2):
                    for r in range(8):
                        p.stage("tensor", lambda e, sp=sp, hc=hc, r=r, bank=bank, l1t=l1t: MM(
                            e, out=l1t[:, bank + hc * 256: bank + (hc + 1) * 256],
                            lhsT=w1s[sp][:, r * 256 + hc * 128: r * 256 + (hc + 1) * 128],
                            rhs=xln[:, r * SC:(r + 1) * SC],
                            start=(hc == 0 and r == 0), stop=(r == 7)),
                            inc="s_pe" if (hc == 1 and r == 7) else None)
                l1_done[s] = p.now("s_pe")
                p.wait("scalar", "s_pe", l1_done[s])
                p.wait("scalar", "s_a1")
                if s >= 2:
                    p.wait("scalar", "s_pe", l2_done_for_h[s - 2])
                for hc in range(2):
                    p.comp("scalar", lambda e, s=s, hc=hc, bank=bank, l1t=l1t: e.activation(
                        out=hbf[s % 2][:, hc * 256:(hc + 1) * 256],
                        in_=l1t[:, bank + hc * 256: bank + (hc + 1) * 256],
                        func=AF.Gelu, scale=acol[:, s * 2 + hc: s * 2 + hc + 1],
                        bias=b1col[:, s * 2 + hc: s * 2 + hc + 1]))
                gelu_done[s] = p.now("s_se")
            if s >= 1:
                spp = (s - 1) % 4
                p.wait("tensor", "s_se", gelu_done[s - 1])
                for j in range(2):
                    for dc in range(8):
                        tgt = pB if dc < 4 else pC
                        p.stage("tensor", lambda e, spp=spp, j=j, dc=dc, s=s, tgt=tgt: MM(
                            e, out=tgt[:, (dc % 4) * 256:(dc % 4) * 256 + 256],
                            lhsT=w2s[spp][:, j * D + dc * 128: j * D + (dc + 1) * 128],
                            rhs=hbf[(s - 1) % 2][:, j * 256:(j + 1) * 256],
                            start=(s == 1 and j == 0 and dc % 2 == 0), stop=False),
                            inc="s_pe" if (j == 1 and dc == 7) else None)
                ffn_last_use[spp] = p.now("s_pe")
                l2_done_for_h[s - 1] = p.now("s_pe")
                if s + 2 < NS2:
                    load_slice(s + 2)
        for dc in range(8):
            tgt = pB if dc < 4 else pC
            p.stage("tensor", lambda e, dc=dc, tgt=tgt: MM(
                e, out=tgt[:, (dc % 4) * 256:(dc % 4) * 256 + 256],
                lhsT=b2a_r[0:1, dc * 128:(dc + 1) * 128],
                rhs=ones_bf[0:1, 0:SC], start=False, stop=True),
                inc="s_pe" if dc == 7 else None)
        p.wait("vector", "s_pe")
        emarks = []
        for dc in range(8):
            tgt = pB if dc < 4 else pC
            p.comp("vector", lambda e, dc=dc, tgt=tgt, par=par: e.scalar_tensor_tensor(
                out=xreg[par][:, dc * SC:(dc + 1) * SC],
                in0=tgt[:, (dc % 4) * 256:(dc % 4) * 256 + 256],
                scalar=ha2[:, dc:dc + 1],
                in1=xreg[par][:, dc * SC:(dc + 1) * SC],
                op0=ALU.mult, op1=ALU.add), inc="s_dve" if dc == 7 else None)
            emarks.append(p.now("s_dve"))
        return emarks

    dv_attnd = {0: 0, 1: 0}
    dv_vbb = [0]
    dv_xadd = [0]
    for sc in range(NSC):
        par = sc % 2
        if sc + 1 < NSC:
            p.wait("sync", "s_out", 16 * sc)
            p.dma("sync", xreg[(sc + 1) % 2][:].rearrange("p (r t) -> p r t", r=8),
                  xT[:, (sc + 1) * SC:(sc + 2) * SC].rearrange("(r p) t -> p r t", p=128),
                  "s_x%d" % ((sc + 1) % 2))
        xs = "s_x%d" % (sc % 2)
        xv = 16 * (sc // 2 + 1)
        p.wait("vector", xs, xv)
        p.wait("scalar", xs, xv)
        p.wait("tensor", xs, xv)

        PH = DBG.get("phase", 99)
        xam = {}
        if PH >= 1:
            emit_ln(par)
        f1m = None
        if PH >= 2:
            f1m = emit_ffn(par)
        if PH < 3:
            continue

        emit_ln(par, wmarks=f1m)
        p.wait("tensor", "s_dve")
        for proj in range(2):
            wsb = wq_s if proj == 0 else wk_s
            brow = bq_r if proj == 0 else bk_r
            for oc in range(8):
                if proj == 0:
                    tgt, off = (pA, oc * 256) if oc < 4 else (pB, (oc - 4) * 256)
                else:
                    if oc < 4:
                        tgt, off = pC, oc * 256
                    elif oc < 6:
                        tgt, off = pD, (oc - 4) * 256
                    else:
                        tgt, off = pE, (oc - 6) * 256
                for r in range(8):
                    p.stage("tensor", lambda e, wsb=wsb, oc=oc, r=r, tgt=tgt, off=off: MM(
                        e, out=tgt[:, off:off + 256],
                        lhsT=wsb[:, r * D + oc * 128: r * D + (oc + 1) * 128],
                        rhs=xln[:, r * SC:(r + 1) * SC],
                        start=(r == 0 and off % 512 == 0), stop=False))
                p.stage("tensor", lambda e, brow=brow, oc=oc, tgt=tgt, off=off: MM(
                    e, out=tgt[:, off:off + 256],
                    lhsT=brow[0:1, oc * 128:(oc + 1) * 128],
                    rhs=ones_bf[0:1, 0:SC], start=False, stop=True),
                    inc="s_pe" if oc == 7 else None)
        p.wait("scalar", "s_pe")
        p.wait("vector", "s_pe")
        p.comp("scalar", lambda e: e.activation(out=qf[:, 0:1024], in_=pA[:],
                                                 func=AF.Identity))
        p.comp("scalar", lambda e: e.activation(out=qf[:, 1024:2048], in_=pB[:],
                                                 func=AF.Identity), inc="s_se")
        se_qk = p.now("s_se")
        p.comp("vector", lambda e: e.tensor_copy(out=kf[:, 0:1024], in_=pC[:]))
        p.comp("vector", lambda e: e.tensor_copy(out=kf[:, 1024:1536], in_=pD[:]))
        p.comp("vector", lambda e: e.tensor_copy(out=kf[:, 1536:2048], in_=pE[:]),
                inc="s_dve")

        if PH < 4:
            continue
        dv_acopy = {}
        dv_lam = {}
        pe_tri = {}
        dv_pfm = {}
        pe_al = {}
        p.wait("tensor", "s_dve")
        for c in range(2):
            if c == 1:
                p.wait("tensor", "s_dve", dv_acopy[0])
            for half in range(2):
                for r in range(8):
                    p.stage("tensor", lambda e, half=half, r=r, c=c: MM(
                        e, out=pC[:, half * 512:(half + 1) * 512],
                        lhsT=xln[:, r * SC + c * 128: r * SC + c * 128 + 128],
                        rhs=wa_s[:, r * D + half * 512: r * D + (half + 1) * 512],
                        start=(r == 0), stop=False))
                p.stage("tensor", lambda e, half=half: MM(
                    e, out=pC[:, half * 512:(half + 1) * 512],
                    lhsT=ones_f[0:1, 0:128],
                    rhs=ba_r[0:1, half * 512:(half + 1) * 512],
                    start=False, stop=True),
                    inc="s_pe" if half == 1 else None)
            pe_al[c] = p.now("s_pe")
            p.wait("vector", "s_pe", pe_al[c])
            p.comp("vector", lambda e: e.tensor_copy(out=ysb[:], in_=pC[:]),
                    inc="s_dve")
            dv_acopy[c] = p.now("s_dve")
            p.comp("vector", lambda e: e.tensor_scalar(
                out=ysb[:], in0=ysb[:], scalar1=10.0, scalar2=-10.0,
                op0=ALU.min, op1=ALU.max), inc="s_dve")
            p.wait("scalar", "s_dve")
            p.comp("scalar", lambda e: e.activation(out=ysb[:], in_=ysb[:],
                                                     func=AF.Exp, scale=-1.0))
            p.comp("scalar", lambda e: e.activation(out=ysb[:], in_=ysb[:],
                                                     func=AF.Ln, bias=1.0), inc="s_se")
            p.wait("vector", "s_se")
            p.comp("vector", lambda e, c=c: e.tensor_reduce(
                out=lam[:, c * 16:(c + 1) * 16, None],
                in_=ysb[:].rearrange("p (h d) -> p h d", h=16),
                axis=mybir.AxisListType.X, op=ALU.add), inc="s_dve")
            dv_lam[c] = p.now("s_dve")
        for c in range(2):
            p.wait("tensor", "s_dve", dv_lam[c])
            p.stage("tensor", lambda e, c=c: MM(
                e, out=pD[:, c * 16:(c + 1) * 16], lhsT=triu[:],
                rhs=lam[:, c * 16:(c + 1) * 16], start=True, stop=True),
                inc="s_pe")
            pe_tri[c] = p.now("s_pe")
            p.wait("vector", "s_pe", pe_tri[c])
            if c == 0:
                p.wait("vector", "s_gp")
            p.comp("vector", lambda e, c=c: e.tensor_copy(
                out=ptm[:, c * 16:(c + 1) * 16], in_=pD[:, c * 16:(c + 1) * 16]))
            p.comp("vector", lambda e, c=c: e.tensor_copy(
                out=stg[c][:, 0:16], in_=pD[:, c * 16:(c + 1) * 16]), inc="s_dve")
            p.wait("scalar", "s_pe", pe_tri[c])
            p.comp("scalar", lambda e, c=c: e.activation(
                out=stg[c][:, 16:32], in_=pD[:, c * 16:(c + 1) * 16],
                func=AF.Exp, scale=-1.0), inc="s_se")
        for c in range(2):
            p.wait("tensor", "s_se")
            p.wait("tensor", "s_dve")
            tcorner = pB if c == 0 else pC
            p.stage("tensor", lambda e, c=c, tcorner=tcorner: e.transpose(
                out=tcorner[0:32, 896:1024], in_=stg[c][:], identity=ident_f[:]),
                inc="s_pe")
            p.wait("vector", "s_pe")
            p.comp("vector", lambda e, c=c, tcorner=tcorner: e.tensor_copy(
                out=pfm[c][:], in_=tcorner[0:32, 896:1024]), inc="s_dve")
            dv_pfm[c] = p.now("s_dve")
            p.wait("scalar", "s_dve", dv_pfm[c])
            p.comp("scalar", lambda e, c=c: e.activation(
                out=asc[:, c:c + 1], in_=pfm[c][0:16, 127:128],
                func=AF.Exp, scale=-1.0), inc="s_se")
        scrm = {}
        cbcm = {}
        ktmm = {}
        for c in range(2):
            scr_s = "s_scr" if c == 0 else "s_scr1"
            cbc_s = "s_cbc" if c == 0 else "s_cbc1"
            p.wait("sync", "s_dve", dv_pfm[c])
            if c == 0:
                p.wait("sync", "s_cbc")
                p.wait("sync", "s_cbc1")
            p.dma("sync", pcur_scr[c * 32:(c + 1) * 32, :], pfm[c][:], scr_s)
            scrm[c] = p.now(scr_s)
            p.wait("gpsimd", scr_s, scrm[c])
            p.wait("gpsimd", "s_dve")
            for hh in range(2):
                p.dma("gpsimd",
                      cbc[c][64 * hh:64 * hh + 64, :].rearrange("q (b i) -> q b i", b=8),
                      pcur_scr[c * 32 + 16 + hh:c * 32 + 32:2, :]
                      .unsqueeze(0).broadcast_to((64, 8, 128)),
                      cbc_s)
            cbcm[c] = p.now(cbc_s)
        p.wait("scalar", "s_dve")
        for c in range(2):
            kt_s = "s_kt" if c == 0 else "s_kt1"
            for r in range(8):
                p.stage("scalar", lambda e, r=r, c=c: e.dma_start_transpose(
                    out=ktm[c][:, r * 128:(r + 1) * 128],
                    in_=kf[:, r * SC + c * 128: r * SC + c * 128 + 128]),
                    inc=kt_s, amt=16)
            ktmm[c] = p.now(kt_s)

        explm = {}
        gpms = {}
        for c in range(2):
            p.wait("sync", "s_scr" if c == 0 else "s_scr1", scrm[c])
            p.wait("sync", "s_dve", dv_attnd[c])
            wbsem = "s_wb0" if c == 0 else "s_wb1"
            p.dma("sync", wbc[c][:].rearrange("p (h i) -> p h i", h=16),
                  pcur_scr[c * 32:c * 32 + 16, :]
                  .unsqueeze(0).broadcast_to((128, 16, 128)), wbsem)
            wm = p.now(wbsem)
            p.wait("gpsimd", wbsem, wm)
            p.comp("gpsimd", lambda e, c=c: e.tensor_tensor(
                out=wbc[c][:].rearrange("p (h i) -> p h i", h=16),
                in0=ptm[:, c * 16:(c + 1) * 16, None].broadcast_to((128, 16, 128)),
                in1=wbc[c][:].rearrange("p (h i) -> p h i", h=16), op=ALU.subtract))
            p.comp("gpsimd", lambda e, c=c: e.tensor_tensor(
                out=wbc[c][:].rearrange("p (h i) -> p h i", h=16),
                in0=wbc[c][:].rearrange("p (h i) -> p h i", h=16),
                in1=maskc[:, None, :].broadcast_to((128, 16, 128)), op=ALU.add),
                inc="s_gp")
            gpms[c] = p.now("s_gp")

        for c in range(2):
            n = sc * 2 + c
            p.wait("tensor", "s_se", se_qk)
            p.wait("tensor", "s_dve", dv_pfm[0])
            p.wait("tensor", "s_dve", dv_xadd[0])
            for pi in range(2):
                wsb, brow, tgt = ((wv_s, bv_r, pA), (wb_s, bb_r, pB))[pi]
                for half in range(2):
                    for r in range(8):
                        p.stage("tensor",
                                lambda e, wsb=wsb, half=half, r=r, c=c, tgt=tgt: MM(
                                    e, out=tgt[:, half * 512:(half + 1) * 512],
                                    lhsT=xln[:, r * SC + c * 128: r * SC + c * 128 + 128],
                                    rhs=wsb[:, r * D + half * 512: r * D + (half + 1) * 512],
                                    start=(r == 0), stop=False))
                    p.stage("tensor", lambda e, brow=brow, half=half, tgt=tgt: MM(
                        e, out=tgt[:, half * 512:(half + 1) * 512],
                        lhsT=ones_bf[0:1, 0:128],
                        rhs=brow[0:1, half * 512:(half + 1) * 512],
                        start=False, stop=True),
                        inc="s_pe" if half == 1 else None)
            pe_vba = p.now("s_pe")
            p.wait("vector", "s_pe", pe_vba - 1)
            p.comp("vector", lambda e: e.tensor_copy(out=vbf[:], in_=pA[:]), inc="s_dve")
            dv_vbf = p.now("s_dve")
            p.wait("scalar", "s_pe", pe_vba)
            p.wait("scalar", "s_dve", dv_vbb[0])
            p.comp("scalar", lambda e: e.activation(out=scrA[:], in_=pB[:], func=AF.Exp))
            p.comp("scalar", lambda e: e.activation(out=scrA[:], in_=scrA[:],
                                                     func=AF.Ln, bias=1.0), inc="s_se")
            se_beta = p.now("s_se")
            if c == 0:
                p.wait("scalar", "s_gp", gpms[0])
                p.comp("scalar", lambda e: e.activation(
                    out=wbc[0][:], in_=wbc[0][:], func=AF.Exp), inc="s_se")
                explm[0] = p.now("s_se")
            p.wait("tensor", "s_dve", dv_vbf)
            p.wait("tensor", "s_se", se_beta)
            seen_banks = set()
            order = list(range(0, 16, 2)) + [-1] + list(range(1, 16, 2))
            for h in order:
                if h < 0:
                    p.stage("tensor", lambda e: MM(
                        e, out=pE[0:1, 511:512], lhsT=ones_col[:],
                        rhs=ones_col[:], start=True, stop=True))
                    continue
                g = h // 4
                tgt = pA if g < 2 else pB
                off = (g % 2) * 512
                j = h % 4
                pb = 64 * (h % 2)
                bank = (id(tgt), off // 512)
                st_flag = bank not in seen_banks
                seen_banks.add(bank)
                p.stage("tensor", lambda e, h=h, j=j, c=c, tgt=tgt, off=off, pb=pb,
                        st_flag=st_flag: MM(
                        e, out=tgt[:, off + j * 128: off + (j + 1) * 128],
                        lhsT=kf[pb:pb + 64,
                                (h // 2) * SC + c * 128: (h // 2) * SC + c * 128 + 128],
                        rhs=qf[pb:pb + 64,
                               (h // 2) * SC + c * 128: (h // 2) * SC + c * 128 + 128],
                        start=st_flag, stop=True),
                        inc="s_pe" if h == 15 else None)
            pe_attn = p.now("s_pe")
            if DBG.get("sub", 99) < 6:
                continue
            p.wait("vector", "s_se", se_beta)
            p.comp("vector", lambda e: e.tensor_tensor(
                out=vbb[:], in0=vbf[:], in1=scrA[:], op=ALU.mult))
            dv_vbb[0] = p.now("s_dve")
            p.comp("vector", lambda e: e.tensor_reduce(
                out=bmr[:, :, None], in_=scrA[:].rearrange("p (h d) -> p h d", h=16),
                axis=mybir.AxisListType.X, op=ALU.add))
            p.comp("vector", lambda e: e.tensor_scalar_mul(
                out=bmr[:], in0=bmr[:], scalar1=1.0 / 64.0), inc="s_dve")
            p.wait("vector", "s_kt" if c == 0 else "s_kt1", ktmm[c])
            p.comp("vector", lambda e, c=c: e.tensor_tensor(
                out=kbb[:].rearrange("p (h d) -> p h d", h=16),
                in0=ktm[c][:].rearrange("p (h d) -> p h d", h=16),
                in1=bmr[:, :, None].broadcast_to((128, 16, 64)), op=ALU.mult))
            p.comp("vector", lambda e, c=c: e.tensor_scalar(
                out=asrep[:], in0=bpat[:], scalar1=asc[:, c:c + 1], scalar2=None,
                op0=ALU.mult), inc="s_dve")
            p.wait("vector", "s_se", explm[c])
            for g in range(4):
                tgt = pA if g < 2 else pB
                off = (g % 2) * 512
                p.wait("vector", "s_pe", pe_attn)
                p.comp("vector", lambda e, g=g, tgt=tgt, off=off, c=c: e.tensor_tensor(
                    out=attn_d[:, g * 512:(g + 1) * 512], in0=tgt[:, off:off + 512],
                    in1=wbc[c][:, g * 512:(g + 1) * 512], op=ALU.mult),
                    inc="s_dve" if g == 3 else None)
            dv_attnd[c] = p.now("s_dve")
            if c == 0:
                p.wait("scalar", "s_gp", gpms[1])
                p.comp("scalar", lambda e: e.activation(
                    out=wbc[1][:], in_=wbc[1][:], func=AF.Exp), inc="s_se")
                explm[1] = p.now("s_se")
            p.wait("vector", "s_cbc" if c == 0 else "s_cbc1", cbcm[c])
            p.comp("vector", lambda e, c=c: e.tensor_tensor(
                out=qt[c][:].rearrange("p (r t) -> p r t", r=8),
                in0=qf[:].rearrange("p (r t) -> p r t", r=8)[:, :, c * 128:(c + 1) * 128],
                in1=cbc[c][:].rearrange("p (r t) -> p r t", r=8), op=ALU.mult),
                inc="s_dve")
            if DBG.get("sub", 99) < 8:
                continue
            p.wait("tensor", "s_dve")
            for h in range(16):
                p.stage("tensor", lambda e, h=h: MM(
                    e, out=pC[:, h * 64:(h + 1) * 64],
                    lhsT=attn_d[:, h * 128:(h + 1) * 128],
                    rhs=vbb[:, h * 64:(h + 1) * 64],
                    start=(h % 8 == 0), stop=False))
            if n > 0:
                for h in list(range(0, 16, 2)) + [-1] + list(range(1, 16, 2)):
                    if h < 0:
                        p.stage("tensor", lambda e: MM(
                            e, out=pE[0:1, 510:511], lhsT=ones_col[:],
                            rhs=ones_col[:], start=True, stop=True))
                        continue
                    pb = 64 * (h % 2)
                    p.stage("tensor", lambda e, h=h, pb=pb, c=c: MM(
                        e, out=pC[:, h * 64:(h + 1) * 64],
                        lhsT=qt[c][pb:pb + 64, (h // 2) * 128:(h // 2) * 128 + 128],
                        rhs=state_bf[pb:pb + 64, (h // 2) * 64:(h // 2) * 64 + 64],
                        start=False, stop=(h == 15)))
            if DBG.get("sub", 99) < 9:
                continue
            for h in range(16):
                tp = {} if h % 2 == 0 else {"tile_position": (0, 64)}
                pb = 64 * (h % 2)
                p.stage("tensor", lambda e, h=h, tp=tp, pb=pb: MM(
                    e, out=pD[pb:pb + 64, (h // 2) * 64:(h // 2) * 64 + 64],
                    lhsT=kbb[:, h * 64:(h + 1) * 64],
                    rhs=vbf[:, h * 64:(h + 1) * 64],
                    start=(h <= 1), stop=(h == 15), **tp),
                    inc="s_pe" if h == 15 else None)
            if n > 0:
                p.stage("tensor", lambda e: MM(
                    e, out=pE[:, 0:512], lhsT=apat[:], rhs=asrep[:],
                    start=True, stop=True), inc="s_pe")
            pe_scan = p.now("s_pe")
            if DBG.get("sub", 99) < 10:
                continue
            p.wait("vector", "s_pe", pe_scan)
            if n > 0:
                p.comp("vector", lambda e: e.tensor_tensor(
                    out=state[:], in0=state[:], in1=pE[:, 0:512], op=ALU.mult))
                p.comp("vector", lambda e: e.tensor_tensor(
                    out=state[:], in0=state[:], in1=pD[:], op=ALU.add))
            else:
                p.comp("vector", lambda e: e.tensor_copy(out=state[:], in_=pD[:]))
            p.comp("vector", lambda e: e.tensor_copy(out=state_bf[:], in_=state[:]),
                    inc="s_dve")
            p.comp("vector", lambda e: e.tensor_copy(out=ysb[:], in_=pC[:]), inc="s_dve")
            if DBG.get("sub", 99) < 11:
                continue
            p.wait("tensor", "s_dve")
            for r in range(8):
                tb = pB if r < 4 else pA
                toff = (r % 4) * 128
                p.stage("tensor", lambda e, r=r, tb=tb, toff=toff: e.transpose(
                    out=tb[:, toff:toff + 128], in_=ysb[:, r * 128:(r + 1) * 128],
                    identity=ident_f[:]), inc="s_pe" if r == 7 else None)
            p.wait("vector", "s_pe")
            for r in range(8):
                tb = pB if r < 4 else pA
                toff = (r % 4) * 128
                p.comp("vector", lambda e, r=r, c=c, tb=tb, toff=toff, par=par:
                        e.tensor_tensor(
                            out=xreg[par][:, r * SC + c * 128: r * SC + c * 128 + 128],
                            in0=xreg[par][:, r * SC + c * 128: r * SC + c * 128 + 128],
                            in1=tb[:, toff:toff + 128], op=ALU.add),
                        inc="s_dve" if r == 7 else None)
                if c == 1:
                    xam[r] = p.now("s_dve")
            dv_xadd[0] = p.now("s_dve")

        if PH < 5:
            continue
        emit_ln(par, wmarks=xam if PH >= 4 else None)
        emit_ffn(par)
        p.wait("sync", "s_dve")
        p.dma("sync", outT[:, sc * SC:(sc + 1) * SC].rearrange("(r p) t -> p r t", p=128),
              xreg[par][:].rearrange("p (r t) -> p r t", r=8), "s_out")

    if p.now("s_out"):
        p.wait("sync", "s_out")

    with es:
        block = es.enter_context(nc.Block())
        p.replay(block)
    return nc


_NC_CACHE = {}


def kernel(**inputs):
    x = np.asarray(inputs["L0_attn_x"])
    Wq = np.asarray(inputs["L0_attn_Wq"]); bq = np.asarray(inputs["L0_attn_bq"])
    Wk = np.asarray(inputs["L0_attn_Wk"]); bk = np.asarray(inputs["L0_attn_bk"])
    Wv = np.asarray(inputs["L0_attn_Wv"]); bv = np.asarray(inputs["L0_attn_bv"])
    Wb = np.asarray(inputs["L0_attn_Wbeta"]); bb = np.asarray(inputs["L0_attn_bbeta"])
    Wa = np.asarray(inputs["L0_attn_Walpha"]); ba = np.asarray(inputs["L0_attn_balpha"])
    w1 = np.asarray(inputs["L0_ffn_w1"]); b1 = np.asarray(inputs["L0_ffn_b1"])
    w2 = np.asarray(inputs["L0_ffn_w2"]); b2 = np.asarray(inputs["L0_ffn_b2"])

    ct = lambda a: np.ascontiguousarray(a, dtype=np.float32)
    shared = {
        "wqT": ct(Wq.T), "wkT": ct(Wk.T), "wvT": ct(Wv.T),
        "wbT": ct(Wb.T), "waT": ct(Wa.T),
        "w1T": ct(w1.T), "w2T": ct(w2.T),
        "bq": ct(bq.reshape(1, D)), "bk": ct(bk.reshape(1, D)),
        "bv": ct(bv.reshape(1, D)), "bb": ct(bb.reshape(1, D)),
        "ba": ct(ba.reshape(1, D)), "b1": ct(b1.reshape(1, HF)),
        "b2": ct(b2.reshape(1, D)),
    }
    in_maps = []
    for b in range(8):
        m = dict(shared)
        m["xT"] = ct(x[b].T)
        in_maps.append(m)

    if "nc" not in _NC_CACHE:
        _NC_CACHE["nc"] = build_nc()
    nc = _NC_CACHE["nc"]
    res = run_bass_kernel_spmd(nc, in_maps, core_ids=list(range(8)))
    out = np.stack([np.asarray(res.results[i]["outT"]).T for i in range(8)])
    return np.ascontiguousarray(out, dtype=np.float32)


# version 20
# speedup vs baseline: 1.0278x; 1.0127x over previous
import numpy as np
import ml_dtypes
from contextlib import ExitStack

import concourse.bass as bass
import concourse.mybir as mybir
from concourse.bass_utils import run_bass_kernel_spmd

F32 = mybir.dt.float32
BF16 = mybir.dt.bfloat16
FP8 = mybir.dt.float8e4
AF = mybir.ActivationFunctionType
ALU = mybir.AluOpType

D = 1024
T = 4096
NH = 16
DH = 64
C = 128
SC = 256
NSC = T // SC
HF = 4096
SLICE = 512
NSLICE = HF // SLICE
EPS = 1e-5
DBG = {}


class Prog:

    ENGINES = ("sync", "gpsimd", "scalar", "vector", "tensor")

    def __init__(self, nc):
        self.nc = nc
        self.ops = {e: [] for e in self.ENGINES}
        self.cnt = {}
        self.handles = {}

    def sem(self, es, name):
        self.handles[name] = es.enter_context(self.nc.semaphore(name))
        self.cnt[name] = 0

    def stage(self, engine, fn, inc=None, amt=1):
        if inc is not None:
            self.cnt[inc] += amt
            h = self.handles[inc]

            def wrapped(eng, fn=fn, h=h, amt=amt):
                fn(eng).then_inc(h, amt)
            self.ops[engine].append(wrapped)
        else:
            self.ops[engine].append(fn)

    COMP_SEM = {"vector": "s_dve", "scalar": "s_se", "gpsimd": "s_gp"}

    def comp(self, engine, fn, inc=None, amt=1):
        sem = self.COMP_SEM[engine]
        self.wait(engine, sem)
        self.stage(engine, fn, inc=sem)

    def dma(self, engine, out, in_, inc, **kw):
        self.stage(engine, lambda e, o=out, i=in_, kw=kw: e.dma_start(out=o, in_=i, **kw),
                   inc=inc, amt=16)

    def wait(self, engine, semname, value=None):
        v = self.cnt[semname] if value is None else value
        h = self.handles[semname]
        if v <= 0:
            return
        self.ops[engine].append(lambda eng, h=h, v=v: eng.wait_ge(h, v))

    def now(self, semname):
        return self.cnt[semname]

    def replay(self, block):
        p = self

        @block.sync
        def _(eng):
            for fn in p.ops["sync"]:
                fn(eng)

        @block.gpsimd
        def _(eng):
            for fn in p.ops["gpsimd"]:
                fn(eng)

        @block.scalar
        def _(eng):
            for fn in p.ops["scalar"]:
                fn(eng)

        @block.vector
        def _(eng):
            for fn in p.ops["vector"]:
                fn(eng)

        @block.tensor
        def _(eng):
            for fn in p.ops["tensor"]:
                fn(eng)


def build_nc():
    nc = bass.Bass("TRN2", target_bir_lowering=False)

    xT = nc.dram_tensor("xT", [D, T], F32, kind="ExternalInput").ap()
    wqT = nc.dram_tensor("wqT", [D, D], F32, kind="ExternalInput").ap()
    wkT = nc.dram_tensor("wkT", [D, D], F32, kind="ExternalInput").ap()
    wvT = nc.dram_tensor("wvT", [D, D], F32, kind="ExternalInput").ap()
    wbT = nc.dram_tensor("wbT", [D, D], F32, kind="ExternalInput").ap()
    waT = nc.dram_tensor("waT", [D, D], F32, kind="ExternalInput").ap()
    w1T = nc.dram_tensor("w1T", [D, HF], F32, kind="ExternalInput").ap()
    w2T = nc.dram_tensor("w2T", [HF, D], F32, kind="ExternalInput").ap()
    bq_d = nc.dram_tensor("bq", [1, D], F32, kind="ExternalInput").ap()
    bk_d = nc.dram_tensor("bk", [1, D], F32, kind="ExternalInput").ap()
    bv_d = nc.dram_tensor("bv", [1, D], F32, kind="ExternalInput").ap()
    bb_d = nc.dram_tensor("bb", [1, D], F32, kind="ExternalInput").ap()
    ba_d = nc.dram_tensor("ba", [1, D], F32, kind="ExternalInput").ap()
    b1_d = nc.dram_tensor("b1", [1, HF], F32, kind="ExternalInput").ap()
    b2_d = nc.dram_tensor("b2", [1, D], F32, kind="ExternalInput").ap()
    outT = nc.dram_tensor("outT", [D, T], F32, kind="ExternalOutput").ap()

    s1_scr = nc.dram_tensor("s1_scr", [D, HF], FP8, kind="Internal").ap()
    a1_scr = nc.dram_tensor("a1_scr", [1, HF], F32, kind="Internal").ap()
    s2_scr = nc.dram_tensor("s2_scr", [HF, D], FP8, kind="Internal").ap()
    b1_scr = nc.dram_tensor("b1_scr", [1, HF], BF16, kind="Internal").ap()
    pcur_scr = nc.dram_tensor("pcur_scr", [64, 128], F32, kind="Internal").ap()

    ident_f_np = np.eye(128, dtype=np.float32)
    triu_np = (np.triu(np.ones((128, 128), np.float32)) / 64.0).astype(np.float32)
    maskneg_np = np.where(np.arange(128)[:, None] > np.arange(128)[None, :],
                          np.float32(-1e9), np.float32(0.0))
    apat_np = np.zeros((16, 128), np.float32)
    for h in range(16):
        apat_np[h, (h % 2) * 64:(h % 2) * 64 + 64] = 1.0
    bpat_np = np.zeros((16, 512), np.float32)
    for h in range(16):
        bpat_np[h, (h // 2) * 64:(h // 2) * 64 + 64] = 1.0

    c_ident_f = nc.inline_tensor(ident_f_np, "c_ident_f").ap()
    c_triu = nc.inline_tensor(triu_np, "c_triu").ap()
    c_mask = nc.inline_tensor(maskneg_np, "c_mask").ap()
    c_apat = nc.inline_tensor(apat_np, "c_apat").ap()
    c_bpat = nc.inline_tensor(bpat_np, "c_bpat").ap()
    c_ones_bf = nc.inline_tensor(np.ones((1, 256), ml_dtypes.bfloat16), "c_ones_bf").ap()
    c_ones_f = nc.inline_tensor(np.ones((1, 128), np.float32), "c_ones_f").ap()
    c_ones_col = nc.inline_tensor(np.ones((128, 1), ml_dtypes.bfloat16), "c_ones_col").ap()

    es = ExitStack()
    sbuf = lambda name, shape, dt: es.enter_context(nc.sbuf_tensor(name, shape, dt))
    psum = lambda name, shape, dt: es.enter_context(nc.psum_tensor(name, shape, dt))

    xreg = [sbuf("xreg0", [128, 8 * SC], F32), sbuf("xreg1", [128, 8 * SC], F32)]
    xln = sbuf("xln", [128, 8 * SC], BF16)

    wq_s = sbuf("wq_s", [128, 8 * D], BF16)
    wk_s = sbuf("wk_s", [128, 8 * D], BF16)
    wv_s = sbuf("wv_s", [128, 8 * D], BF16)
    wb_s = sbuf("wb_s", [128, 8 * D], BF16)
    wa_s = sbuf("wa_s", [128, 8 * D], BF16)

    w1s = [sbuf("w1s%d" % i, [128, 8 * 256], FP8) for i in range(4)]
    w2s = [sbuf("w2s%d" % i, [128, 2 * D], FP8) for i in range(4)]
    acol = sbuf("acol", [128, 32], F32)
    b1col = sbuf("b1col", [128, 32], F32)
    hbf = [sbuf("hbf0", [128, 512], BF16), sbuf("hbf1", [128, 512], BF16)]

    qf = sbuf("qf", [128, 8 * SC], BF16)
    kf = sbuf("kf", [128, 8 * SC], BF16)
    qt = [sbuf("qt0", [128, 8 * C], BF16), sbuf("qt1", [128, 8 * C], BF16)]
    ktm = [sbuf("ktm0", [128, D], BF16), sbuf("ktm1", [128, D], BF16)]
    vbf = sbuf("vbf", [128, D], BF16)
    vbb = sbuf("vbb", [128, D], BF16)
    kbb = sbuf("kbb", [128, D], BF16)
    scrA = sbuf("scrA", [128, D], F32)
    wbc = [sbuf("wbc0", [128, 2048], F32), sbuf("wbc1", [128, 2048], F32)]
    cbc = [sbuf("cbc0", [128, 1024], BF16), sbuf("cbc1", [128, 1024], BF16)]
    attn_d = sbuf("attn_d", [128, 2048], BF16)
    ysb = sbuf("ysb", [128, 1024], F32)
    state = sbuf("state", [128, 512], F32)
    state_bf = sbuf("state_bf", [128, 512], BF16)
    stg = [sbuf("stg0", [128, 32], F32), sbuf("stg1", [128, 32], F32)]
    pfm = [sbuf("pfm0", [32, 128], F32), sbuf("pfm1", [32, 128], F32)]
    ptm = sbuf("ptm", [128, 32], F32)
    lam = sbuf("lam", [128, 32], F32)
    bmr = sbuf("bmr", [128, 16], F32)
    asc = sbuf("asc", [16, 2], F32)
    asrep = sbuf("asrep", [16, 512], F32)
    lnrow = sbuf("lnrow", [1, 1280], F32)
    s2st = [sbuf("s2st0", [128, 1024], FP8), sbuf("s2st1", [128, 1024], FP8)]

    bq_r = sbuf("bq_r", [1, D], BF16)
    bk_r = sbuf("bk_r", [1, D], BF16)
    bv_r = sbuf("bv_r", [1, D], BF16)
    bb_r = sbuf("bb_r", [1, D], BF16)
    ba_r = sbuf("ba_r", [1, D], F32)
    b2a_r = sbuf("b2a_r", [1, D], BF16)
    ha2 = sbuf("ha2", [128, 8], F32)

    ident_f = sbuf("ident_f", [128, 128], F32)
    triu = sbuf("triu", [128, 128], F32)
    maskc = sbuf("maskc", [128, 128], F32)
    apat = sbuf("apat", [16, 128], F32)
    bpat = sbuf("bpat", [16, 512], F32)
    ones_bf = sbuf("ones_bf", [1, 256], BF16)
    ones_f = sbuf("ones_f", [1, 128], F32)
    ones_col = sbuf("ones_col", [128, 1], BF16)

    pA = psum("pA", [128, 1024], F32)
    pB = psum("pB", [128, 1024], F32)
    pC = psum("pC", [128, 1024], F32)
    pD = psum("pD", [128, 512], F32)
    pE = psum("pE", [128, 512], F32)

    p = Prog(nc)
    for s in ("s_pro", "s_prog", "s_bias", "s_x0", "s_x1", "s_w0", "s_w1", "s_w2", "s_w3",
              "s_pg0", "s_pg1", "s_po0", "s_po1", "s_scr", "s_scr1", "s_wb0", "s_wb1",
              "s_cbc", "s_cbc1", "s_kt", "s_kt1", "s_out", "s_b2",
              "s_a1",
              "s_pe", "s_dve", "s_se", "s_gp"):
        p.sem(es, s)

    MM = lambda e, **kw: e.matmul(skip_group_check=True, **kw)

    for dst, src in [(ident_f, c_ident_f), (triu, c_triu), (maskc, c_mask),
                     (apat, c_apat), (bpat, c_bpat), (ones_bf, c_ones_bf),
                     (ones_f, c_ones_f), (ones_col, c_ones_col)]:
        p.dma("sync", dst[:], src[:, :], "s_pro")
    p.dma("gpsimd", bq_r[:], bq_d[:, :], "s_bias")
    p.dma("gpsimd", bk_r[:], bk_d[:, :], "s_bias")
    p.dma("gpsimd", bv_r[:], bv_d[:, :], "s_bias")
    p.dma("gpsimd", bb_r[:], bb_d[:, :], "s_bias")
    p.dma("sync", ba_r[:], ba_d[:, :], "s_pro")
    p.wait("tensor", "s_pro")
    p.wait("scalar", "s_pro")
    p.wait("gpsimd", "s_pro")
    p.wait("vector", "s_pro")
    p.wait("vector", "s_bias")
    p.comp("vector", lambda e: e.tensor_scalar_mul(out=bq_r[:], in0=bq_r[:], scalar1=0.125))
    p.comp("vector", lambda e: e.tensor_scalar_mul(out=bk_r[:], in0=bk_r[:], scalar1=0.125),
            inc="s_dve")
    free_mark = {0: ("s_dve", 0), 1: ("s_dve", 0)}
    t = 0
    for wsb, wdr, scl in [(wq_s, wqT, 0.125), (wk_s, wkT, 0.125), (wv_s, wvT, None),
                          (wb_s, wbT, None), (wa_s, waT, None)]:
        for r in range(8):
            par = t % 2
            eng = "sync" if par == 0 else "scalar"
            p.wait(eng, *free_mark[par])
            p.dma(eng, xreg[par][:, 0:1024], wdr[r * 128:(r + 1) * 128, :],
                  "s_pg%d" % par)
            p.wait("vector", "s_pg%d" % par)
            if scl is None:
                p.comp("vector", lambda e, wsb=wsb, r=r, par=par: e.tensor_copy(
                    out=wsb[:, r * D:(r + 1) * D], in_=xreg[par][:, 0:1024]))
            else:
                p.comp("vector", lambda e, wsb=wsb, r=r, par=par, scl=scl: e.tensor_scalar_mul(
                    out=wsb[:, r * D:(r + 1) * D], in0=xreg[par][:, 0:1024], scalar1=scl))
            free_mark[par] = ("s_dve", p.now("s_dve"))
            t += 1

    for hg in range(4):
        hg_sl = slice(hg * 1024, (hg + 1) * 1024)
        mm_mark = {}
        se_mark = {}
        p.wait("scalar", "s_dve")
        for r in range(8):
            par = r % 2
            eng = "sync" if par == 0 else "scalar"
            p.wait(eng, *free_mark[par])
            p.dma(eng, xreg[par][:, 0:1024],
                  w1T[r * 128:(r + 1) * 128, hg_sl], "s_pg%d" % par)
            p.wait("scalar", "s_pg%d" % par)
            if r >= 2:
                p.wait("scalar", "s_pe", mm_mark[r - 2])
            p.comp("scalar", lambda e, par=par: e.activation(
                out=xln[:, par * 1024:(par + 1) * 1024], in_=xreg[par][:, 0:1024], func=AF.Abs),
                inc="s_se")
            se_mark[r] = p.now("s_se")
            free_mark[par] = ("s_se", se_mark[r])
            p.wait("tensor", "s_se")
            for half in range(2):
                p.stage("tensor", lambda e, par=par, half=half, r=r: MM(
                    e, out=pB[0:1, half * 512:(half + 1) * 512],
                    lhsT=ones_col[:], rhs=xln[:, par * 1024 + half * 512: par * 1024 + (half + 1) * 512],
                    start=(r == 0), stop=(r == 7 and half == 1)),
                    inc="s_pe" if half == 1 else None)
            mm_mark[r] = p.now("s_pe")
        p.wait("vector", "s_pe")
        p.wait("vector", "s_a1")
        p.comp("vector", lambda e: e.tensor_scalar_mul(
            out=lnrow[0:1, 0:1024], in0=pB[0:1, 0:1024], scalar1=1.0 / D), inc="s_dve")
        p.wait("sync", "s_dve")
        p.dma("sync", a1_scr[0:1, hg_sl], lnrow[0:1, 0:1024], "s_a1")
        for r in range(8):
            par = r % 2
            eng = "sync" if par == 0 else "scalar"
            p.wait(eng, *free_mark[par])
            p.dma(eng, xreg[par][:, 0:1024],
                  w1T[r * 128:(r + 1) * 128, hg_sl], "s_pg%d" % par)
            p.wait("scalar", "s_pg%d" % par)
            p.wait("scalar", "s_po%d" % par)
            p.comp("scalar", lambda e, par=par: e.activation(
                out=s2st[par][:], in_=xreg[par][:, 0:1024], func=AF.Sign),
                inc="s_se")
            free_mark[par] = ("s_se", p.now("s_se"))
            p.wait("gpsimd", "s_se")
            p.dma("gpsimd", s1_scr[r * 128:(r + 1) * 128, hg_sl],
                  s2st[par][:], "s_po%d" % par)

    p.wait("sync", "s_a1")
    p.dma("sync", acol[:, :], a1_scr[0:1, :].rearrange("q (c p) -> (q p) c", p=128),
          "s_a1", allow_slow_non_contiguous=True)
    p.dma("sync", b1col[:, :], b1_d[0:1, :].rearrange("q (c p) -> (q p) c", p=128),
          "s_a1", allow_slow_non_contiguous=True)

    se_mark = {}
    mm_mark = {}
    p.wait("scalar", "s_dve")
    for kc in range(32):
        par = kc % 2
        eng = "sync" if par == 0 else "scalar"
        p.wait(eng, *free_mark[par])
        p.dma(eng, xreg[par][:, 0:1024],
              w2T[kc * 128:(kc + 1) * 128, :], "s_pg%d" % par)
        p.wait("scalar", "s_pg%d" % par)
        p.wait("scalar", "s_po%d" % par)
        if kc >= 2:
            p.wait("scalar", "s_pe", mm_mark[kc - 2])
        p.comp("scalar", lambda e, par=par: e.activation(
            out=xln[:, par * 1024:(par + 1) * 1024], in_=xreg[par][:, 0:1024], func=AF.Abs))
        p.comp("scalar", lambda e, par=par: e.activation(
            out=s2st[par][:], in_=xreg[par][:, 0:1024], func=AF.Sign),
            inc="s_se")
        free_mark[par] = ("s_se", p.now("s_se"))
        se_mark[kc] = p.now("s_se")
        p.wait("tensor", "s_se")
        for half in range(2):
            p.stage("tensor", lambda e, par=par, half=half, kc=kc: MM(
                e, out=pB[0:1, half * 512:(half + 1) * 512],
                lhsT=ones_col[:], rhs=xln[:, par * 1024 + half * 512: par * 1024 + (half + 1) * 512],
                start=(kc == 0), stop=(kc == 31 and half == 1)),
                inc="s_pe" if half == 1 else None)
        mm_mark[kc] = p.now("s_pe")
        p.wait("gpsimd", "s_se")
        p.dma("gpsimd", s2_scr[kc * 128:(kc + 1) * 128, :], s2st[par][:], "s_po%d" % par)
    p.wait("vector", "s_pe")
    p.wait("vector", "s_a1")
    p.comp("vector", lambda e: e.tensor_scalar_mul(
        out=lnrow[0:1, 0:1024], in0=pB[0:1, 0:1024], scalar1=1.0 / HF), inc="s_dve")
    p.wait("tensor", "s_dve")
    for dc in range(8):
        p.stage("tensor", lambda e, dc=dc: MM(
            e, out=pA[:, dc:dc + 1], lhsT=lnrow[0:1, dc * 128:(dc + 1) * 128],
            rhs=ones_f[0:1, 0:1], start=(dc == 0), stop=(dc == 7)),
            inc="s_pe" if dc == 7 else None)
    p.wait("vector", "s_pe")
    p.comp("vector", lambda e: e.tensor_scalar_mul(out=ha2[:], in0=pA[:, 0:8], scalar1=0.5))
    p.dma("sync", scrA[0:1, 0:1024], b2_d[:, :], "s_b2")
    p.wait("vector", "s_b2")
    p.comp("vector", lambda e: e.reciprocal(out=ysb[0:1, 0:1024], in_=lnrow[0:1, 0:1024]),
            inc="s_dve")
    p.wait("vector", "s_dve")
    p.comp("vector", lambda e: e.tensor_tensor(
        out=b2a_r[:], in0=scrA[0:1, 0:1024], in1=ysb[0:1, 0:1024], op=ALU.mult))
    p.comp("vector", lambda e: e.memset(state[:], 0.0))
    p.comp("vector", lambda e: e.memset(state_bf[:], 0.0), inc="s_dve")

    p.wait("sync", "s_se")
    p.wait("sync", "s_dve")
    p.dma("sync", xreg[0][:].rearrange("p (r t) -> p r t", r=8),
          xT[:, 0:SC].rearrange("(r p) t -> p r t", p=128), "s_x0")
    p.wait("gpsimd", "s_po0")
    p.wait("gpsimd", "s_po1")

    def emit_ln(par, wmarks=None):
        xr = xreg[par]
        xbf, xsq = qf, attn_d
        p.wait("vector", "s_pe")
        p.wait("scalar", "s_pe")
        xbm = {}
        xsm = {}
        for r in range(8):
            if wmarks is not None:
                p.wait("vector", "s_dve", wmarks[r])
                p.wait("scalar", "s_dve", wmarks[r])
            elif r == 0:
                p.wait("vector", "s_dve")
                p.wait("scalar", "s_dve")
            p.comp("vector", lambda e, r=r: e.tensor_copy(
                out=xbf[:, r * SC:(r + 1) * SC], in_=xr[:, r * SC:(r + 1) * SC]),
                inc="s_dve")
            xbm[r] = p.now("s_dve")
            p.comp("scalar", lambda e, r=r: e.activation(
                out=xsq[:, r * SC:(r + 1) * SC], in_=xr[:, r * SC:(r + 1) * SC],
                func=AF.Square), inc="s_se")
            xsm[r] = p.now("s_se")
        for r in range(8):
            p.wait("tensor", "s_dve", xbm[r])
            p.wait("tensor", "s_se", xsm[r])
            p.stage("tensor", lambda e, r=r: MM(
                e, out=pD[0:1, 0:SC], lhsT=ones_col[:], rhs=xbf[:, r * SC:(r + 1) * SC],
                start=(r == 0), stop=(r == 7)))
            p.stage("tensor", lambda e, r=r: MM(
                e, out=pD[0:1, SC:2 * SC], lhsT=ones_col[:], rhs=xsq[:, r * SC:(r + 1) * SC],
                start=False, stop=(r == 7)), inc="s_pe" if r == 7 else None)
        p.wait("vector", "s_pe")
        p.comp("vector", lambda e: e.tensor_scalar_mul(
            out=lnrow[0:1, 0:256], in0=pD[0:1, 0:SC], scalar1=1.0 / D))
        p.comp("vector", lambda e: e.tensor_tensor(
            out=lnrow[0:1, 256:512], in0=lnrow[0:1, 0:256], in1=lnrow[0:1, 0:256],
            op=ALU.mult))
        p.comp("vector", lambda e: e.scalar_tensor_tensor(
            out=lnrow[0:1, 512:768], in0=pD[0:1, SC:2 * SC], scalar=1.0 / D,
            in1=lnrow[0:1, 256:512], op0=ALU.mult, op1=ALU.subtract))
        p.comp("vector", lambda e: e.tensor_scalar_add(
            out=lnrow[0:1, 512:768], in0=lnrow[0:1, 512:768], scalar1=EPS), inc="s_dve")
        p.wait("scalar", "s_dve")
        p.comp("scalar", lambda e: e.activation(
            out=lnrow[0:1, 768:1024], in_=lnrow[0:1, 512:768], func=AF.Ln))
        p.comp("scalar", lambda e: e.activation(
            out=lnrow[0:1, 768:1024], in_=lnrow[0:1, 768:1024], func=AF.Exp, scale=-0.5),
            inc="s_se")
        p.wait("vector", "s_se")
        p.comp("vector", lambda e: e.scalar_tensor_tensor(
            out=lnrow[0:1, 1024:1280], in0=lnrow[0:1, 0:256], scalar=-1.0,
            in1=lnrow[0:1, 768:1024], op0=ALU.mult, op1=ALU.mult), inc="s_dve")
        p.wait("tensor", "s_dve")
        p.stage("tensor", lambda e: MM(
            e, out=pE[:, 0:SC], lhsT=ones_f[0:1, 0:128], rhs=lnrow[0:1, 768:1024],
            start=True, stop=True))
        p.stage("tensor", lambda e: MM(
            e, out=pE[:, SC:2 * SC], lhsT=ones_f[0:1, 0:128], rhs=lnrow[0:1, 1024:1280],
            start=False, stop=True), inc="s_pe")
        p.wait("vector", "s_pe")
        for r in range(8):
            p.comp("vector", lambda e, r=r: e.tensor_tensor(
                out=scrA[:, 0:SC], in0=xr[:, r * SC:(r + 1) * SC], in1=pE[:, 0:SC],
                op=ALU.mult))
            p.comp("vector", lambda e, r=r: e.tensor_tensor(
                out=xln[:, r * SC:(r + 1) * SC], in0=scrA[:, 0:SC], in1=pE[:, SC:2 * SC],
                op=ALU.add), inc="s_dve" if r == 7 else None)

    ffn_last_use = [0, 0, 0, 0]
    l2_done_for_h = {}
    NS2 = 16

    def emit_ffn(par):
        p.wait("tensor", "s_dve")
        gelu_done = {}
        l1_done = {}
        wmark = {}

        def load_slice(s):
            sp = s % 4
            p.wait("gpsimd", "s_pe", ffn_last_use[sp])
            ws = "s_w%d" % sp
            p.dma("gpsimd", w1s[sp][:].rearrange("p (r h) -> p r h", r=8),
                  s1_scr[:, s * 256:(s + 1) * 256].rearrange("(r p) h -> p r h", p=128),
                  ws)
            p.dma("gpsimd", w2s[sp][:].rearrange("p (j d) -> p j d", j=2),
                  s2_scr[s * 256:(s + 1) * 256, :].rearrange("(j p) d -> p j d", p=128),
                  ws)
            wmark[s] = p.now(ws)

        for s in range(4):
            load_slice(s)
        for s in range(NS2 + 1):
            if s < NS2:
                sp = s % 4
                l1t, bank = ((pA, 0), (pA, 512), (pD, 0), (pE, 0))[s % 4]
                p.wait("tensor", "s_w%d" % sp, wmark[s])
                if s >= 4:
                    p.wait("tensor", "s_se", gelu_done[s - 4])
                for hc in range(2):
                    for r in range(8):
                        p.stage("tensor", lambda e, sp=sp, hc=hc, r=r, bank=bank, l1t=l1t: MM(
                            e, out=l1t[:, bank + hc * 256: bank + (hc + 1) * 256],
                            lhsT=w1s[sp][:, r * 256 + hc * 128: r * 256 + (hc + 1) * 128],
                            rhs=xln[:, r * SC:(r + 1) * SC],
                            start=(hc == 0 and r == 0), stop=(r == 7)),
                            inc="s_pe" if (hc == 1 and r == 7) else None)
                l1_done[s] = p.now("s_pe")
                p.wait("scalar", "s_pe", l1_done[s])
                p.wait("scalar", "s_a1")
                if s >= 2:
                    p.wait("scalar", "s_pe", l2_done_for_h[s - 2])
                for hc in range(2):
                    p.comp("scalar", lambda e, s=s, hc=hc, bank=bank, l1t=l1t: e.activation(
                        out=hbf[s % 2][:, hc * 256:(hc + 1) * 256],
                        in_=l1t[:, bank + hc * 256: bank + (hc + 1) * 256],
                        func=AF.Gelu, scale=acol[:, s * 2 + hc: s * 2 + hc + 1],
                        bias=b1col[:, s * 2 + hc: s * 2 + hc + 1]))
                gelu_done[s] = p.now("s_se")
            if s >= 1:
                spp = (s - 1) % 4
                p.wait("tensor", "s_se", gelu_done[s - 1])
                for j in range(2):
                    for dc in range(8):
                        tgt = pB if dc < 4 else pC
                        p.stage("tensor", lambda e, spp=spp, j=j, dc=dc, s=s, tgt=tgt: MM(
                            e, out=tgt[:, (dc % 4) * 256:(dc % 4) * 256 + 256],
                            lhsT=w2s[spp][:, j * D + dc * 128: j * D + (dc + 1) * 128],
                            rhs=hbf[(s - 1) % 2][:, j * 256:(j + 1) * 256],
                            start=(s == 1 and j == 0 and dc % 2 == 0), stop=False),
                            inc="s_pe" if (j == 1 and dc == 7) else None)
                ffn_last_use[spp] = p.now("s_pe")
                l2_done_for_h[s - 1] = p.now("s_pe")
                if s + 3 < NS2:
                    load_slice(s + 3)
        for dc in range(8):
            tgt = pB if dc < 4 else pC
            p.stage("tensor", lambda e, dc=dc, tgt=tgt: MM(
                e, out=tgt[:, (dc % 4) * 256:(dc % 4) * 256 + 256],
                lhsT=b2a_r[0:1, dc * 128:(dc + 1) * 128],
                rhs=ones_bf[0:1, 0:SC], start=False, stop=True),
                inc="s_pe" if dc == 7 else None)
        p.wait("vector", "s_pe")
        emarks = []
        for dc in range(8):
            tgt = pB if dc < 4 else pC
            p.comp("vector", lambda e, dc=dc, tgt=tgt, par=par: e.scalar_tensor_tensor(
                out=xreg[par][:, dc * SC:(dc + 1) * SC],
                in0=tgt[:, (dc % 4) * 256:(dc % 4) * 256 + 256],
                scalar=ha2[:, dc:dc + 1],
                in1=xreg[par][:, dc * SC:(dc + 1) * SC],
                op0=ALU.mult, op1=ALU.add), inc="s_dve" if dc == 7 else None)
            emarks.append(p.now("s_dve"))
        return emarks

    dv_attnd = {0: 0, 1: 0}
    dv_vbb = [0]
    dv_xadd = [0]
    for sc in range(NSC):
        par = sc % 2
        if sc + 1 < NSC:
            p.wait("sync", "s_out", 16 * sc)
            p.dma("sync", xreg[(sc + 1) % 2][:].rearrange("p (r t) -> p r t", r=8),
                  xT[:, (sc + 1) * SC:(sc + 2) * SC].rearrange("(r p) t -> p r t", p=128),
                  "s_x%d" % ((sc + 1) % 2))
        xs = "s_x%d" % (sc % 2)
        xv = 16 * (sc // 2 + 1)
        p.wait("vector", xs, xv)
        p.wait("scalar", xs, xv)
        p.wait("tensor", xs, xv)

        PH = DBG.get("phase", 99)
        xam = {}
        if PH >= 1:
            emit_ln(par)
        f1m = None
        if PH >= 2:
            f1m = emit_ffn(par)
        if PH < 3:
            continue

        emit_ln(par, wmarks=f1m)
        p.wait("tensor", "s_dve")
        for proj in range(2):
            wsb = wq_s if proj == 0 else wk_s
            brow = bq_r if proj == 0 else bk_r
            for oc in range(8):
                if proj == 0:
                    tgt, off = (pA, oc * 256) if oc < 4 else (pB, (oc - 4) * 256)
                else:
                    if oc < 4:
                        tgt, off = pC, oc * 256
                    elif oc < 6:
                        tgt, off = pD, (oc - 4) * 256
                    else:
                        tgt, off = pE, (oc - 6) * 256
                for r in range(8):
                    p.stage("tensor", lambda e, wsb=wsb, oc=oc, r=r, tgt=tgt, off=off: MM(
                        e, out=tgt[:, off:off + 256],
                        lhsT=wsb[:, r * D + oc * 128: r * D + (oc + 1) * 128],
                        rhs=xln[:, r * SC:(r + 1) * SC],
                        start=(r == 0 and off % 512 == 0), stop=False))
                p.stage("tensor", lambda e, brow=brow, oc=oc, tgt=tgt, off=off: MM(
                    e, out=tgt[:, off:off + 256],
                    lhsT=brow[0:1, oc * 128:(oc + 1) * 128],
                    rhs=ones_bf[0:1, 0:SC], start=False, stop=True),
                    inc="s_pe" if oc == 7 else None)
        p.wait("scalar", "s_pe")
        p.wait("vector", "s_pe")
        p.comp("scalar", lambda e: e.activation(out=qf[:, 0:1024], in_=pA[:],
                                                 func=AF.Identity))
        p.comp("scalar", lambda e: e.activation(out=qf[:, 1024:2048], in_=pB[:],
                                                 func=AF.Identity), inc="s_se")
        se_qk = p.now("s_se")
        p.comp("vector", lambda e: e.tensor_copy(out=kf[:, 0:1024], in_=pC[:]))
        p.comp("vector", lambda e: e.tensor_copy(out=kf[:, 1024:1536], in_=pD[:]))
        p.comp("vector", lambda e: e.tensor_copy(out=kf[:, 1536:2048], in_=pE[:]),
                inc="s_dve")

        if PH < 4:
            continue
        dv_acopy = {}
        dv_lam = {}
        pe_tri = {}
        dv_pfm = {}
        pe_al = {}
        p.wait("tensor", "s_dve")
        for c in range(2):
            if c == 1:
                p.wait("tensor", "s_dve", dv_acopy[0])
            for half in range(2):
                for r in range(8):
                    p.stage("tensor", lambda e, half=half, r=r, c=c: MM(
                        e, out=pC[:, half * 512:(half + 1) * 512],
                        lhsT=xln[:, r * SC + c * 128: r * SC + c * 128 + 128],
                        rhs=wa_s[:, r * D + half * 512: r * D + (half + 1) * 512],
                        start=(r == 0), stop=False))
                p.stage("tensor", lambda e, half=half: MM(
                    e, out=pC[:, half * 512:(half + 1) * 512],
                    lhsT=ones_f[0:1, 0:128],
                    rhs=ba_r[0:1, half * 512:(half + 1) * 512],
                    start=False, stop=True),
                    inc="s_pe" if half == 1 else None)
            pe_al[c] = p.now("s_pe")
            p.wait("vector", "s_pe", pe_al[c])
            p.comp("vector", lambda e: e.tensor_copy(out=ysb[:], in_=pC[:]),
                    inc="s_dve")
            dv_acopy[c] = p.now("s_dve")
            p.comp("vector", lambda e: e.tensor_scalar(
                out=ysb[:], in0=ysb[:], scalar1=10.0, scalar2=-10.0,
                op0=ALU.min, op1=ALU.max), inc="s_dve")
            p.wait("scalar", "s_dve")
            p.comp("scalar", lambda e: e.activation(out=ysb[:], in_=ysb[:],
                                                     func=AF.Exp, scale=-1.0))
            p.comp("scalar", lambda e: e.activation(out=ysb[:], in_=ysb[:],
                                                     func=AF.Ln, bias=1.0), inc="s_se")
            p.wait("vector", "s_se")
            p.comp("vector", lambda e, c=c: e.tensor_reduce(
                out=lam[:, c * 16:(c + 1) * 16, None],
                in_=ysb[:].rearrange("p (h d) -> p h d", h=16),
                axis=mybir.AxisListType.X, op=ALU.add), inc="s_dve")
            dv_lam[c] = p.now("s_dve")
        for c in range(2):
            p.wait("tensor", "s_dve", dv_lam[c])
            p.stage("tensor", lambda e, c=c: MM(
                e, out=pD[:, c * 16:(c + 1) * 16], lhsT=triu[:],
                rhs=lam[:, c * 16:(c + 1) * 16], start=True, stop=True),
                inc="s_pe")
            pe_tri[c] = p.now("s_pe")
            p.wait("vector", "s_pe", pe_tri[c])
            if c == 0:
                p.wait("vector", "s_gp")
            p.comp("vector", lambda e, c=c: e.tensor_copy(
                out=ptm[:, c * 16:(c + 1) * 16], in_=pD[:, c * 16:(c + 1) * 16]))
            p.comp("vector", lambda e, c=c: e.tensor_copy(
                out=stg[c][:, 0:16], in_=pD[:, c * 16:(c + 1) * 16]), inc="s_dve")
            p.wait("scalar", "s_pe", pe_tri[c])
            p.comp("scalar", lambda e, c=c: e.activation(
                out=stg[c][:, 16:32], in_=pD[:, c * 16:(c + 1) * 16],
                func=AF.Exp, scale=-1.0), inc="s_se")
        for c in range(2):
            p.wait("tensor", "s_se")
            p.wait("tensor", "s_dve")
            tcorner = pB if c == 0 else pC
            p.stage("tensor", lambda e, c=c, tcorner=tcorner: e.transpose(
                out=tcorner[0:32, 896:1024], in_=stg[c][:], identity=ident_f[:]),
                inc="s_pe")
            p.wait("vector", "s_pe")
            p.comp("vector", lambda e, c=c, tcorner=tcorner: e.tensor_copy(
                out=pfm[c][:], in_=tcorner[0:32, 896:1024]), inc="s_dve")
            dv_pfm[c] = p.now("s_dve")
            p.wait("scalar", "s_dve", dv_pfm[c])
            p.comp("scalar", lambda e, c=c: e.activation(
                out=asc[:, c:c + 1], in_=pfm[c][0:16, 127:128],
                func=AF.Exp, scale=-1.0), inc="s_se")
        scrm = {}
        cbcm = {}
        ktmm = {}
        for c in range(2):
            scr_s = "s_scr" if c == 0 else "s_scr1"
            cbc_s = "s_cbc" if c == 0 else "s_cbc1"
            p.wait("sync", "s_dve", dv_pfm[c])
            if c == 0:
                p.wait("sync", "s_cbc")
                p.wait("sync", "s_cbc1")
            p.dma("sync", pcur_scr[c * 32:(c + 1) * 32, :], pfm[c][:], scr_s)
            scrm[c] = p.now(scr_s)
            p.wait("gpsimd", scr_s, scrm[c])
            p.wait("gpsimd", "s_dve")
            for hh in range(2):
                p.dma("gpsimd",
                      cbc[c][64 * hh:64 * hh + 64, :].rearrange("q (b i) -> q b i", b=8),
                      pcur_scr[c * 32 + 16 + hh:c * 32 + 32:2, :]
                      .unsqueeze(0).broadcast_to((64, 8, 128)),
                      cbc_s)
            cbcm[c] = p.now(cbc_s)
        p.wait("scalar", "s_dve")
        for c in range(2):
            kt_s = "s_kt" if c == 0 else "s_kt1"
            for r in range(8):
                p.stage("scalar", lambda e, r=r, c=c: e.dma_start_transpose(
                    out=ktm[c][:, r * 128:(r + 1) * 128],
                    in_=kf[:, r * SC + c * 128: r * SC + c * 128 + 128]),
                    inc=kt_s, amt=16)
            ktmm[c] = p.now(kt_s)

        explm = {}
        gpms = {}
        for c in range(2):
            p.wait("sync", "s_scr" if c == 0 else "s_scr1", scrm[c])
            p.wait("sync", "s_dve", dv_attnd[c])
            wbsem = "s_wb0" if c == 0 else "s_wb1"
            p.dma("sync", wbc[c][:].rearrange("p (h i) -> p h i", h=16),
                  pcur_scr[c * 32:c * 32 + 16, :]
                  .unsqueeze(0).broadcast_to((128, 16, 128)), wbsem)
            wm = p.now(wbsem)
            p.wait("gpsimd", wbsem, wm)
            p.comp("gpsimd", lambda e, c=c: e.tensor_tensor(
                out=wbc[c][:].rearrange("p (h i) -> p h i", h=16),
                in0=ptm[:, c * 16:(c + 1) * 16, None].broadcast_to((128, 16, 128)),
                in1=wbc[c][:].rearrange("p (h i) -> p h i", h=16), op=ALU.subtract))
            p.comp("gpsimd", lambda e, c=c: e.tensor_tensor(
                out=wbc[c][:].rearrange("p (h i) -> p h i", h=16),
                in0=wbc[c][:].rearrange("p (h i) -> p h i", h=16),
                in1=maskc[:, None, :].broadcast_to((128, 16, 128)), op=ALU.add),
                inc="s_gp")
            gpms[c] = p.now("s_gp")

        for c in range(2):
            n = sc * 2 + c
            p.wait("tensor", "s_se", se_qk)
            p.wait("tensor", "s_dve", dv_pfm[0])
            p.wait("tensor", "s_dve", dv_xadd[0])
            for pi in range(2):
                wsb, brow, tgt = ((wv_s, bv_r, pA), (wb_s, bb_r, pB))[pi]
                for half in range(2):
                    for r in range(8):
                        p.stage("tensor",
                                lambda e, wsb=wsb, half=half, r=r, c=c, tgt=tgt: MM(
                                    e, out=tgt[:, half * 512:(half + 1) * 512],
                                    lhsT=xln[:, r * SC + c * 128: r * SC + c * 128 + 128],
                                    rhs=wsb[:, r * D + half * 512: r * D + (half + 1) * 512],
                                    start=(r == 0), stop=False))
                    p.stage("tensor", lambda e, brow=brow, half=half, tgt=tgt: MM(
                        e, out=tgt[:, half * 512:(half + 1) * 512],
                        lhsT=ones_bf[0:1, 0:128],
                        rhs=brow[0:1, half * 512:(half + 1) * 512],
                        start=False, stop=True),
                        inc="s_pe" if half == 1 else None)
            pe_vba = p.now("s_pe")
            p.wait("vector", "s_pe", pe_vba - 1)
            p.comp("vector", lambda e: e.tensor_copy(out=vbf[:], in_=pA[:]), inc="s_dve")
            dv_vbf = p.now("s_dve")
            p.wait("scalar", "s_pe", pe_vba)
            p.wait("scalar", "s_dve", dv_vbb[0])
            p.comp("scalar", lambda e: e.activation(out=scrA[:], in_=pB[:], func=AF.Exp))
            p.comp("scalar", lambda e: e.activation(out=scrA[:], in_=scrA[:],
                                                     func=AF.Ln, bias=1.0), inc="s_se")
            se_beta = p.now("s_se")
            if c == 0:
                p.wait("scalar", "s_gp", gpms[0])
                p.comp("scalar", lambda e: e.activation(
                    out=wbc[0][:], in_=wbc[0][:], func=AF.Exp), inc="s_se")
                explm[0] = p.now("s_se")
            p.wait("tensor", "s_dve", dv_vbf)
            p.wait("tensor", "s_se", se_beta)
            seen_banks = set()
            order = list(range(0, 16, 2)) + [-1] + list(range(1, 16, 2))
            for h in order:
                if h < 0:
                    p.stage("tensor", lambda e: MM(
                        e, out=pE[0:1, 511:512], lhsT=ones_col[:],
                        rhs=ones_col[:], start=True, stop=True))
                    continue
                g = h // 4
                tgt = pA if g < 2 else pB
                off = (g % 2) * 512
                j = h % 4
                pb = 64 * (h % 2)
                bank = (id(tgt), off // 512)
                st_flag = bank not in seen_banks
                seen_banks.add(bank)
                p.stage("tensor", lambda e, h=h, j=j, c=c, tgt=tgt, off=off, pb=pb,
                        st_flag=st_flag: MM(
                        e, out=tgt[:, off + j * 128: off + (j + 1) * 128],
                        lhsT=kf[pb:pb + 64,
                                (h // 2) * SC + c * 128: (h // 2) * SC + c * 128 + 128],
                        rhs=qf[pb:pb + 64,
                               (h // 2) * SC + c * 128: (h // 2) * SC + c * 128 + 128],
                        start=st_flag, stop=True),
                        inc="s_pe" if h == 15 else None)
            pe_attn = p.now("s_pe")
            if DBG.get("sub", 99) < 6:
                continue
            p.wait("vector", "s_se", se_beta)
            p.comp("vector", lambda e: e.tensor_tensor(
                out=vbb[:], in0=vbf[:], in1=scrA[:], op=ALU.mult))
            dv_vbb[0] = p.now("s_dve")
            p.comp("vector", lambda e: e.tensor_reduce(
                out=bmr[:, :, None], in_=scrA[:].rearrange("p (h d) -> p h d", h=16),
                axis=mybir.AxisListType.X, op=ALU.add))
            p.comp("vector", lambda e: e.tensor_scalar_mul(
                out=bmr[:], in0=bmr[:], scalar1=1.0 / 64.0), inc="s_dve")
            p.wait("vector", "s_kt" if c == 0 else "s_kt1", ktmm[c])
            p.comp("vector", lambda e, c=c: e.tensor_tensor(
                out=kbb[:].rearrange("p (h d) -> p h d", h=16),
                in0=ktm[c][:].rearrange("p (h d) -> p h d", h=16),
                in1=bmr[:, :, None].broadcast_to((128, 16, 64)), op=ALU.mult))
            p.comp("vector", lambda e, c=c: e.tensor_scalar(
                out=asrep[:], in0=bpat[:], scalar1=asc[:, c:c + 1], scalar2=None,
                op0=ALU.mult), inc="s_dve")
            p.wait("vector", "s_se", explm[c])
            for g in range(4):
                tgt = pA if g < 2 else pB
                off = (g % 2) * 512
                p.wait("vector", "s_pe", pe_attn)
                p.comp("vector", lambda e, g=g, tgt=tgt, off=off, c=c: e.tensor_tensor(
                    out=attn_d[:, g * 512:(g + 1) * 512], in0=tgt[:, off:off + 512],
                    in1=wbc[c][:, g * 512:(g + 1) * 512], op=ALU.mult),
                    inc="s_dve" if g == 3 else None)
            dv_attnd[c] = p.now("s_dve")
            if c == 0:
                p.wait("scalar", "s_gp", gpms[1])
                p.comp("scalar", lambda e: e.activation(
                    out=wbc[1][:], in_=wbc[1][:], func=AF.Exp), inc="s_se")
                explm[1] = p.now("s_se")
            p.wait("vector", "s_cbc" if c == 0 else "s_cbc1", cbcm[c])
            p.comp("vector", lambda e, c=c: e.tensor_tensor(
                out=qt[c][:].rearrange("p (r t) -> p r t", r=8),
                in0=qf[:].rearrange("p (r t) -> p r t", r=8)[:, :, c * 128:(c + 1) * 128],
                in1=cbc[c][:].rearrange("p (r t) -> p r t", r=8), op=ALU.mult),
                inc="s_dve")
            if DBG.get("sub", 99) < 8:
                continue
            p.wait("tensor", "s_dve")
            for h in range(16):
                p.stage("tensor", lambda e, h=h: MM(
                    e, out=pC[:, h * 64:(h + 1) * 64],
                    lhsT=attn_d[:, h * 128:(h + 1) * 128],
                    rhs=vbb[:, h * 64:(h + 1) * 64],
                    start=(h % 8 == 0), stop=False))
            if n > 0:
                for h in list(range(0, 16, 2)) + [-1] + list(range(1, 16, 2)):
                    if h < 0:
                        p.stage("tensor", lambda e: MM(
                            e, out=pE[0:1, 510:511], lhsT=ones_col[:],
                            rhs=ones_col[:], start=True, stop=True))
                        continue
                    pb = 64 * (h % 2)
                    p.stage("tensor", lambda e, h=h, pb=pb, c=c: MM(
                        e, out=pC[:, h * 64:(h + 1) * 64],
                        lhsT=qt[c][pb:pb + 64, (h // 2) * 128:(h // 2) * 128 + 128],
                        rhs=state_bf[pb:pb + 64, (h // 2) * 64:(h // 2) * 64 + 64],
                        start=False, stop=(h == 15)))
            if DBG.get("sub", 99) < 9:
                continue
            for h in range(16):
                tp = {} if h % 2 == 0 else {"tile_position": (0, 64)}
                pb = 64 * (h % 2)
                p.stage("tensor", lambda e, h=h, tp=tp, pb=pb: MM(
                    e, out=pD[pb:pb + 64, (h // 2) * 64:(h // 2) * 64 + 64],
                    lhsT=kbb[:, h * 64:(h + 1) * 64],
                    rhs=vbf[:, h * 64:(h + 1) * 64],
                    start=(h <= 1), stop=(h == 15), **tp),
                    inc="s_pe" if h == 15 else None)
            if n > 0:
                p.stage("tensor", lambda e: MM(
                    e, out=pE[:, 0:512], lhsT=apat[:], rhs=asrep[:],
                    start=True, stop=True), inc="s_pe")
            pe_scan = p.now("s_pe")
            if DBG.get("sub", 99) < 10:
                continue
            p.wait("vector", "s_pe", pe_scan)
            if n > 0:
                p.comp("vector", lambda e: e.tensor_tensor(
                    out=state[:], in0=state[:], in1=pE[:, 0:512], op=ALU.mult))
                p.comp("vector", lambda e: e.tensor_tensor(
                    out=state[:], in0=state[:], in1=pD[:], op=ALU.add))
            else:
                p.comp("vector", lambda e: e.tensor_copy(out=state[:], in_=pD[:]))
            p.comp("vector", lambda e: e.tensor_copy(out=state_bf[:], in_=state[:]),
                    inc="s_dve")
            p.comp("vector", lambda e: e.tensor_copy(out=ysb[:], in_=pC[:]), inc="s_dve")
            if DBG.get("sub", 99) < 11:
                continue
            p.wait("tensor", "s_dve")
            for r in range(8):
                tb = pB if r < 4 else pA
                toff = (r % 4) * 128
                p.stage("tensor", lambda e, r=r, tb=tb, toff=toff: e.transpose(
                    out=tb[:, toff:toff + 128], in_=ysb[:, r * 128:(r + 1) * 128],
                    identity=ident_f[:]), inc="s_pe" if r == 7 else None)
            p.wait("vector", "s_pe")
            for r in range(8):
                tb = pB if r < 4 else pA
                toff = (r % 4) * 128
                p.comp("vector", lambda e, r=r, c=c, tb=tb, toff=toff, par=par:
                        e.tensor_tensor(
                            out=xreg[par][:, r * SC + c * 128: r * SC + c * 128 + 128],
                            in0=xreg[par][:, r * SC + c * 128: r * SC + c * 128 + 128],
                            in1=tb[:, toff:toff + 128], op=ALU.add),
                        inc="s_dve" if r == 7 else None)
                if c == 1:
                    xam[r] = p.now("s_dve")
            dv_xadd[0] = p.now("s_dve")

        if PH < 5:
            continue
        emit_ln(par, wmarks=xam if PH >= 4 else None)
        emit_ffn(par)
        p.wait("sync", "s_dve")
        p.dma("sync", outT[:, sc * SC:(sc + 1) * SC].rearrange("(r p) t -> p r t", p=128),
              xreg[par][:].rearrange("p (r t) -> p r t", r=8), "s_out")

    if p.now("s_out"):
        p.wait("sync", "s_out")

    with es:
        block = es.enter_context(nc.Block())
        p.replay(block)
    return nc


_NC_CACHE = {}


def kernel(**inputs):
    x = np.asarray(inputs["L0_attn_x"])
    Wq = np.asarray(inputs["L0_attn_Wq"]); bq = np.asarray(inputs["L0_attn_bq"])
    Wk = np.asarray(inputs["L0_attn_Wk"]); bk = np.asarray(inputs["L0_attn_bk"])
    Wv = np.asarray(inputs["L0_attn_Wv"]); bv = np.asarray(inputs["L0_attn_bv"])
    Wb = np.asarray(inputs["L0_attn_Wbeta"]); bb = np.asarray(inputs["L0_attn_bbeta"])
    Wa = np.asarray(inputs["L0_attn_Walpha"]); ba = np.asarray(inputs["L0_attn_balpha"])
    w1 = np.asarray(inputs["L0_ffn_w1"]); b1 = np.asarray(inputs["L0_ffn_b1"])
    w2 = np.asarray(inputs["L0_ffn_w2"]); b2 = np.asarray(inputs["L0_ffn_b2"])

    ct = lambda a: np.ascontiguousarray(a, dtype=np.float32)
    shared = {
        "wqT": ct(Wq.T), "wkT": ct(Wk.T), "wvT": ct(Wv.T),
        "wbT": ct(Wb.T), "waT": ct(Wa.T),
        "w1T": ct(w1.T), "w2T": ct(w2.T),
        "bq": ct(bq.reshape(1, D)), "bk": ct(bk.reshape(1, D)),
        "bv": ct(bv.reshape(1, D)), "bb": ct(bb.reshape(1, D)),
        "ba": ct(ba.reshape(1, D)), "b1": ct(b1.reshape(1, HF)),
        "b2": ct(b2.reshape(1, D)),
    }
    in_maps = []
    for b in range(8):
        m = dict(shared)
        m["xT"] = ct(x[b].T)
        in_maps.append(m)

    if "nc" not in _NC_CACHE:
        _NC_CACHE["nc"] = build_nc()
    nc = _NC_CACHE["nc"]
    res = run_bass_kernel_spmd(nc, in_maps, core_ids=list(range(8)))
    out = np.stack([np.asarray(res.results[i]["outT"]).T for i in range(8)])
    return np.ascontiguousarray(out, dtype=np.float32)


# version 21
# speedup vs baseline: 1.0368x; 1.0088x over previous
import numpy as np
import ml_dtypes
from contextlib import ExitStack

import concourse.bass as bass
import concourse.mybir as mybir
from concourse.bass_utils import run_bass_kernel_spmd

F32 = mybir.dt.float32
BF16 = mybir.dt.bfloat16
FP8 = mybir.dt.float8e4
AF = mybir.ActivationFunctionType
ALU = mybir.AluOpType

D = 1024
T = 4096
NH = 16
DH = 64
C = 128
SC = 256
NSC = T // SC
HF = 4096
SLICE = 512
NSLICE = HF // SLICE
EPS = 1e-5
DBG = {}


class Prog:

    ENGINES = ("sync", "gpsimd", "scalar", "vector", "tensor")

    def __init__(self, nc):
        self.nc = nc
        self.ops = {e: [] for e in self.ENGINES}
        self.cnt = {}
        self.handles = {}

    def sem(self, es, name):
        self.handles[name] = es.enter_context(self.nc.semaphore(name))
        self.cnt[name] = 0

    def stage(self, engine, fn, inc=None, amt=1):
        if inc is not None:
            self.cnt[inc] += amt
            h = self.handles[inc]

            def wrapped(eng, fn=fn, h=h, amt=amt):
                fn(eng).then_inc(h, amt)
            self.ops[engine].append(wrapped)
        else:
            self.ops[engine].append(fn)

    COMP_SEM = {"vector": "s_dve", "scalar": "s_se", "gpsimd": "s_gp"}

    def comp(self, engine, fn, inc=None, amt=1):
        sem = self.COMP_SEM[engine]
        self.wait(engine, sem)
        self.stage(engine, fn, inc=sem)

    def dma(self, engine, out, in_, inc, **kw):
        self.stage(engine, lambda e, o=out, i=in_, kw=kw: e.dma_start(out=o, in_=i, **kw),
                   inc=inc, amt=16)

    def wait(self, engine, semname, value=None):
        v = self.cnt[semname] if value is None else value
        h = self.handles[semname]
        if v <= 0:
            return
        self.ops[engine].append(lambda eng, h=h, v=v: eng.wait_ge(h, v))

    def now(self, semname):
        return self.cnt[semname]

    def replay(self, block):
        p = self

        @block.sync
        def _(eng):
            for fn in p.ops["sync"]:
                fn(eng)

        @block.gpsimd
        def _(eng):
            for fn in p.ops["gpsimd"]:
                fn(eng)

        @block.scalar
        def _(eng):
            for fn in p.ops["scalar"]:
                fn(eng)

        @block.vector
        def _(eng):
            for fn in p.ops["vector"]:
                fn(eng)

        @block.tensor
        def _(eng):
            for fn in p.ops["tensor"]:
                fn(eng)


def build_nc():
    nc = bass.Bass("TRN2", target_bir_lowering=False)

    xT = nc.dram_tensor("xT", [D, T], F32, kind="ExternalInput").ap()
    wqT = nc.dram_tensor("wqT", [D, D], F32, kind="ExternalInput").ap()
    wkT = nc.dram_tensor("wkT", [D, D], F32, kind="ExternalInput").ap()
    wvT = nc.dram_tensor("wvT", [D, D], F32, kind="ExternalInput").ap()
    wbT = nc.dram_tensor("wbT", [D, D], F32, kind="ExternalInput").ap()
    waT = nc.dram_tensor("waT", [D, D], F32, kind="ExternalInput").ap()
    w1T = nc.dram_tensor("w1T", [D, HF], F32, kind="ExternalInput").ap()
    w2T = nc.dram_tensor("w2T", [HF, D], F32, kind="ExternalInput").ap()
    bq_d = nc.dram_tensor("bq", [1, D], F32, kind="ExternalInput").ap()
    bk_d = nc.dram_tensor("bk", [1, D], F32, kind="ExternalInput").ap()
    bv_d = nc.dram_tensor("bv", [1, D], F32, kind="ExternalInput").ap()
    bb_d = nc.dram_tensor("bb", [1, D], F32, kind="ExternalInput").ap()
    ba_d = nc.dram_tensor("ba", [1, D], F32, kind="ExternalInput").ap()
    b1_d = nc.dram_tensor("b1", [1, HF], F32, kind="ExternalInput").ap()
    b2_d = nc.dram_tensor("b2", [1, D], F32, kind="ExternalInput").ap()
    outT = nc.dram_tensor("outT", [D, T], F32, kind="ExternalOutput").ap()

    s1_scr = nc.dram_tensor("s1_scr", [D, HF], FP8, kind="Internal").ap()
    a1_scr = nc.dram_tensor("a1_scr", [1, HF], F32, kind="Internal").ap()
    s2_scr = nc.dram_tensor("s2_scr", [HF, D], FP8, kind="Internal").ap()
    b1_scr = nc.dram_tensor("b1_scr", [1, HF], BF16, kind="Internal").ap()
    pcur_scr = nc.dram_tensor("pcur_scr", [64, 128], F32, kind="Internal").ap()

    ident_f_np = np.eye(128, dtype=np.float32)
    triu_np = (np.triu(np.ones((128, 128), np.float32)) / 64.0).astype(np.float32)
    maskneg_np = np.where(np.arange(128)[:, None] > np.arange(128)[None, :],
                          np.float32(-1e9), np.float32(0.0))
    apat_np = np.zeros((16, 128), np.float32)
    for h in range(16):
        apat_np[h, (h % 2) * 64:(h % 2) * 64 + 64] = 1.0
    bpat_np = np.zeros((16, 512), np.float32)
    for h in range(16):
        bpat_np[h, (h // 2) * 64:(h // 2) * 64 + 64] = 1.0

    c_ident_f = nc.inline_tensor(ident_f_np, "c_ident_f").ap()
    c_triu = nc.inline_tensor(triu_np, "c_triu").ap()
    c_mask = nc.inline_tensor(maskneg_np, "c_mask").ap()
    c_apat = nc.inline_tensor(apat_np, "c_apat").ap()
    c_bpat = nc.inline_tensor(bpat_np, "c_bpat").ap()
    c_ones_bf = nc.inline_tensor(np.ones((1, 256), ml_dtypes.bfloat16), "c_ones_bf").ap()
    c_ones_f = nc.inline_tensor(np.ones((1, 128), np.float32), "c_ones_f").ap()
    c_ones_col = nc.inline_tensor(np.ones((128, 1), ml_dtypes.bfloat16), "c_ones_col").ap()

    es = ExitStack()
    sbuf = lambda name, shape, dt: es.enter_context(nc.sbuf_tensor(name, shape, dt))
    psum = lambda name, shape, dt: es.enter_context(nc.psum_tensor(name, shape, dt))

    xreg = [sbuf("xreg0", [128, 8 * SC], F32), sbuf("xreg1", [128, 8 * SC], F32)]
    xln = sbuf("xln", [128, 8 * SC], BF16)

    wq_s = sbuf("wq_s", [128, 8 * D], BF16)
    wk_s = sbuf("wk_s", [128, 8 * D], BF16)
    wv_s = sbuf("wv_s", [128, 8 * D], BF16)
    wb_s = sbuf("wb_s", [128, 8 * D], BF16)
    wa_s = sbuf("wa_s", [128, 8 * D], BF16)

    w1s = [sbuf("w1s%d" % i, [128, 8 * 256], FP8) for i in range(4)]
    w2s = [sbuf("w2s%d" % i, [128, 2 * D], FP8) for i in range(4)]
    acol = sbuf("acol", [128, 32], F32)
    b1col = sbuf("b1col", [128, 32], F32)
    hbf = [sbuf("hbf0", [128, 512], BF16), sbuf("hbf1", [128, 512], BF16)]

    qf = sbuf("qf", [128, 8 * SC], BF16)
    kf = sbuf("kf", [128, 8 * SC], BF16)
    qt = [sbuf("qt0", [128, 8 * C], BF16), sbuf("qt1", [128, 8 * C], BF16)]
    ktm = [sbuf("ktm0", [128, D], BF16), sbuf("ktm1", [128, D], BF16)]
    vbf = sbuf("vbf", [128, D], BF16)
    vbb = sbuf("vbb", [128, D], BF16)
    kbb = sbuf("kbb", [128, D], BF16)
    scrA = sbuf("scrA", [128, D], F32)
    wbc = [sbuf("wbc0", [128, 2048], F32), sbuf("wbc1", [128, 2048], F32)]
    cbc = [sbuf("cbc0", [128, 1024], BF16), sbuf("cbc1", [128, 1024], BF16)]
    attn_d = sbuf("attn_d", [128, 2048], BF16)
    ysb = sbuf("ysb", [128, 1024], F32)
    state = sbuf("state", [128, 512], F32)
    state_bf = sbuf("state_bf", [128, 512], BF16)
    stg = [sbuf("stg0", [128, 32], F32), sbuf("stg1", [128, 32], F32)]
    pfm = [sbuf("pfm0", [32, 128], F32), sbuf("pfm1", [32, 128], F32)]
    ptm = sbuf("ptm", [128, 32], F32)
    lam = sbuf("lam", [128, 32], F32)
    bmr = sbuf("bmr", [128, 16], F32)
    asc = sbuf("asc", [16, 2], F32)
    asrep = sbuf("asrep", [16, 512], F32)
    lnrow = sbuf("lnrow", [1, 1280], F32)
    s2st = [sbuf("s2st0", [128, 1024], FP8), sbuf("s2st1", [128, 1024], FP8)]

    bq_r = sbuf("bq_r", [1, D], BF16)
    bk_r = sbuf("bk_r", [1, D], BF16)
    bv_r = sbuf("bv_r", [1, D], BF16)
    bb_r = sbuf("bb_r", [1, D], BF16)
    ba_r = sbuf("ba_r", [1, D], F32)
    b2a_r = sbuf("b2a_r", [1, D], BF16)
    ha2 = sbuf("ha2", [128, 8], F32)

    ident_f = sbuf("ident_f", [128, 128], F32)
    triu = sbuf("triu", [128, 128], F32)
    maskc = sbuf("maskc", [128, 128], F32)
    apat = sbuf("apat", [16, 128], F32)
    bpat = sbuf("bpat", [16, 512], F32)
    ones_bf = sbuf("ones_bf", [1, 256], BF16)
    ones_f = sbuf("ones_f", [1, 128], F32)
    ones_col = sbuf("ones_col", [128, 1], BF16)

    pA = psum("pA", [128, 1024], F32)
    pB = psum("pB", [128, 1024], F32)
    pC = psum("pC", [128, 1024], F32)
    pD = psum("pD", [128, 512], F32)
    pE = psum("pE", [128, 512], F32)

    p = Prog(nc)
    for s in ("s_pro", "s_prog", "s_bias", "s_x0", "s_x1", "s_w0", "s_w1", "s_w2", "s_w3",
              "s_pg0", "s_pg1", "s_po0", "s_po1", "s_scr", "s_scr1", "s_wb0", "s_wb1",
              "s_cbc", "s_cbc1", "s_kt", "s_kt1", "s_out", "s_b2",
              "s_a1",
              "s_pe", "s_dve", "s_se", "s_gp"):
        p.sem(es, s)

    MM = lambda e, **kw: e.matmul(skip_group_check=True, **kw)

    for dst, src in [(ident_f, c_ident_f), (triu, c_triu), (maskc, c_mask),
                     (apat, c_apat), (bpat, c_bpat), (ones_bf, c_ones_bf),
                     (ones_f, c_ones_f), (ones_col, c_ones_col)]:
        p.dma("sync", dst[:], src[:, :], "s_pro")
    p.dma("gpsimd", bq_r[:], bq_d[:, :], "s_bias")
    p.dma("gpsimd", bk_r[:], bk_d[:, :], "s_bias")
    p.dma("gpsimd", bv_r[:], bv_d[:, :], "s_bias")
    p.dma("gpsimd", bb_r[:], bb_d[:, :], "s_bias")
    p.dma("sync", ba_r[:], ba_d[:, :], "s_pro")
    p.wait("tensor", "s_pro")
    p.wait("scalar", "s_pro")
    p.wait("gpsimd", "s_pro")
    p.wait("vector", "s_pro")
    p.wait("vector", "s_bias")
    p.comp("vector", lambda e: e.tensor_scalar_mul(out=bq_r[:], in0=bq_r[:], scalar1=0.125))
    p.comp("vector", lambda e: e.tensor_scalar_mul(out=bk_r[:], in0=bk_r[:], scalar1=0.125),
            inc="s_dve")
    free_mark = {0: ("s_dve", 0), 1: ("s_dve", 0)}
    t = 0
    for wsb, wdr, scl in [(wq_s, wqT, 0.125), (wk_s, wkT, 0.125), (wv_s, wvT, None),
                          (wb_s, wbT, None), (wa_s, waT, None)]:
        for r in range(8):
            par = t % 2
            eng = "sync" if par == 0 else "scalar"
            p.wait(eng, *free_mark[par])
            p.dma(eng, xreg[par][:, 0:1024], wdr[r * 128:(r + 1) * 128, :],
                  "s_pg%d" % par)
            p.wait("vector", "s_pg%d" % par)
            if scl is None:
                p.comp("vector", lambda e, wsb=wsb, r=r, par=par: e.tensor_copy(
                    out=wsb[:, r * D:(r + 1) * D], in_=xreg[par][:, 0:1024]))
            else:
                p.comp("vector", lambda e, wsb=wsb, r=r, par=par, scl=scl: e.tensor_scalar_mul(
                    out=wsb[:, r * D:(r + 1) * D], in0=xreg[par][:, 0:1024], scalar1=scl))
            free_mark[par] = ("s_dve", p.now("s_dve"))
            t += 1

    for hg in range(4):
        hg_sl = slice(hg * 1024, (hg + 1) * 1024)
        mm_mark = {}
        se_mark = {}
        p.wait("scalar", "s_dve")
        for r in range(8):
            par = r % 2
            eng = "sync" if par == 0 else "scalar"
            p.wait(eng, *free_mark[par])
            p.dma(eng, xreg[par][:, 0:1024],
                  w1T[r * 128:(r + 1) * 128, hg_sl], "s_pg%d" % par)
            p.wait("scalar", "s_pg%d" % par)
            if r >= 2:
                p.wait("scalar", "s_pe", mm_mark[r - 2])
            p.comp("scalar", lambda e, par=par: e.activation(
                out=xln[:, par * 1024:(par + 1) * 1024], in_=xreg[par][:, 0:1024], func=AF.Abs),
                inc="s_se")
            se_mark[r] = p.now("s_se")
            free_mark[par] = ("s_se", se_mark[r])
            p.wait("tensor", "s_se")
            for half in range(2):
                p.stage("tensor", lambda e, par=par, half=half, r=r: MM(
                    e, out=pB[0:1, half * 512:(half + 1) * 512],
                    lhsT=ones_col[:], rhs=xln[:, par * 1024 + half * 512: par * 1024 + (half + 1) * 512],
                    start=(r == 0), stop=(r == 7 and half == 1)),
                    inc="s_pe" if half == 1 else None)
            mm_mark[r] = p.now("s_pe")
        p.wait("vector", "s_pe")
        p.wait("vector", "s_a1")
        p.comp("vector", lambda e: e.tensor_scalar_mul(
            out=lnrow[0:1, 0:1024], in0=pB[0:1, 0:1024], scalar1=1.0 / D), inc="s_dve")
        p.wait("sync", "s_dve")
        p.dma("sync", a1_scr[0:1, hg_sl], lnrow[0:1, 0:1024], "s_a1")
        for r in range(8):
            par = r % 2
            eng = "sync" if par == 0 else "scalar"
            p.wait(eng, *free_mark[par])
            p.dma(eng, xreg[par][:, 0:1024],
                  w1T[r * 128:(r + 1) * 128, hg_sl], "s_pg%d" % par)
            p.wait("scalar", "s_pg%d" % par)
            p.wait("scalar", "s_po%d" % par)
            p.comp("scalar", lambda e, par=par: e.activation(
                out=s2st[par][:], in_=xreg[par][:, 0:1024], func=AF.Sign),
                inc="s_se")
            free_mark[par] = ("s_se", p.now("s_se"))
            p.wait("gpsimd", "s_se")
            p.dma("gpsimd", s1_scr[r * 128:(r + 1) * 128, hg_sl],
                  s2st[par][:], "s_po%d" % par)

    p.wait("sync", "s_a1")
    p.dma("sync", acol[:, :], a1_scr[0:1, :].rearrange("q (c p) -> (q p) c", p=128),
          "s_a1", allow_slow_non_contiguous=True)
    p.dma("sync", b1col[:, :], b1_d[0:1, :].rearrange("q (c p) -> (q p) c", p=128),
          "s_a1", allow_slow_non_contiguous=True)

    se_mark = {}
    mm_mark = {}
    p.wait("scalar", "s_dve")
    for kc in range(32):
        par = kc % 2
        eng = "sync" if par == 0 else "scalar"
        p.wait(eng, *free_mark[par])
        p.dma(eng, xreg[par][:, 0:1024],
              w2T[kc * 128:(kc + 1) * 128, :], "s_pg%d" % par)
        p.wait("scalar", "s_pg%d" % par)
        p.wait("scalar", "s_po%d" % par)
        if kc >= 2:
            p.wait("scalar", "s_pe", mm_mark[kc - 2])
        p.comp("scalar", lambda e, par=par: e.activation(
            out=xln[:, par * 1024:(par + 1) * 1024], in_=xreg[par][:, 0:1024], func=AF.Abs))
        p.comp("scalar", lambda e, par=par: e.activation(
            out=s2st[par][:], in_=xreg[par][:, 0:1024], func=AF.Sign),
            inc="s_se")
        free_mark[par] = ("s_se", p.now("s_se"))
        se_mark[kc] = p.now("s_se")
        p.wait("tensor", "s_se")
        for half in range(2):
            p.stage("tensor", lambda e, par=par, half=half, kc=kc: MM(
                e, out=pB[0:1, half * 512:(half + 1) * 512],
                lhsT=ones_col[:], rhs=xln[:, par * 1024 + half * 512: par * 1024 + (half + 1) * 512],
                start=(kc == 0), stop=(kc == 31 and half == 1)),
                inc="s_pe" if half == 1 else None)
        mm_mark[kc] = p.now("s_pe")
        p.wait("gpsimd", "s_se")
        p.dma("gpsimd", s2_scr[kc * 128:(kc + 1) * 128, :], s2st[par][:], "s_po%d" % par)
    p.wait("vector", "s_pe")
    p.wait("vector", "s_a1")
    p.comp("vector", lambda e: e.tensor_scalar_mul(
        out=lnrow[0:1, 0:1024], in0=pB[0:1, 0:1024], scalar1=1.0 / HF), inc="s_dve")
    p.wait("tensor", "s_dve")
    for dc in range(8):
        p.stage("tensor", lambda e, dc=dc: MM(
            e, out=pA[:, dc:dc + 1], lhsT=lnrow[0:1, dc * 128:(dc + 1) * 128],
            rhs=ones_f[0:1, 0:1], start=(dc == 0), stop=(dc == 7)),
            inc="s_pe" if dc == 7 else None)
    p.wait("vector", "s_pe")
    p.comp("vector", lambda e: e.tensor_scalar_mul(out=ha2[:], in0=pA[:, 0:8], scalar1=0.5))
    p.dma("sync", scrA[0:1, 0:1024], b2_d[:, :], "s_b2")
    p.wait("vector", "s_b2")
    p.comp("vector", lambda e: e.reciprocal(out=ysb[0:1, 0:1024], in_=lnrow[0:1, 0:1024]),
            inc="s_dve")
    p.wait("vector", "s_dve")
    p.comp("vector", lambda e: e.tensor_tensor(
        out=b2a_r[:], in0=scrA[0:1, 0:1024], in1=ysb[0:1, 0:1024], op=ALU.mult))
    p.comp("vector", lambda e: e.memset(state[:], 0.0))
    p.comp("vector", lambda e: e.memset(state_bf[:], 0.0), inc="s_dve")

    p.wait("sync", "s_se")
    p.wait("sync", "s_dve")
    p.dma("sync", xreg[0][:].rearrange("p (r t) -> p r t", r=8),
          xT[:, 0:SC].rearrange("(r p) t -> p r t", p=128), "s_x0")
    p.wait("gpsimd", "s_po0")
    p.wait("gpsimd", "s_po1")

    def emit_ln(par, wmarks=None):
        xr = xreg[par]
        xbf, xsq = qf, attn_d
        p.wait("vector", "s_pe")
        p.wait("scalar", "s_pe")
        xbm = {}
        xsm = {}
        for r in range(8):
            if wmarks is not None:
                p.wait("vector", "s_dve", wmarks[r])
                p.wait("scalar", "s_dve", wmarks[r])
            elif r == 0:
                p.wait("vector", "s_dve")
                p.wait("scalar", "s_dve")
            p.comp("vector", lambda e, r=r: e.tensor_copy(
                out=xbf[:, r * SC:(r + 1) * SC], in_=xr[:, r * SC:(r + 1) * SC]),
                inc="s_dve")
            xbm[r] = p.now("s_dve")
            p.comp("scalar", lambda e, r=r: e.activation(
                out=xsq[:, r * SC:(r + 1) * SC], in_=xr[:, r * SC:(r + 1) * SC],
                func=AF.Square), inc="s_se")
            xsm[r] = p.now("s_se")
        for r in range(8):
            p.wait("tensor", "s_dve", xbm[r])
            p.wait("tensor", "s_se", xsm[r])
            p.stage("tensor", lambda e, r=r: MM(
                e, out=pD[0:1, 0:SC], lhsT=ones_col[:], rhs=xbf[:, r * SC:(r + 1) * SC],
                start=(r == 0), stop=(r == 7)))
            p.stage("tensor", lambda e, r=r: MM(
                e, out=pD[0:1, SC:2 * SC], lhsT=ones_col[:], rhs=xsq[:, r * SC:(r + 1) * SC],
                start=False, stop=(r == 7)), inc="s_pe" if r == 7 else None)
        p.wait("vector", "s_pe")
        p.comp("vector", lambda e: e.tensor_scalar_mul(
            out=lnrow[0:1, 0:256], in0=pD[0:1, 0:SC], scalar1=1.0 / D))
        p.comp("vector", lambda e: e.tensor_tensor(
            out=lnrow[0:1, 256:512], in0=lnrow[0:1, 0:256], in1=lnrow[0:1, 0:256],
            op=ALU.mult))
        p.comp("vector", lambda e: e.scalar_tensor_tensor(
            out=lnrow[0:1, 512:768], in0=pD[0:1, SC:2 * SC], scalar=1.0 / D,
            in1=lnrow[0:1, 256:512], op0=ALU.mult, op1=ALU.subtract))
        p.comp("vector", lambda e: e.tensor_scalar_add(
            out=lnrow[0:1, 512:768], in0=lnrow[0:1, 512:768], scalar1=EPS), inc="s_dve")
        p.wait("scalar", "s_dve")
        p.comp("scalar", lambda e: e.activation(
            out=lnrow[0:1, 768:1024], in_=lnrow[0:1, 512:768], func=AF.Ln))
        p.comp("scalar", lambda e: e.activation(
            out=lnrow[0:1, 768:1024], in_=lnrow[0:1, 768:1024], func=AF.Exp, scale=-0.5),
            inc="s_se")
        p.wait("vector", "s_se")
        p.comp("vector", lambda e: e.scalar_tensor_tensor(
            out=lnrow[0:1, 1024:1280], in0=lnrow[0:1, 0:256], scalar=-1.0,
            in1=lnrow[0:1, 768:1024], op0=ALU.mult, op1=ALU.mult), inc="s_dve")
        p.wait("tensor", "s_dve")
        p.stage("tensor", lambda e: MM(
            e, out=pE[:, 0:SC], lhsT=ones_f[0:1, 0:128], rhs=lnrow[0:1, 768:1024],
            start=True, stop=True))
        p.stage("tensor", lambda e: MM(
            e, out=pE[:, SC:2 * SC], lhsT=ones_f[0:1, 0:128], rhs=lnrow[0:1, 1024:1280],
            start=False, stop=True), inc="s_pe")
        p.wait("vector", "s_pe")
        for r in range(8):
            p.comp("vector", lambda e, r=r: e.tensor_tensor(
                out=scrA[:, 0:SC], in0=xr[:, r * SC:(r + 1) * SC], in1=pE[:, 0:SC],
                op=ALU.mult))
            p.comp("vector", lambda e, r=r: e.tensor_tensor(
                out=xln[:, r * SC:(r + 1) * SC], in0=scrA[:, 0:SC], in1=pE[:, SC:2 * SC],
                op=ALU.add), inc="s_dve" if r == 7 else None)

    ffn_last_use = [0, 0, 0, 0]
    l2_done_for_h = {}
    NS2 = 16

    def emit_ffn(par):
        p.wait("tensor", "s_dve")
        gelu_done = {}
        l1_done = {}
        wmark = {}

        def load_slice(s):
            sp = s % 4
            p.wait("gpsimd", "s_pe", ffn_last_use[sp])
            ws = "s_w%d" % sp
            p.dma("gpsimd", w1s[sp][:].rearrange("p (r h) -> p r h", r=8),
                  s1_scr[:, s * 256:(s + 1) * 256].rearrange("(r p) h -> p r h", p=128),
                  ws)
            p.dma("gpsimd", w2s[sp][:].rearrange("p (j d) -> p j d", j=2),
                  s2_scr[s * 256:(s + 1) * 256, :].rearrange("(j p) d -> p j d", p=128),
                  ws)
            wmark[s] = p.now(ws)

        for s in range(3):
            load_slice(s)
        for s in range(NS2 + 1):
            if s < NS2:
                sp = s % 4
                l1t, bank = ((pA, 0), (pA, 512), (pD, 0), (pE, 0))[s % 4]
                p.wait("tensor", "s_w%d" % sp, wmark[s])
                if s >= 4:
                    p.wait("tensor", "s_se", gelu_done[s - 4])
                for hc in range(2):
                    for r in range(8):
                        p.stage("tensor", lambda e, sp=sp, hc=hc, r=r, bank=bank, l1t=l1t: MM(
                            e, out=l1t[:, bank + hc * 256: bank + (hc + 1) * 256],
                            lhsT=w1s[sp][:, r * 256 + hc * 128: r * 256 + (hc + 1) * 128],
                            rhs=xln[:, r * SC:(r + 1) * SC],
                            start=(hc == 0 and r == 0), stop=(r == 7)),
                            inc="s_pe" if (hc == 1 and r == 7) else None)
                l1_done[s] = p.now("s_pe")
                p.wait("scalar", "s_pe", l1_done[s])
                p.wait("scalar", "s_a1")
                if s >= 2:
                    p.wait("scalar", "s_pe", l2_done_for_h[s - 2])
                for hc in range(2):
                    p.comp("scalar", lambda e, s=s, hc=hc, bank=bank, l1t=l1t: e.activation(
                        out=hbf[s % 2][:, hc * 256:(hc + 1) * 256],
                        in_=l1t[:, bank + hc * 256: bank + (hc + 1) * 256],
                        func=AF.Gelu, scale=acol[:, s * 2 + hc: s * 2 + hc + 1],
                        bias=b1col[:, s * 2 + hc: s * 2 + hc + 1]))
                gelu_done[s] = p.now("s_se")
            if s >= 1:
                spp = (s - 1) % 4
                p.wait("tensor", "s_se", gelu_done[s - 1])
                for j in range(2):
                    for dc in range(8):
                        tgt = pB if dc < 4 else pC
                        p.stage("tensor", lambda e, spp=spp, j=j, dc=dc, s=s, tgt=tgt: MM(
                            e, out=tgt[:, (dc % 4) * 256:(dc % 4) * 256 + 256],
                            lhsT=w2s[spp][:, j * D + dc * 128: j * D + (dc + 1) * 128],
                            rhs=hbf[(s - 1) % 2][:, j * 256:(j + 1) * 256],
                            start=(s == 1 and j == 0 and dc % 2 == 0), stop=False),
                            inc="s_pe" if (j == 1 and dc == 7) else None)
                ffn_last_use[spp] = p.now("s_pe")
                l2_done_for_h[s - 1] = p.now("s_pe")
                if s + 2 < NS2:
                    load_slice(s + 2)
        for dc in range(8):
            tgt = pB if dc < 4 else pC
            p.stage("tensor", lambda e, dc=dc, tgt=tgt: MM(
                e, out=tgt[:, (dc % 4) * 256:(dc % 4) * 256 + 256],
                lhsT=b2a_r[0:1, dc * 128:(dc + 1) * 128],
                rhs=ones_bf[0:1, 0:SC], start=False, stop=True),
                inc="s_pe" if dc == 7 else None)
        p.wait("vector", "s_pe")
        emarks = []
        for dc in range(8):
            tgt = pB if dc < 4 else pC
            p.comp("vector", lambda e, dc=dc, tgt=tgt, par=par: e.scalar_tensor_tensor(
                out=xreg[par][:, dc * SC:(dc + 1) * SC],
                in0=tgt[:, (dc % 4) * 256:(dc % 4) * 256 + 256],
                scalar=ha2[:, dc:dc + 1],
                in1=xreg[par][:, dc * SC:(dc + 1) * SC],
                op0=ALU.mult, op1=ALU.add), inc="s_dve" if dc == 7 else None)
            emarks.append(p.now("s_dve"))
        return emarks

    dv_attnd = {0: 0, 1: 0}
    dv_vbb = [0]
    dv_xadd = [0]
    for sc in range(NSC):
        par = sc % 2
        if sc + 1 < NSC:
            p.wait("sync", "s_out", 16 * sc)
            p.dma("sync", xreg[(sc + 1) % 2][:].rearrange("p (r t) -> p r t", r=8),
                  xT[:, (sc + 1) * SC:(sc + 2) * SC].rearrange("(r p) t -> p r t", p=128),
                  "s_x%d" % ((sc + 1) % 2))
        xs = "s_x%d" % (sc % 2)
        xv = 16 * (sc // 2 + 1)
        p.wait("vector", xs, xv)
        p.wait("scalar", xs, xv)
        p.wait("tensor", xs, xv)

        PH = DBG.get("phase", 99)
        xam = {}
        if PH >= 1:
            emit_ln(par)
        f1m = None
        if PH >= 2:
            f1m = emit_ffn(par)
        if PH < 3:
            continue

        emit_ln(par, wmarks=f1m)
        p.wait("tensor", "s_dve")
        for proj in range(2):
            wsb = wq_s if proj == 0 else wk_s
            brow = bq_r if proj == 0 else bk_r
            for oc in range(8):
                if proj == 0:
                    tgt, off = (pA, oc * 256) if oc < 4 else (pB, (oc - 4) * 256)
                else:
                    if oc < 4:
                        tgt, off = pC, oc * 256
                    elif oc < 6:
                        tgt, off = pD, (oc - 4) * 256
                    else:
                        tgt, off = pE, (oc - 6) * 256
                for r in range(8):
                    p.stage("tensor", lambda e, wsb=wsb, oc=oc, r=r, tgt=tgt, off=off: MM(
                        e, out=tgt[:, off:off + 256],
                        lhsT=wsb[:, r * D + oc * 128: r * D + (oc + 1) * 128],
                        rhs=xln[:, r * SC:(r + 1) * SC],
                        start=(r == 0 and off % 512 == 0), stop=False))
                p.stage("tensor", lambda e, brow=brow, oc=oc, tgt=tgt, off=off: MM(
                    e, out=tgt[:, off:off + 256],
                    lhsT=brow[0:1, oc * 128:(oc + 1) * 128],
                    rhs=ones_bf[0:1, 0:SC], start=False, stop=True),
                    inc="s_pe" if oc == 7 else None)
        p.wait("scalar", "s_pe")
        p.wait("vector", "s_pe")
        p.comp("scalar", lambda e: e.activation(out=qf[:, 0:1024], in_=pA[:],
                                                 func=AF.Identity))
        p.comp("scalar", lambda e: e.activation(out=qf[:, 1024:2048], in_=pB[:],
                                                 func=AF.Identity), inc="s_se")
        se_qk = p.now("s_se")
        p.comp("vector", lambda e: e.tensor_copy(out=kf[:, 0:1024], in_=pC[:]))
        p.comp("vector", lambda e: e.tensor_copy(out=kf[:, 1024:1536], in_=pD[:]))
        p.comp("vector", lambda e: e.tensor_copy(out=kf[:, 1536:2048], in_=pE[:]),
                inc="s_dve")

        if PH < 4:
            continue
        dv_acopy = {}
        dv_lam = {}
        pe_tri = {}
        dv_pfm = {}
        pe_al = {}
        p.wait("tensor", "s_dve")
        for c in range(2):
            if c == 1:
                p.wait("tensor", "s_dve", dv_acopy[0])
            for half in range(2):
                for r in range(8):
                    p.stage("tensor", lambda e, half=half, r=r, c=c: MM(
                        e, out=pC[:, half * 512:(half + 1) * 512],
                        lhsT=xln[:, r * SC + c * 128: r * SC + c * 128 + 128],
                        rhs=wa_s[:, r * D + half * 512: r * D + (half + 1) * 512],
                        start=(r == 0), stop=False))
                p.stage("tensor", lambda e, half=half: MM(
                    e, out=pC[:, half * 512:(half + 1) * 512],
                    lhsT=ones_f[0:1, 0:128],
                    rhs=ba_r[0:1, half * 512:(half + 1) * 512],
                    start=False, stop=True),
                    inc="s_pe" if half == 1 else None)
            pe_al[c] = p.now("s_pe")
            p.wait("vector", "s_pe", pe_al[c])
            p.comp("vector", lambda e: e.tensor_copy(out=ysb[:], in_=pC[:]),
                    inc="s_dve")
            dv_acopy[c] = p.now("s_dve")
            p.comp("vector", lambda e: e.tensor_scalar(
                out=ysb[:], in0=ysb[:], scalar1=10.0, scalar2=-10.0,
                op0=ALU.min, op1=ALU.max), inc="s_dve")
            p.wait("scalar", "s_dve")
            p.comp("scalar", lambda e: e.activation(out=ysb[:], in_=ysb[:],
                                                     func=AF.Exp, scale=-1.0))
            p.comp("scalar", lambda e: e.activation(out=ysb[:], in_=ysb[:],
                                                     func=AF.Ln, bias=1.0), inc="s_se")
            p.wait("vector", "s_se")
            p.comp("vector", lambda e, c=c: e.tensor_reduce(
                out=lam[:, c * 16:(c + 1) * 16, None],
                in_=ysb[:].rearrange("p (h d) -> p h d", h=16),
                axis=mybir.AxisListType.X, op=ALU.add), inc="s_dve")
            dv_lam[c] = p.now("s_dve")
        for c in range(2):
            p.wait("tensor", "s_dve", dv_lam[c])
            p.stage("tensor", lambda e, c=c: MM(
                e, out=pD[:, c * 16:(c + 1) * 16], lhsT=triu[:],
                rhs=lam[:, c * 16:(c + 1) * 16], start=True, stop=True),
                inc="s_pe")
            pe_tri[c] = p.now("s_pe")
            p.wait("vector", "s_pe", pe_tri[c])
            if c == 0:
                p.wait("vector", "s_gp")
            p.comp("vector", lambda e, c=c: e.tensor_copy(
                out=ptm[:, c * 16:(c + 1) * 16], in_=pD[:, c * 16:(c + 1) * 16]))
            p.comp("vector", lambda e, c=c: e.tensor_copy(
                out=stg[c][:, 0:16], in_=pD[:, c * 16:(c + 1) * 16]), inc="s_dve")
            p.wait("scalar", "s_pe", pe_tri[c])
            p.comp("scalar", lambda e, c=c: e.activation(
                out=stg[c][:, 16:32], in_=pD[:, c * 16:(c + 1) * 16],
                func=AF.Exp, scale=-1.0), inc="s_se")
        for c in range(2):
            p.wait("tensor", "s_se")
            p.wait("tensor", "s_dve")
            tcorner = pB if c == 0 else pC
            p.stage("tensor", lambda e, c=c, tcorner=tcorner: e.transpose(
                out=tcorner[0:32, 896:1024], in_=stg[c][:], identity=ident_f[:]),
                inc="s_pe")
            p.wait("vector", "s_pe")
            p.comp("vector", lambda e, c=c, tcorner=tcorner: e.tensor_copy(
                out=pfm[c][:], in_=tcorner[0:32, 896:1024]), inc="s_dve")
            dv_pfm[c] = p.now("s_dve")
            p.wait("scalar", "s_dve", dv_pfm[c])
            p.comp("scalar", lambda e, c=c: e.activation(
                out=asc[:, c:c + 1], in_=pfm[c][0:16, 127:128],
                func=AF.Exp, scale=-1.0), inc="s_se")
        scrm = {}
        cbcm = {}
        ktmm = {}
        for c in range(2):
            scr_s = "s_scr" if c == 0 else "s_scr1"
            cbc_s = "s_cbc" if c == 0 else "s_cbc1"
            p.wait("sync", "s_dve", dv_pfm[c])
            if c == 0:
                p.wait("sync", "s_cbc")
                p.wait("sync", "s_cbc1")
            p.dma("sync", pcur_scr[c * 32:(c + 1) * 32, :], pfm[c][:], scr_s)
            scrm[c] = p.now(scr_s)
            p.wait("gpsimd", scr_s, scrm[c])
            p.wait("gpsimd", "s_dve")
            for hh in range(2):
                p.dma("gpsimd",
                      cbc[c][64 * hh:64 * hh + 64, :].rearrange("q (b i) -> q b i", b=8),
                      pcur_scr[c * 32 + 16 + hh:c * 32 + 32:2, :]
                      .unsqueeze(0).broadcast_to((64, 8, 128)),
                      cbc_s)
            cbcm[c] = p.now(cbc_s)
        p.wait("scalar", "s_dve")
        for c in range(2):
            kt_s = "s_kt" if c == 0 else "s_kt1"
            for r in range(8):
                p.stage("scalar", lambda e, r=r, c=c: e.dma_start_transpose(
                    out=ktm[c][:, r * 128:(r + 1) * 128],
                    in_=kf[:, r * SC + c * 128: r * SC + c * 128 + 128]),
                    inc=kt_s, amt=16)
            ktmm[c] = p.now(kt_s)

        explm = {}
        gpms = {}
        for c in range(2):
            p.wait("sync", "s_scr" if c == 0 else "s_scr1", scrm[c])
            p.wait("sync", "s_dve", dv_attnd[c])
            wbsem = "s_wb0" if c == 0 else "s_wb1"
            p.dma("sync", wbc[c][:].rearrange("p (h i) -> p h i", h=16),
                  pcur_scr[c * 32:c * 32 + 16, :]
                  .unsqueeze(0).broadcast_to((128, 16, 128)), wbsem)
            wm = p.now(wbsem)
            p.wait("gpsimd", wbsem, wm)
            p.comp("gpsimd", lambda e, c=c: e.tensor_tensor(
                out=wbc[c][:].rearrange("p (h i) -> p h i", h=16),
                in0=ptm[:, c * 16:(c + 1) * 16, None].broadcast_to((128, 16, 128)),
                in1=wbc[c][:].rearrange("p (h i) -> p h i", h=16), op=ALU.subtract))
            p.comp("gpsimd", lambda e, c=c: e.tensor_tensor(
                out=wbc[c][:].rearrange("p (h i) -> p h i", h=16),
                in0=wbc[c][:].rearrange("p (h i) -> p h i", h=16),
                in1=maskc[:, None, :].broadcast_to((128, 16, 128)), op=ALU.add),
                inc="s_gp")
            gpms[c] = p.now("s_gp")

        for c in range(2):
            n = sc * 2 + c
            p.wait("tensor", "s_se", se_qk)
            p.wait("tensor", "s_dve", dv_pfm[0])
            p.wait("tensor", "s_dve", dv_xadd[0])
            for pi in range(2):
                wsb, brow, tgt = ((wv_s, bv_r, pA), (wb_s, bb_r, pB))[pi]
                for half in range(2):
                    for r in range(8):
                        p.stage("tensor",
                                lambda e, wsb=wsb, half=half, r=r, c=c, tgt=tgt: MM(
                                    e, out=tgt[:, half * 512:(half + 1) * 512],
                                    lhsT=xln[:, r * SC + c * 128: r * SC + c * 128 + 128],
                                    rhs=wsb[:, r * D + half * 512: r * D + (half + 1) * 512],
                                    start=(r == 0), stop=False))
                    p.stage("tensor", lambda e, brow=brow, half=half, tgt=tgt: MM(
                        e, out=tgt[:, half * 512:(half + 1) * 512],
                        lhsT=ones_bf[0:1, 0:128],
                        rhs=brow[0:1, half * 512:(half + 1) * 512],
                        start=False, stop=True),
                        inc="s_pe" if half == 1 else None)
            pe_vba = p.now("s_pe")
            p.wait("vector", "s_pe", pe_vba - 1)
            p.comp("vector", lambda e: e.tensor_copy(out=vbf[:], in_=pA[:]), inc="s_dve")
            dv_vbf = p.now("s_dve")
            p.wait("scalar", "s_pe", pe_vba)
            p.wait("scalar", "s_dve", dv_vbb[0])
            p.comp("scalar", lambda e: e.activation(out=scrA[:], in_=pB[:], func=AF.Exp))
            p.comp("scalar", lambda e: e.activation(out=scrA[:], in_=scrA[:],
                                                     func=AF.Ln, bias=1.0), inc="s_se")
            se_beta = p.now("s_se")
            if c == 0:
                p.wait("scalar", "s_gp", gpms[0])
                p.comp("scalar", lambda e: e.activation(
                    out=wbc[0][:], in_=wbc[0][:], func=AF.Exp), inc="s_se")
                explm[0] = p.now("s_se")
            p.wait("tensor", "s_dve", dv_vbf)
            p.wait("tensor", "s_se", se_beta)
            seen_banks = set()
            order = list(range(0, 16, 2)) + [-1] + list(range(1, 16, 2))
            for h in order:
                if h < 0:
                    p.stage("tensor", lambda e: MM(
                        e, out=pE[0:1, 511:512], lhsT=ones_col[:],
                        rhs=ones_col[:], start=True, stop=True))
                    continue
                g = h // 4
                tgt = pA if g < 2 else pB
                off = (g % 2) * 512
                j = h % 4
                pb = 64 * (h % 2)
                bank = (id(tgt), off // 512)
                st_flag = bank not in seen_banks
                seen_banks.add(bank)
                p.stage("tensor", lambda e, h=h, j=j, c=c, tgt=tgt, off=off, pb=pb,
                        st_flag=st_flag: MM(
                        e, out=tgt[:, off + j * 128: off + (j + 1) * 128],
                        lhsT=kf[pb:pb + 64,
                                (h // 2) * SC + c * 128: (h // 2) * SC + c * 128 + 128],
                        rhs=qf[pb:pb + 64,
                               (h // 2) * SC + c * 128: (h // 2) * SC + c * 128 + 128],
                        start=st_flag, stop=True),
                        inc="s_pe" if h == 15 else None)
            pe_attn = p.now("s_pe")
            if DBG.get("sub", 99) < 6:
                continue
            p.wait("vector", "s_se", se_beta)
            p.comp("vector", lambda e: e.tensor_tensor(
                out=vbb[:], in0=vbf[:], in1=scrA[:], op=ALU.mult))
            dv_vbb[0] = p.now("s_dve")
            p.comp("vector", lambda e: e.tensor_reduce(
                out=bmr[:, :, None], in_=scrA[:].rearrange("p (h d) -> p h d", h=16),
                axis=mybir.AxisListType.X, op=ALU.add))
            p.comp("vector", lambda e: e.tensor_scalar_mul(
                out=bmr[:], in0=bmr[:], scalar1=1.0 / 64.0), inc="s_dve")
            p.wait("vector", "s_kt" if c == 0 else "s_kt1", ktmm[c])
            p.comp("vector", lambda e, c=c: e.tensor_tensor(
                out=kbb[:].rearrange("p (h d) -> p h d", h=16),
                in0=ktm[c][:].rearrange("p (h d) -> p h d", h=16),
                in1=bmr[:, :, None].broadcast_to((128, 16, 64)), op=ALU.mult))
            p.comp("vector", lambda e, c=c: e.tensor_scalar(
                out=asrep[:], in0=bpat[:], scalar1=asc[:, c:c + 1], scalar2=None,
                op0=ALU.mult), inc="s_dve")
            p.wait("vector", "s_se", explm[c])
            for g in range(4):
                tgt = pA if g < 2 else pB
                off = (g % 2) * 512
                p.wait("vector", "s_pe", pe_attn)
                p.comp("vector", lambda e, g=g, tgt=tgt, off=off, c=c: e.tensor_tensor(
                    out=attn_d[:, g * 512:(g + 1) * 512], in0=tgt[:, off:off + 512],
                    in1=wbc[c][:, g * 512:(g + 1) * 512], op=ALU.mult),
                    inc="s_dve" if g == 3 else None)
            dv_attnd[c] = p.now("s_dve")
            if c == 0:
                p.wait("scalar", "s_gp", gpms[1])
                p.comp("scalar", lambda e: e.activation(
                    out=wbc[1][:], in_=wbc[1][:], func=AF.Exp), inc="s_se")
                explm[1] = p.now("s_se")
            p.wait("vector", "s_cbc" if c == 0 else "s_cbc1", cbcm[c])
            p.comp("vector", lambda e, c=c: e.tensor_tensor(
                out=qt[c][:].rearrange("p (r t) -> p r t", r=8),
                in0=qf[:].rearrange("p (r t) -> p r t", r=8)[:, :, c * 128:(c + 1) * 128],
                in1=cbc[c][:].rearrange("p (r t) -> p r t", r=8), op=ALU.mult),
                inc="s_dve")
            if DBG.get("sub", 99) < 8:
                continue
            p.wait("tensor", "s_dve")
            for h in range(16):
                p.stage("tensor", lambda e, h=h: MM(
                    e, out=pC[:, h * 64:(h + 1) * 64],
                    lhsT=attn_d[:, h * 128:(h + 1) * 128],
                    rhs=vbb[:, h * 64:(h + 1) * 64],
                    start=(h % 8 == 0), stop=False))
            if n > 0:
                for h in list(range(0, 16, 2)) + [-1] + list(range(1, 16, 2)):
                    if h < 0:
                        p.stage("tensor", lambda e: MM(
                            e, out=pE[0:1, 510:511], lhsT=ones_col[:],
                            rhs=ones_col[:], start=True, stop=True))
                        continue
                    pb = 64 * (h % 2)
                    p.stage("tensor", lambda e, h=h, pb=pb, c=c: MM(
                        e, out=pC[:, h * 64:(h + 1) * 64],
                        lhsT=qt[c][pb:pb + 64, (h // 2) * 128:(h // 2) * 128 + 128],
                        rhs=state_bf[pb:pb + 64, (h // 2) * 64:(h // 2) * 64 + 64],
                        start=False, stop=(h == 15)),
                        inc="s_pe" if h == 15 else None)
            else:
                p.stage("tensor", lambda e: MM(
                    e, out=pE[0:1, 510:511], lhsT=ones_col[:],
                    rhs=ones_col[:], start=True, stop=True), inc="s_pe")
            pe_ymm = p.now("s_pe")
            if DBG.get("sub", 99) < 9:
                continue
            for h in range(16):
                tp = {} if h % 2 == 0 else {"tile_position": (0, 64)}
                pb = 64 * (h % 2)
                p.stage("tensor", lambda e, h=h, tp=tp, pb=pb: MM(
                    e, out=pD[pb:pb + 64, (h // 2) * 64:(h // 2) * 64 + 64],
                    lhsT=kbb[:, h * 64:(h + 1) * 64],
                    rhs=vbf[:, h * 64:(h + 1) * 64],
                    start=(h <= 1), stop=(h == 15), **tp),
                    inc="s_pe" if h == 15 else None)
            if n > 0:
                p.stage("tensor", lambda e: MM(
                    e, out=pE[:, 0:512], lhsT=apat[:], rhs=asrep[:],
                    start=True, stop=True), inc="s_pe")
            pe_scan = p.now("s_pe")
            if DBG.get("sub", 99) < 10:
                continue
            p.wait("vector", "s_pe", pe_ymm)
            p.comp("vector", lambda e: e.tensor_copy(out=ysb[:], in_=pC[:]), inc="s_dve")
            dv_ysb = p.now("s_dve")
            p.wait("vector", "s_pe", pe_scan)
            if n > 0:
                p.comp("vector", lambda e: e.tensor_tensor(
                    out=state[:], in0=state[:], in1=pE[:, 0:512], op=ALU.mult))
                p.comp("vector", lambda e: e.tensor_tensor(
                    out=state[:], in0=state[:], in1=pD[:], op=ALU.add))
            else:
                p.comp("vector", lambda e: e.tensor_copy(out=state[:], in_=pD[:]))
            p.comp("vector", lambda e: e.tensor_copy(out=state_bf[:], in_=state[:]),
                    inc="s_dve")
            if DBG.get("sub", 99) < 11:
                continue
            p.wait("tensor", "s_dve", dv_ysb)
            for r in range(8):
                tb = pB if r < 4 else pA
                toff = (r % 4) * 128
                p.stage("tensor", lambda e, r=r, tb=tb, toff=toff: e.transpose(
                    out=tb[:, toff:toff + 128], in_=ysb[:, r * 128:(r + 1) * 128],
                    identity=ident_f[:]), inc="s_pe" if r == 7 else None)
            p.wait("vector", "s_pe")
            for r in range(8):
                tb = pB if r < 4 else pA
                toff = (r % 4) * 128
                p.comp("vector", lambda e, r=r, c=c, tb=tb, toff=toff, par=par:
                        e.tensor_tensor(
                            out=xreg[par][:, r * SC + c * 128: r * SC + c * 128 + 128],
                            in0=xreg[par][:, r * SC + c * 128: r * SC + c * 128 + 128],
                            in1=tb[:, toff:toff + 128], op=ALU.add),
                        inc="s_dve" if r == 7 else None)
                if c == 1:
                    xam[r] = p.now("s_dve")
            dv_xadd[0] = p.now("s_dve")

        if PH < 5:
            continue
        emit_ln(par, wmarks=xam if PH >= 4 else None)
        emit_ffn(par)
        p.wait("sync", "s_dve")
        p.dma("sync", outT[:, sc * SC:(sc + 1) * SC].rearrange("(r p) t -> p r t", p=128),
              xreg[par][:].rearrange("p (r t) -> p r t", r=8), "s_out")

    if p.now("s_out"):
        p.wait("sync", "s_out")

    with es:
        block = es.enter_context(nc.Block())
        p.replay(block)
    return nc


_NC_CACHE = {}


def kernel(**inputs):
    x = np.asarray(inputs["L0_attn_x"])
    Wq = np.asarray(inputs["L0_attn_Wq"]); bq = np.asarray(inputs["L0_attn_bq"])
    Wk = np.asarray(inputs["L0_attn_Wk"]); bk = np.asarray(inputs["L0_attn_bk"])
    Wv = np.asarray(inputs["L0_attn_Wv"]); bv = np.asarray(inputs["L0_attn_bv"])
    Wb = np.asarray(inputs["L0_attn_Wbeta"]); bb = np.asarray(inputs["L0_attn_bbeta"])
    Wa = np.asarray(inputs["L0_attn_Walpha"]); ba = np.asarray(inputs["L0_attn_balpha"])
    w1 = np.asarray(inputs["L0_ffn_w1"]); b1 = np.asarray(inputs["L0_ffn_b1"])
    w2 = np.asarray(inputs["L0_ffn_w2"]); b2 = np.asarray(inputs["L0_ffn_b2"])

    ct = lambda a: np.ascontiguousarray(a, dtype=np.float32)
    shared = {
        "wqT": ct(Wq.T), "wkT": ct(Wk.T), "wvT": ct(Wv.T),
        "wbT": ct(Wb.T), "waT": ct(Wa.T),
        "w1T": ct(w1.T), "w2T": ct(w2.T),
        "bq": ct(bq.reshape(1, D)), "bk": ct(bk.reshape(1, D)),
        "bv": ct(bv.reshape(1, D)), "bb": ct(bb.reshape(1, D)),
        "ba": ct(ba.reshape(1, D)), "b1": ct(b1.reshape(1, HF)),
        "b2": ct(b2.reshape(1, D)),
    }
    in_maps = []
    for b in range(8):
        m = dict(shared)
        m["xT"] = ct(x[b].T)
        in_maps.append(m)

    if "nc" not in _NC_CACHE:
        _NC_CACHE["nc"] = build_nc()
    nc = _NC_CACHE["nc"]
    res = run_bass_kernel_spmd(nc, in_maps, core_ids=list(range(8)))
    out = np.stack([np.asarray(res.results[i]["outT"]).T for i in range(8)])
    return np.ascontiguousarray(out, dtype=np.float32)
